# version 02
# baseline (speedup 1.0000x reference)
import contextlib
import numpy as np
import concourse.bass as bass
import concourse.mybir as mybir
from concourse.bass_utils import run_bass_kernel_spmd

F32 = mybir.dt.float32
BF16 = mybir.dt.bfloat16
AF = mybir.ActivationFunctionType
ALU = mybir.AluOpType
AX = mybir.AxisListType.X

HD = 4096
DSSM = 8192
NH = 64
KCONV = 4
CONV_DIM = DSSM + 2 * 128
PROJ = DSSM + CONV_DIM + NH
SPAN = 1024
HALO = 3
TT = SPAN + HALO
Q = 128
NCHUNK = 8
NTILE = 8
NDC = 32
EPS_LN = 1e-5
EPS_RMS = 1.1920929e-07
EPS_MAM = 1e-5
NEG_BIG = -30000.0

ENGS = ("sync", "scalar", "gpsimd", "vector", "tensor")


class Sem:
    def __init__(self, nc, ctx, name):
        self.h = ctx.enter_context(nc.semaphore(name))
        self.n = 0


class Plan:

    CHAINED = ("vector", "scalar", "gpsimd")

    def __init__(self, nc):
        self.nc = nc
        self.q = {e: [] for e in ENGS}
        self.nbar = 0
        self.chain = {}
        self.te_sem = None
        self.dma_sems = []

    def set_chain(self, eng, sem):
        self.chain[eng] = sem

    def _chain_wait(self, eng):
        cs = self.chain.get(eng)
        if cs is not None and cs.n > 0:
            v = cs.n
            self.q[eng].append(lambda e, cs=cs, v=v: e.wait_ge(cs.h, v))

    def emit(self, eng, fn, inc=None, dma=False):
        cs = self.chain.get(eng)
        if cs is not None:
            self._chain_wait(eng)
            if dma or (inc is not None and inc[0] is not cs):
                sem, k = inc
                sem.n += k
                self.q[eng].append(lambda e, fn=fn, sem=sem, k=k: fn(e).then_inc(sem.h, k))
            else:
                cs.n += 1
                self.q[eng].append(lambda e, fn=fn, cs=cs: fn(e).then_inc(cs.h, 1))
            return
        if inc is not None:
            sem, k = inc
            sem.n += k
            self.q[eng].append(lambda e, fn=fn, sem=sem, k=k: fn(e).then_inc(sem.h, k))
        else:
            self.q[eng].append(fn)

    def wait(self, eng, sem, val=None):
        v = sem.n if val is None else val
        if v > 0:
            self.q[eng].append(lambda e, sem=sem, v=v: e.wait_ge(sem.h, v))

    def barrier(self, s_bar):
        self.nbar += 1
        tgt = 5 * self.nbar
        for s_ in self.dma_sems:
            if s_.n > 0:
                self.wait("sync", s_, s_.n)
        for e in ENGS:
            self._chain_wait(e)
            if e == "tensor" and self.te_sem is not None and self.te_sem.n > 0:
                self.wait(e, self.te_sem, self.te_sem.n)
            self.q[e].append(lambda eng, s=s_bar: eng.sem_inc(s.h, 1))
        s_bar.n = tgt
        for e in ENGS:
            self.wait(e, s_bar, tgt)


def build_program(debug=False):
    nc = bass.Bass("TRN2", target_bir_lowering=False)
    ctx = contextlib.ExitStack()

    xh = nc.dram_tensor("xh", [TT, HD], F32, kind="ExternalInput").ap()
    w_in = nc.dram_tensor("w_in", [HD, PROJ], F32, kind="ExternalInput").ap()
    conv_w = nc.dram_tensor("conv_w", [CONV_DIM, KCONV], F32, kind="ExternalInput").ap()
    dt_bias = nc.dram_tensor("dt_bias", [NH, 1], F32, kind="ExternalInput").ap()
    a_log = nc.dram_tensor("a_log", [NH, 1], F32, kind="ExternalInput").ap()
    d_par = nc.dram_tensor("d_par", [1, NH], F32, kind="ExternalInput").ap()
    nw_in = nc.dram_tensor("nw_in", [NH, 128], F32, kind="ExternalInput").ap()
    w_out = nc.dram_tensor("w_out", [DSSM, HD], F32, kind="ExternalInput").ap()
    w_res = nc.dram_tensor("w_res", [16, HD], F32, kind="ExternalInput").ap()
    w_pre = nc.dram_tensor("w_pre", [4, HD], F32, kind="ExternalInput").ap()
    w_post = nc.dram_tensor("w_post", [4, HD], F32, kind="ExternalInput").ap()
    h_bases = nc.dram_tensor("h_bases", [1, 24], F32, kind="ExternalInput").ap()
    gate = nc.dram_tensor("gate", [1, 1], F32, kind="ExternalInput").ap()
    w_br = nc.dram_tensor("w_br", [1024, 1024], F32, kind="ExternalInput").ap()
    betweenT = nc.dram_tensor("betweenT", [8, 8], F32, kind="ExternalInput").ap()
    prevmask = nc.dram_tensor("prevmask", [8, 1], F32, kind="ExternalInput").ap()
    out = nc.dram_tensor("out", [SPAN, HD], F32, kind="ExternalOutput").ap()

    dbg = {}
    if debug:
        for nm, shp, dt in (("dbg_z", [DSSM, SPAN], BF16), ("dbg_x", [DSSM, SPAN], BF16),
                            ("dbg_dt", [NH, TT], F32), ("dbg_y", [DSSM, SPAN], BF16),
                            ("dbg_yn", [DSSM, SPAN], BF16), ("dbg_h", [128, NH * 128], F32),
                            ("dbg_hall", [128, NTILE * 24], F32),
                            ("dbg_xnfm", [128, NDC * TT], BF16),
                            ("dbg_bc", [128, 2 * SPAN], BF16)):
            dbg[nm] = nc.dram_tensor(nm, shp, dt, kind="ExternalOutput").ap()

    xn_scr = nc.dram_tensor("xn_scr", [SPAN, HD], BF16).ap()
    z_scr = nc.dram_tensor("z_scr", [DSSM, SPAN], BF16).ap()
    x_scr = nc.dram_tensor("x_scr", [DSSM, SPAN], BF16).ap()
    y_scr = nc.dram_tensor("y_scr", [DSSM, SPAN], BF16).ap()
    yn_scr = nc.dram_tensor("yn_scr", [DSSM, SPAN], BF16).ap()
    cd_scr = nc.dram_tensor("cd_scr", [DSSM, SPAN], BF16).ap()
    cs_scr = [nc.dram_tensor(f"cs_scr{i}", [NH, Q], F32).ap() for i in range(2)]
    ce_scr = [nc.dram_tensor(f"ce_scr{i}", [NH, 1], F32).ap() for i in range(2)]
    rstd_scr = nc.dram_tensor("rstd_scr", [NCHUNK, Q], F32).ap()
    small_scr = nc.dram_tensor("small_scr", [1, 64], F32).ap()
    wj_scr = nc.dram_tensor("wj_scr", [1, 8 * NH], F32).ap()
    cc_s_in = nc.dram_tensor("cc_s_in", [DSSM, 128], BF16)
    cc_s_out = nc.dram_tensor("cc_s_out", [8 * DSSM, 128], BF16)
    cc_m_in = nc.dram_tensor("cc_m_in", [128, NH], F32)
    cc_m_out = nc.dram_tensor("cc_m_out", [8 * 128, NH], F32)

    P = Plan(nc)
    s_st = [Sem(nc, ctx, f"s_st{i}") for i in range(2)]
    s_p0 = Sem(nc, ctx, "s_p0")
    s_x = [Sem(nc, ctx, f"s_x{i}") for i in range(2)]
    s_spxn = [Sem(nc, ctx, f"s_spxn{i}") for i in range(2)]
    s_ffn = Sem(nc, ctx, "s_ffn")
    s_wffn = Sem(nc, ctx, "s_wffn")
    s_gb = Sem(nc, ctx, "s_gb")
    s_sm = Sem(nc, ctx, "s_sm")
    s_w = [Sem(nc, ctx, f"s_w{i}") for i in range(2)]
    s_z = [Sem(nc, ctx, f"s_z{i}") for i in range(2)]
    s_xo = [Sem(nc, ctx, f"s_xo{i}") for i in range(2)]
    s_mx = [Sem(nc, ctx, f"s_mx{i}") for i in range(2)]
    s_sl = [Sem(nc, ctx, f"s_sl{i}") for i in range(2)]
    s_spcs = [Sem(nc, ctx, f"s_spcs{i}") for i in range(2)]
    s_ab = [Sem(nc, ctx, f"s_ab{i}") for i in range(2)]
    s_rt = [Sem(nc, ctx, f"s_rt{i}") for i in range(2)]
    s_y = [Sem(nc, ctx, f"s_y{i}") for i in range(2)]
    s_spg = Sem(nc, ctx, "s_spg")
    s_meta = Sem(nc, ctx, "s_meta")
    s_ccw = Sem(nc, ctx, "s_ccw")
    s_os = [Sem(nc, ctx, f"s_os{i}") for i in range(2)]
    P.dma_sems = [s_p0, s_ffn, s_wffn, s_gb, s_sm, s_spg, s_meta, s_ccw] + \
        s_st + s_x + s_spxn + s_w + s_z + s_xo + s_mx + s_sl + s_spcs + s_ab + s_rt + s_y + s_os
    s_v = Sem(nc, ctx, "s_v")
    s_sc = Sem(nc, ctx, "s_sc")
    s_te = Sem(nc, ctx, "s_te")
    s_gp = Sem(nc, ctx, "s_gp")
    s_cc = Sem(nc, ctx, "s_cc")
    s_bar = Sem(nc, ctx, "s_bar")
    P.set_chain("vector", s_v)
    P.set_chain("scalar", s_sc)
    P.set_chain("gpsimd", s_gp)
    P.te_sem = s_te

    def sb(cx, name, shape, dt):
        return cx.enter_context(nc.sbuf_tensor(name, shape, dt))


    pf = [ctx.enter_context(nc.psum_tensor(f"pf{i}", [128, 512], F32)) for i in range(6)]
    pbf = [ctx.enter_context(nc.psum_tensor(f"pbf{i}", [128, 1024], BF16)) for i in range(2)]

    ident_bf = sb(ctx, "ident_bf", [128, 128], BF16)
    ident_f = sb(ctx, "ident_f", [128, 128], F32)
    mneg4 = sb(ctx, "mneg4", [128, 512], F32)
    onescol_bf = sb(ctx, "onescol_bf", [128, 1], BF16)
    cbias = sb(ctx, "cbias", [128, 4], F32)
    convw = sb(ctx, "convw", [128, 66 * KCONV], F32)
    dtb_col = sb(ctx, "dtb_col", [NH, 1], F32)
    nega_col = sb(ctx, "nega_col", [NH, 1], F32)
    dbc = sb(ctx, "dbc", [128, NH], F32)
    nwcols = sb(ctx, "nwcols", [128, NH], BF16)
    bfm = sb(ctx, "bfm", [128, SPAN], BF16)
    cfm = sb(ctx, "cfm", [128, SPAN], BF16)
    dtraw = sb(ctx, "dtraw", [NH, TT], F32)
    dt_sp = sb(ctx, "dt_sp", [NH, SPAN], F32)
    logdA = sb(ctx, "logdA", [NH, SPAN], F32)
    logtot = sb(ctx, "logtot", [NH, 1], F32)
    h_all = sb(ctx, "h_all", [128, NTILE * 24], F32)
    rstdf_all = sb(ctx, "rstdf_all", [128, NTILE], F32)
    nw_tmp = sb(ctx, "nw_tmp", [NH, 128], F32)

    P.emit("gpsimd", lambda g: g.memset(ident_bf[:], 0.0))
    P.emit("gpsimd", lambda g: g.affine_select(
        out=ident_bf[:], in_=ident_bf[:], compare_op=ALU.not_equal, fill=1.0,
        base=0, pattern=[[-1, 128]], channel_multiplier=1))
    P.emit("gpsimd", lambda g: g.memset(ident_f[:], 0.0))
    P.emit("gpsimd", lambda g: g.affine_select(
        out=ident_f[:], in_=ident_f[:], compare_op=ALU.not_equal, fill=1.0,
        base=0, pattern=[[-1, 128]], channel_multiplier=1))
    P.emit("gpsimd", lambda g: g.memset(mneg4[:], 0.0))
    for j in range(4):
        P.emit("gpsimd", lambda g, j=j: g.affine_select(
            out=mneg4[:, 128 * j:128 * (j + 1)], in_=mneg4[:, 128 * j:128 * (j + 1)],
            compare_op=ALU.is_ge, fill=NEG_BIG, base=0,
            pattern=[[1, 128]], channel_multiplier=-1))
    P.emit("gpsimd", lambda g: g.memset(cbias[:, 0:1], EPS_LN))
    P.emit("gpsimd", lambda g: g.memset(cbias[:, 1:2], EPS_RMS))
    P.emit("gpsimd", lambda g: g.memset(cbias[:, 2:3], 1.0))
    P.emit("gpsimd", lambda g: g.memset(cbias[:, 3:4], EPS_MAM))
    P.emit("gpsimd", lambda g: g.memset(onescol_bf[:], 1.0), inc=(s_gp, 1))
    gp_p0 = s_gp.n

    P.emit("sync", lambda e: e.dma_start(
        out=convw.ap().rearrange("p (cc k) -> p cc k", k=KCONV),
        in_=conv_w.rearrange("(cc p) k -> p cc k", p=128)), inc=(s_p0, 16), dma=True)
    P.emit("sync", lambda e: e.dma_start(out=dtb_col[:], in_=dt_bias), inc=(s_p0, 16), dma=True)
    P.emit("sync", lambda e: e.dma_start(out=nega_col[:], in_=a_log), inc=(s_p0, 16), dma=True)
    P.emit("sync", lambda e: e.dma_start(out=dbc[:], in_=d_par.broadcast_to((128, NH))), inc=(s_p0, 16), dma=True)
    P.emit("sync", lambda e: e.dma_start(out=nw_tmp[:], in_=nw_in), inc=(s_p0, 16), dma=True)
    P.wait("scalar", s_p0, s_p0.n)
    P.emit("scalar", lambda e: e.activation(nega_col[:], nega_col[:], AF.Exp), inc=(s_sc, 1))
    P.wait("vector", s_sc, s_sc.n)
    P.emit("vector", lambda v: v.tensor_scalar_mul(nega_col[:], nega_col[:], -1.0), inc=(s_v, 1))

    P.wait("tensor", s_p0, s_p0.n)
    P.wait("tensor", s_gp, gp_p0)
    P.emit("tensor", lambda t: t.transpose(pf[0].ap()[0:128, 0:64], nw_tmp[:], ident_f[0:64, 0:64]), inc=(s_te, 1))
    P.wait("vector", s_te, s_te.n)
    P.emit("vector", lambda v: v.tensor_copy(nwcols[:], pf[0][0:128, 0:64]), inc=(s_v, 1))
    v_nw = s_v.n

    cxP12 = contextlib.ExitStack()
    xnfm = sb(cxP12, "xnfm", [128, NDC * TT], BF16)
    wbr_sb = sb(cxP12, "wbr_sb", [128, 8 * 1024], BF16)
    wcatT = sb(cxP12, "wcatT", [128, NDC * 24], BF16)
    hb_bc = sb(cxP12, "hb_bc", [128, 24], F32)
    gcol = sb(cxP12, "gcol", [128, 1], F32)
    gicol = sb(cxP12, "gicol", [128, 16], F32)
    logit_raw = sb(cxP12, "logit_raw", [128, NTILE * 24], F32)
    wcat_bf = sb(cxP12, "wcat_bf", [24, HD], BF16)
    cx1 = contextlib.ExitStack()
    xt_f = [sb(cx1, f"xt_f{i}", [128, HD], F32) for i in range(2)]
    xnf_t = sb(cx1, "xnf_t", [128, HD], F32)
    sqt = sb(cx1, "sqt", [128, HD], F32)
    xn_bf = [sb(cx1, f"xn_bf{i}", [128, HD], BF16) for i in range(2)]
    stat = sb(cx1, "stat", [128, 8], F32)

    te_tr_done = []
    sp_xn = []
    v_x_used = []
    sc_ffn = []
    for i in range(NTILE + 1):
        buf = i % 2
        is_halo = (i == NTILE)
        rows = HALO if is_halo else 128
        row0 = 0 if is_halo else HALO + 128 * i
        tcol = 0 if is_halo else HALO + 128 * i
        if i >= 2:
            P.wait("sync", s_v, v_x_used[i - 2])
        P.emit("sync", lambda e, buf=buf, rows=rows, row0=row0: e.dma_start(
            out=xt_f[buf][0:rows, :], in_=xh[row0:row0 + rows, :]), inc=(s_x[buf], 16), dma=True)
        P.wait("vector", s_x[buf], s_x[buf].n)
        if i >= 1:
            P.wait("vector", s_sc, sc_ffn[i - 1])
        P.emit("vector", lambda v, buf=buf, rows=rows: v.reduce_sum(
            stat[0:rows, 0:1], xt_f[buf][0:rows, :], axis=AX))
        P.emit("vector", lambda v, rows=rows: v.tensor_scalar_mul(
            stat[0:rows, 1:2], stat[0:rows, 0:1], -1.0 / HD))
        P.emit("vector", lambda v, buf=buf, rows=rows: v.scalar_tensor_tensor(
            xnf_t[0:rows, :], xt_f[buf][0:rows, :], stat[0:rows, 1:2],
            xt_f[buf][0:rows, :], ALU.add, ALU.bypass), inc=(s_v, 1))
        v_x_used.append(s_v.n)
        P.wait("scalar", s_v, s_v.n)
        P.wait("scalar", s_gp, gp_p0)
        P.emit("scalar", lambda e, rows=rows: e.activation(
            sqt[0:rows, :], xnf_t[0:rows, :], AF.Square, accum_out=stat[0:rows, 2:3]))
        P.emit("scalar", lambda e, rows=rows: e.activation(
            stat[0:rows, 3:4], stat[0:rows, 2:3], AF.Ln, bias=cbias[0:rows, 0:1], scale=1.0 / HD))
        P.emit("scalar", lambda e, rows=rows: e.activation(
            stat[0:rows, 4:5], stat[0:rows, 3:4], AF.Exp, scale=-0.5), inc=(s_sc, 1))
        P.wait("vector", s_sc, s_sc.n)
        P.emit("vector", lambda v, rows=rows: v.scalar_tensor_tensor(
            xnf_t[0:rows, :], xnf_t[0:rows, :], stat[0:rows, 4:5],
            xnf_t[0:rows, :], ALU.mult, ALU.bypass), inc=(s_v, 1))
        v_xn32 = s_v.n
        if i >= 2:
            P.wait("vector", s_te, te_tr_done[i - 2])
            P.wait("vector", *sp_xn[i - 2])
        P.emit("vector", lambda v, buf=buf, rows=rows: v.tensor_copy(
            xn_bf[buf][0:rows, :], xnf_t[0:rows, :]), inc=(s_v, 1))
        v_xnbf = s_v.n
        if not is_halo:
            P.wait("scalar", s_v, v_xn32)
            P.emit("scalar", lambda e: e.activation(
                sqt[:, :], xnf_t[:, :], AF.Square, accum_out=stat[0:128, 5:6]))
            P.emit("scalar", lambda e: e.activation(
                stat[0:128, 6:7], stat[0:128, 5:6], AF.Ln, bias=cbias[:, 1:2], scale=1.0 / HD))
            P.emit("scalar", lambda e, i=i: e.activation(
                rstdf_all[:, i:i + 1], stat[0:128, 6:7], AF.Exp, scale=-0.5), inc=(s_sc, 1))
            P.wait("scalar", s_v, v_xnbf)
            P.emit("scalar", lambda e, buf=buf, i=i: e.dma_start(
                out=xn_scr[128 * i:128 * (i + 1), :], in_=xn_bf[buf][:, :]), inc=(s_spxn[buf], 16), dma=True)
        sc_ffn.append(s_sc.n)
        sp_xn.append((s_spxn[buf], s_spxn[buf].n))
        P.wait("tensor", s_v, v_xnbf)
        tr_v = {}
        for dc in range(NDC):
            pd = pbf[dc % 2]
            if dc >= 2:
                P.wait("tensor", s_v, tr_v[dc - 2])
            P.emit("tensor", lambda t, buf=buf, dc=dc, rows=rows, pd=pd: t.transpose(
                pd[0:128, 0:rows], xn_bf[buf][0:rows, 128 * dc:128 * (dc + 1)],
                ident_bf[0:rows, 0:rows]), inc=(s_te, 1))
            P.wait("vector", s_te, s_te.n)
            P.emit("vector", lambda v, dc=dc, tcol=tcol, rows=rows, pd=pd: v.tensor_copy(
                xnfm[:, TT * dc + tcol:TT * dc + tcol + rows], pd[0:128, 0:rows]), inc=(s_v, 1))
            tr_v[dc] = s_v.n
        te_tr_done.append(s_te.n)
    v_xnfm_all = s_v.n
    if debug:
        P.wait("sync", s_v, v_xnfm_all)
        P.emit("sync", lambda e: e.dma_start(out=dbg["dbg_xnfm"], in_=xnfm[:, :]), inc=(s_spg, 16), dma=True)
        P.wait("sync", s_spg, s_spg.n)
    P.barrier(s_bar)
    cx1.close()

    cx2 = contextlib.ExitStack()
    g_tmp = sb(cx2, "g_tmp", [1, 2], F32)
    gi_row = sb(cx2, "gi_row", [1, 17], F32)

    P.emit("sync", lambda e: e.dma_start(out=hb_bc[:], in_=h_bases.broadcast_to((128, 24))), inc=(s_ffn, 16), dma=True)
    P.emit("sync", lambda e: e.dma_start(out=g_tmp[0:1, 0:1], in_=gate), inc=(s_ffn, 16), dma=True)
    P.emit("gpsimd", lambda g: g.dma_start(out=wcat_bf[0:16, :], in_=w_res), inc=(s_wffn, 16), dma=True)
    P.emit("gpsimd", lambda g: g.dma_start(out=wcat_bf[16:20, :], in_=w_pre), inc=(s_wffn, 16), dma=True)
    P.emit("gpsimd", lambda g: g.dma_start(out=wcat_bf[20:24, :], in_=w_post), inc=(s_wffn, 16), dma=True)
    P.emit("gpsimd", lambda g: g.dma_start(
        out=wbr_sb.ap().rearrange("p (kc c) -> p kc c", kc=8),
        in_=w_br.rearrange("(kc p) c -> p kc c", p=128)), inc=(s_wffn, 16), dma=True)
    P.wait("tensor", s_wffn, s_wffn.n)
    tr_v = {}
    for dc in range(NDC):
        pd = pbf[dc % 2]
        if dc >= 2:
            P.wait("tensor", s_v, tr_v[dc - 2])
        P.emit("tensor", lambda t, dc=dc, pd=pd: t.transpose(
            pd[0:128, 0:24], wcat_bf[:, 128 * dc:128 * (dc + 1)], ident_bf[0:24, 0:24]), inc=(s_te, 1))
        P.wait("vector", s_te, s_te.n)
        P.emit("vector", lambda v, dc=dc, pd=pd: v.tensor_copy(
            wcatT[:, 24 * dc:24 * (dc + 1)], pd[0:128, 0:24]), inc=(s_v, 1))
        tr_v[dc] = s_v.n
    v_wcatT = s_v.n

    P.wait("scalar", s_ffn, s_ffn.n)
    P.emit("scalar", lambda e: e.activation(g_tmp[0:1, 0:1], g_tmp[0:1, 0:1], AF.Sigmoid), inc=(s_sc, 1))
    P.wait("vector", s_sc, s_sc.n)
    P.emit("vector", lambda v: v.memset(gi_row[:], 0.0))
    P.emit("vector", lambda v: v.tensor_scalar(
        g_tmp[0:1, 1:2], g_tmp[0:1, 0:1], -1.0, 1.0, ALU.mult, ALU.add))
    for k in range(4):
        P.emit("vector", lambda v, k=k: v.tensor_copy(gi_row[0:1, 5 * k:5 * k + 1], g_tmp[0:1, 1:2]))
    P.emit("vector", lambda v: v.tensor_copy(gi_row[0:1, 16:17], g_tmp[0:1, 0:1]), inc=(s_v, 1))
    P.wait("scalar", s_v, s_v.n)
    P.emit("scalar", lambda e: e.dma_start(out=small_scr[0:1, 0:17], in_=gi_row[:]), inc=(s_sm, 16), dma=True)
    P.wait("sync", s_sm, s_sm.n)
    P.emit("sync", lambda e: e.dma_start(out=gicol[:], in_=small_scr[0:1, 0:16].broadcast_to((128, 16))), inc=(s_gb, 16), dma=True)
    P.emit("sync", lambda e: e.dma_start(out=gcol[:], in_=small_scr[0:1, 16:17].broadcast_to((128, 1))), inc=(s_gb, 16), dma=True)

    P.wait("tensor", s_v, v_wcatT)
    for i in range(NTILE):
        if i >= 1:
            P.wait("tensor", s_v, s_v.n)
        for dc in range(NDC):
            P.emit("tensor", lambda t, i=i, dc=dc: t.matmul(
                pf[4].ap()[0:128, 0:24],
                xnfm[:, TT * dc + HALO + 128 * i: TT * dc + HALO + 128 * (i + 1)],
                wcatT[:, 24 * dc:24 * (dc + 1)],
                start=(dc == 0), stop=(dc == NDC - 1)),
                inc=(s_te, 1) if dc == NDC - 1 else None)
        P.wait("vector", s_te, s_te.n)
        P.wait("vector", s_gb, s_gb.n)
        P.emit("vector", lambda v, i=i: v.scalar_tensor_tensor(
            logit_raw[:, 24 * i:24 * (i + 1)], pf[4][0:128, 0:24], rstdf_all[:, i:i + 1],
            hb_bc[:, :], ALU.mult, ALU.add))
        P.emit("vector", lambda v, i=i: v.tensor_scalar_mul(
            logit_raw[:, 24 * i:24 * i + 16], logit_raw[:, 24 * i:24 * i + 16], 10.0), inc=(s_v, 1))
    v_logits = s_v.n

    cx3 = contextlib.ExitStack()
    zv_b = sb(cx3, "zv_b", [128, 128], F32)
    u_b = sb(cx3, "u_b", [128, 32], F32)
    v_b = sb(cx3, "v_b", [128, 32], F32)
    m_b = sb(cx3, "m_b", [128, 32], F32)
    sst_b = sb(cx3, "sst_b", [128, 32], F32)

    Zr = logit_raw.ap().rearrange("p (t k) -> p t k", k=24)[:, :, 0:16].rearrange(
        "p t (i j) -> p t i j", j=4)
    zv4 = zv_b.ap().rearrange("p (t i j) -> p t i j", i=4, j=4)
    zv4T = zv_b.ap().rearrange("p (t i j) -> p t j i", i=4, j=4)
    uv = u_b.ap().rearrange("p (t i) -> p t i", i=4)
    vv = v_b.ap().rearrange("p (t j) -> p t j", j=4)
    mv = m_b.ap().rearrange("p (t k) -> p t k", k=4)
    sv = sst_b.ap().rearrange("p (t k) -> p t k", k=4)

    P.wait("vector", s_v, v_logits)
    P.emit("vector", lambda v: v.memset(u_b[:], 0.0))
    P.emit("vector", lambda v: v.memset(v_b[:], 0.0))
    for it in range(50):
        for i in range(4):
            P.emit("vector", lambda v, i=i: v.tensor_add(zv4[:, :, i, :], Zr[:, :, i, :], vv[:, :, :]))
        P.emit("vector", lambda v: v.reduce_max(mv[:, :, :], zv4[:, :, :, :], axis=AX))
        for j in range(4):
            P.emit("vector", lambda v, j=j: v.tensor_sub(zv4[:, :, :, j], zv4[:, :, :, j], mv[:, :, :]),
                   inc=(s_v, 1) if j == 3 else None)
        P.wait("scalar", s_v, s_v.n)
        P.emit("scalar", lambda e: e.activation(zv_b[:], zv_b[:], AF.Exp), inc=(s_sc, 1))
        P.wait("vector", s_sc, s_sc.n)
        P.emit("vector", lambda v: v.reduce_sum(sv[:, :, :], zv4[:, :, :, :], axis=AX), inc=(s_v, 1))
        P.wait("scalar", s_v, s_v.n)
        P.emit("scalar", lambda e: e.activation(sst_b[:], sst_b[:], AF.Ln), inc=(s_sc, 1))
        P.wait("vector", s_sc, s_sc.n)
        P.emit("vector", lambda v: v.scalar_tensor_tensor(
            u_b[:], sst_b[:], -1.0, m_b[:], ALU.mult, ALU.subtract))
        for j in range(4):
            P.emit("vector", lambda v, j=j: v.tensor_add(zv4[:, :, :, j], Zr[:, :, :, j], uv[:, :, :]))
        P.emit("vector", lambda v: v.reduce_max(mv[:, :, :], zv4T[:, :, :, :], axis=AX))
        for i in range(4):
            P.emit("vector", lambda v, i=i: v.tensor_sub(zv4[:, :, i, :], zv4[:, :, i, :], mv[:, :, :]),
                   inc=(s_v, 1) if i == 3 else None)
        P.wait("scalar", s_v, s_v.n)
        P.emit("scalar", lambda e: e.activation(zv_b[:], zv_b[:], AF.Exp), inc=(s_sc, 1))
        P.wait("vector", s_sc, s_sc.n)
        P.emit("vector", lambda v: v.reduce_sum(sv[:, :, :], zv4T[:, :, :, :], axis=AX), inc=(s_v, 1))
        P.wait("scalar", s_v, s_v.n)
        P.emit("scalar", lambda e: e.activation(sst_b[:], sst_b[:], AF.Ln), inc=(s_sc, 1))
        P.wait("vector", s_sc, s_sc.n)
        P.emit("vector", lambda v: v.scalar_tensor_tensor(
            v_b[:], sst_b[:], -1.0, m_b[:], ALU.mult, ALU.subtract))
    for i in range(4):
        P.emit("vector", lambda v, i=i: v.tensor_add(zv4[:, :, i, :], Zr[:, :, i, :], vv[:, :, :]))
    for i in range(4):
        for j in range(4):
            P.emit("vector", lambda v, i=i, j=j: v.tensor_add(
                zv4[:, :, i, j:j + 1], zv4[:, :, i, j:j + 1], uv[:, :, i:i + 1]))
    P.emit("vector", lambda v: v.tensor_copy(m_b[:], m_b[:]), inc=(s_v, 1))
    P.wait("scalar", s_v, s_v.n)
    P.emit("scalar", lambda e: e.activation(zv_b[:], zv_b[:], AF.Exp), inc=(s_sc, 1))
    P.wait("vector", s_sc, s_sc.n)
    P.wait("vector", s_gb, s_gb.n)
    for i in range(NTILE):
        P.emit("vector", lambda v, i=i: v.scalar_tensor_tensor(
            h_all[:, 24 * i:24 * i + 16], zv_b[:, 16 * i:16 * (i + 1)],
            gcol[:, 0:1], gicol[:, :], ALU.mult, ALU.add))
    mm8 = m_b.ap()[0:128, 0:8].rearrange("p (t o) -> p t o", o=1)
    ss8 = sst_b.ap()[0:128, 0:8].rearrange("p (t o) -> p t o", o=1)
    for base in (16, 20):
        pl = logit_raw.ap().rearrange("p (t k) -> p t k", k=24)[:, :, base:base + 4]
        hv = h_all.ap().rearrange("p (t k) -> p t k", k=24)[:, :, base:base + 4]
        tmp = zv_b.ap()[0:128, 0:32].rearrange("p (t j) -> p t j", j=4)
        P.emit("vector", lambda v, pl=pl: v.reduce_max(mm8[:, :, :], pl[:, :, :], axis=AX))
        for j in range(4):
            P.emit("vector", lambda v, pl=pl, tmp=tmp, j=j: v.tensor_sub(
                tmp[:, :, j:j + 1], pl[:, :, j:j + 1], mm8[:, :, :]), inc=(s_v, 1) if j == 3 else None)
        P.wait("scalar", s_v, s_v.n)
        P.emit("scalar", lambda e: e.activation(zv_b[0:128, 0:32], zv_b[0:128, 0:32], AF.Exp), inc=(s_sc, 1))
        P.wait("vector", s_sc, s_sc.n)
        P.emit("vector", lambda v, tmp=tmp: v.reduce_sum(ss8[:, :, :], tmp[:, :, :], axis=AX))
        P.emit("vector", lambda v: v.reciprocal(sst_b[0:128, 0:8], sst_b[0:128, 0:8]))
        for j in range(4):
            P.emit("vector", lambda v, hv=hv, tmp=tmp, j=j: v.tensor_mul(
                hv[:, :, j:j + 1], tmp[:, :, j:j + 1], ss8[:, :, :]))
    P.emit("vector", lambda v: v.tensor_copy(m_b[:], m_b[:]), inc=(s_v, 1))
    v_hall = s_v.n
    if debug:
        P.wait("scalar", s_v, v_hall)
        P.emit("scalar", lambda e: e.dma_start(out=dbg["dbg_hall"], in_=h_all[:]), inc=(s_spg, 16), dma=True)

    cxW = contextlib.ExitStack()
    wt_bf = [sb(cxW, f"wt_bf{i}", [128, NDC * 128], BF16) for i in range(2)]
    xbc_bf = [sb(cxW, f"xbc_bf{i}", [128, TT], BF16) for i in range(2)]
    cvb = [sb(cxW, f"cvb{i}", [128, SPAN], BF16) for i in range(2)]
    xout = [sb(cxW, f"xout{i}", [128, SPAN], BF16) for i in range(2)]
    zeb = [sb(cxW, f"zeb{i}", [128, 1024], BF16) for i in range(2)]
    xnm_bf = [sb(cxW, f"xnm_bf{i}", [128, HD], BF16) for i in range(2)]
    xpre_tm = sb(cxW, "xpre_tm", [128, 1024], BF16)
    xpre_fm = sb(cxW, "xpre_fm", [128, 8 * 128], BF16)
    ybr_bf = sb(cxW, "ybr_bf", [128, 1024], BF16)
    ffn_acc = sb(cxW, "ffn_acc", [128, HD], BF16)
    xrl_f = [sb(cxW, f"xrl_f{i}", [128, HD], F32) for i in range(2)]

    p2_ccs = list(range(64, 131)) + list(range(0, 64))
    idx_dt = p2_ccs.index(130)

    w_loaded = {}
    te_chunk = {}
    v_chunk = {}
    sc_chunk = {}
    sp_chunk = {}
    mix_done = {}
    mix_st = {}
    mix_te = {}
    mix_ybr = {}

    def emit_wload(idx):
        cc = p2_ccs[idx]
        wbuf = idx % 2
        width = 64 if cc == 130 else 128
        c0 = 128 * cc
        if idx >= 2:
            P.wait("gpsimd", s_te, te_chunk[idx - 2])
        for gdc in range(4):
            P.emit("gpsimd", lambda g, wbuf=wbuf, gdc=gdc, c0=c0, width=width: g.dma_start(
                out=wt_bf[wbuf].ap().rearrange("p (dc c) -> p dc c", c=128)[:, 8 * gdc:8 * (gdc + 1), 0:width],
                in_=w_in.rearrange("(dc p) c -> p dc c", p=128)[:, 8 * gdc:8 * (gdc + 1), c0:c0 + width]),
                inc=(s_w[wbuf], 16), dma=True)
        w_loaded[idx] = (s_w[wbuf], s_w[wbuf].n)

    emit_wload(0)
    emit_wload(1)

    def plan_mix_tile(i):
        mb = i % 2
        if i >= 2:
            P.wait("sync", s_v, mix_done[i - 2])
            P.wait("sync", *mix_st[i - 2])
        P.wait("sync", *sp_xn[i])
        P.emit("sync", lambda e, mb=mb, i=i: e.dma_start(
            out=xnm_bf[mb][:, :], in_=xn_scr[128 * i:128 * (i + 1), :]), inc=(s_mx[mb], 16), dma=True)
        P.emit("sync", lambda e, mb=mb, i=i: e.dma_start(
            out=xrl_f[mb][:, :], in_=xh[HALO + 128 * i:HALO + 128 * (i + 1), :]), inc=(s_mx[mb], 16), dma=True)
        P.wait("vector", s_mx[mb], s_mx[mb].n)
        P.wait("vector", s_v, v_hall)
        if i >= 1:
            P.wait("vector", s_te, mix_te[i - 1])
        for n in range(4):
            P.emit("vector", lambda v, mb=mb, i=i, n=n: v.scalar_tensor_tensor(
                xpre_tm[:, :], xnm_bf[mb][:, 1024 * n:1024 * (n + 1)],
                h_all[:, 24 * i + 16 + n:24 * i + 17 + n],
                xpre_tm[:, :] if n else xnm_bf[mb][:, 0:1024],
                ALU.mult, ALU.add if n else ALU.bypass), inc=(s_v, 1) if n == 3 else None)
        v_xpre = s_v.n
        P.wait("tensor", s_v, v_xpre)
        P.wait("tensor", s_v, v_chunk[idx_dt])
        P.wait("tensor", s_v, mix_ybr.get(i - 1, 0))
        tr_v = {}
        for kc in range(8):
            pd = pbf[kc % 2]
            if kc >= 2:
                P.wait("tensor", s_v, tr_v[kc - 2])
            P.emit("tensor", lambda t, kc=kc, pd=pd: t.transpose(
                pd[0:128, 0:128], xpre_tm[:, 128 * kc:128 * (kc + 1)], ident_bf[:, :]), inc=(s_te, 1))
            P.wait("vector", s_te, s_te.n)
            P.emit("vector", lambda v, kc=kc, pd=pd: v.tensor_copy(
                xpre_fm[:, 128 * kc:128 * (kc + 1)], pd[0:128, 0:128]), inc=(s_v, 1))
            tr_v[kc] = s_v.n
        v_xprefm = s_v.n
        P.wait("tensor", s_v, v_xprefm)
        for half in range(2):
            for kc in range(8):
                P.emit("tensor", lambda t, half=half, kc=kc: t.matmul(
                    pf[4 + half].ap()[0:128, 0:512], xpre_fm[:, 128 * kc:128 * (kc + 1)],
                    wbr_sb[:, 1024 * kc + 512 * half:1024 * kc + 512 * (half + 1)],
                    start=(kc == 0), stop=(kc == 7)), inc=(s_te, 1) if kc == 7 else None)
        mix_te[i] = s_te.n
        P.wait("vector", s_te, mix_te[i])
        for half in range(2):
            P.emit("vector", lambda v, half=half: v.tensor_copy(
                ybr_bf[:, 512 * half:512 * (half + 1)], pf[4 + half][0:128, 0:512]), inc=(s_v, 1))
        mix_ybr[i] = s_v.n
        for ib in range(4):
            for jb in range(4):
                P.emit("vector", lambda v, mb=mb, i=i, ib=ib, jb=jb: v.scalar_tensor_tensor(
                    ffn_acc[:, 1024 * ib:1024 * (ib + 1)],
                    xnm_bf[mb][:, 1024 * jb:1024 * (jb + 1)],
                    h_all[:, 24 * i + 4 * ib + jb:24 * i + 4 * ib + jb + 1],
                    ffn_acc[:, 1024 * ib:1024 * (ib + 1)] if jb else xnm_bf[mb][:, 0:1024],
                    ALU.mult, ALU.add if jb else ALU.bypass))
            P.emit("vector", lambda v, i=i, ib=ib: v.scalar_tensor_tensor(
                ffn_acc[:, 1024 * ib:1024 * (ib + 1)], ybr_bf[:, :],
                h_all[:, 24 * i + 20 + ib:24 * i + 21 + ib],
                ffn_acc[:, 1024 * ib:1024 * (ib + 1)], ALU.mult, ALU.add))
        P.emit("vector", lambda v, mb=mb: v.tensor_add(xrl_f[mb][:, :], xrl_f[mb][:, :], ffn_acc[:, :]), inc=(s_v, 1))
        mix_done[i] = s_v.n
        P.wait("sync", s_v, mix_done[i])
        P.emit("sync", lambda e, mb=mb, i=i: e.dma_start(
            out=out[128 * i:128 * (i + 1), :], in_=xrl_f[mb][:, :]), inc=(s_st[mb], 16), dma=True)
        mix_st[i] = (s_st[mb], s_st[mb].n)

    P.wait("tensor", s_v, v_xnfm_all)
    for idx, cc in enumerate(p2_ccs):
        pb = idx % 2
        is_dt = (cc == 130)
        is_z = (cc < 64)
        is_x = (64 <= cc < 128)
        width = 64 if is_dt else 128
        tiles = [(HALO, 512), (HALO + 512, 512)] if is_z else [(0, 512), (512, 512), (1024, HALO)]
        P.wait("tensor", *w_loaded[idx])
        if idx >= 2:
            P.wait("tensor", s_v, v_chunk[idx - 2])
        else:
            P.wait("tensor", s_v, v_logits)
        for tb, (t0, tw) in enumerate(tiles):
            pt = pf[2 * pb + tb] if tb < 2 else pf[4 + pb]
            for dc in range(NDC):
                P.emit("tensor", lambda t, pb=pb, dc=dc, t0=t0, tw=tw, pt=pt, width=width: t.matmul(
                    pt.ap()[0:width, 0:tw], wt_bf[pb][:, 128 * dc:128 * dc + width],
                    xnfm[:, TT * dc + t0:TT * dc + t0 + tw],
                    start=(dc == 0), stop=(dc == NDC - 1)),
                    inc=(s_te, 1) if dc == NDC - 1 else None)
        te_chunk[idx] = s_te.n
        if idx + 2 < len(p2_ccs):
            emit_wload(idx + 2)
        P.wait("vector", s_te, te_chunk[idx])
        if is_dt:
            for tb, (t0, tw) in enumerate(tiles):
                pt = pf[2 * pb + tb] if tb < 2 else pf[4 + pb]
                P.emit("vector", lambda v, t0=t0, tw=tw, pt=pt: v.tensor_copy(
                    dtraw[:, t0:t0 + tw], pt[0:64, 0:tw]), inc=(s_v, 1))
            v_chunk[idx] = s_v.n
            sc_chunk[idx] = s_sc.n
            sp_chunk[idx] = (s_spg, s_spg.n)
        elif is_z:
            if idx >= 2:
                P.wait("vector", *sp_chunk[idx - 2])
            for tb, (t0, tw) in enumerate(tiles):
                pt = pf[2 * pb + tb]
                P.emit("vector", lambda v, pb=pb, tb=tb, pt=pt: v.tensor_copy(
                    zeb[pb][:, 512 * tb:512 * (tb + 1)], pt[0:128, 0:512]), inc=(s_v, 1))
            v_chunk[idx] = s_v.n
            sc_chunk[idx] = s_sc.n
            P.wait("scalar", s_v, v_chunk[idx])
            P.emit("scalar", lambda e, pb=pb, cc=cc: e.dma_start(
                out=z_scr[128 * cc:128 * (cc + 1), :], in_=zeb[pb][:, :]), inc=(s_z[pb], 16), dma=True)
            sp_chunk[idx] = (s_z[pb], s_z[pb].n)
        else:
            if idx >= 2:
                P.wait("vector", s_sc, sc_chunk[idx - 2])
            for tb, (t0, tw) in enumerate(tiles):
                pt = pf[2 * pb + tb] if tb < 2 else pf[4 + pb]
                P.emit("vector", lambda v, pb=pb, t0=t0, tw=tw, pt=pt: v.tensor_copy(
                    xbc_bf[pb][:, t0:t0 + tw], pt[0:128, 0:tw]))
            ci = cc - 64
            for k in range(KCONV):
                P.emit("vector", lambda v, pb=pb, ci=ci, k=k: v.scalar_tensor_tensor(
                    cvb[pb][:, :], xbc_bf[pb][:, k:k + SPAN], convw[:, 4 * ci + k:4 * ci + k + 1],
                    cvb[pb][:, :] if k else xbc_bf[pb][:, 0:SPAN],
                    ALU.mult, ALU.add if k else ALU.bypass), inc=(s_v, 1) if k == 3 else None)
            v_chunk[idx] = s_v.n
            P.wait("scalar", s_v, v_chunk[idx])
            if idx >= 2:
                P.wait("scalar", *sp_chunk[idx - 2])
            P.emit("scalar", lambda e, pb=pb: e.activation(
                xout[pb][:, :], cvb[pb][:, :], AF.Silu), inc=(s_sc, 1))
            sc_chunk[idx] = s_sc.n
            if is_x:
                P.emit("scalar", lambda e, pb=pb, ci=ci: e.dma_start(
                    out=x_scr[128 * ci:128 * (ci + 1), :], in_=xout[pb][:, :]), inc=(s_xo[pb], 16), dma=True)
                sp_chunk[idx] = (s_xo[pb], s_xo[pb].n)
            else:
                dst = bfm if cc == 128 else cfm
                P.wait("vector", s_sc, sc_chunk[idx])
                P.emit("vector", lambda v, pb=pb, dst=dst: v.tensor_copy(dst[:, :], xout[pb][:, :]), inc=(s_v, 1))
                v_chunk[idx] = s_v.n
                sp_chunk[idx] = (s_spg, s_spg.n)
        if is_z and (cc % 8 == 7):
            plan_mix_tile(cc // 8)

    P.wait("vector", s_v, v_chunk[idx_dt])
    P.wait("vector", s_p0, s_p0.n)
    P.emit("vector", lambda v: v.scalar_tensor_tensor(
        dt_sp[:, :], dtraw[:, HALO:], dtb_col[:, 0:1], dtraw[:, HALO:], ALU.add, ALU.bypass), inc=(s_v, 1))
    P.wait("scalar", s_v, s_v.n)
    P.emit("scalar", lambda e: e.activation(logdA[:, :], dt_sp[:, :], AF.Abs))
    P.emit("scalar", lambda e: e.activation(logdA[:, :], logdA[:, :], AF.Exp, scale=-1.0))
    P.emit("scalar", lambda e: e.activation(logdA[:, :], logdA[:, :], AF.Ln, bias=cbias[0:NH, 2:3]), inc=(s_sc, 1))
    P.wait("vector", s_sc, s_sc.n)
    P.emit("vector", lambda v: v.tensor_relu(dt_sp[:, :], dt_sp[:, :]))
    P.emit("vector", lambda v: v.tensor_add(dt_sp[:, :], dt_sp[:, :], logdA[:, :]))
    P.emit("vector", lambda v: v.scalar_tensor_tensor(
        logdA[:, :], dt_sp[:, :], nega_col[:, 0:1], dt_sp[:, :], ALU.mult, ALU.bypass), inc=(s_v, 1))
    v_dtprep = s_v.n
    if debug:
        P.wait("scalar", s_v, v_dtprep)
        P.emit("scalar", lambda e: e.dma_start(out=dbg["dbg_dt"], in_=dtraw[:]), inc=(s_spg, 16), dma=True)

    if debug:
        P.wait("sync", s_v, s_v.n)
        P.emit("sync", lambda e: e.dma_start(out=dbg["dbg_bc"].rearrange("p (a t) -> p a t", a=2)[:, 0:1, :], in_=bfm[:, :].unsqueeze(1)), inc=(s_spg, 16), dma=True)
        P.emit("sync", lambda e: e.dma_start(out=dbg["dbg_bc"].rearrange("p (a t) -> p a t", a=2)[:, 1:2, :], in_=cfm[:, :].unsqueeze(1)), inc=(s_spg, 16), dma=True)
        P.wait("sync", s_spg, s_spg.n)
    P.barrier(s_bar)
    cxW.close()
    cx3.close()
    cx2.close()
    cxP12.close()

    cxH = contextlib.ExitStack()
    ab_t = [sb(cxH, f"ab{c}", [128, NH], F32) for c in range(NCHUNK)]
    h_f32 = sb(cxH, "h_f32", [128, NH * 128], F32)
    h_bf = sb(cxH, "h_bf", [128, NH * 128], BF16)
    cx3a = contextlib.ExitStack()
    Xc = [sb(cx3a, f"Xc{i}", [128, NH * 128], BF16) for i in range(2)]
    XTc = [sb(cx3a, f"XTc{i}", [128, NH * 128], BF16) for i in range(2)]
    rt_b = [sb(cx3a, f"rt_b{i}", [128, 512], F32) for i in range(2)]
    rtm_b = [sb(cx3a, f"rtm_b{i}", [128, 512], F32) for i in range(2)]
    edec_b = [sb(cx3a, f"edec_b{i}", [128, 512], BF16) for i in range(2)]
    l4_b = [sb(cx3a, f"l4_b{i}", [128, 512], BF16) for i in range(2)]
    cdec_b = [sb(cx3a, f"cdec_b{i}", [128, 512], BF16) for i in range(2)]
    m4_b = [sb(cx3a, f"m4_b{i}", [128, 512], BF16) for i in range(2)]
    crep = sb(cx3a, "crep", [128, 512], BF16)
    grep = sb(cx3a, "grep", [128, 512], BF16)
    y4_b = [sb(cx3a, f"y4_b{i}", [128, 512], BF16) for i in range(2)]
    gm_b = sb(cx3a, "gm_b", [128, 128], BF16)
    bt_b = sb(cx3a, "bt_b", [128, 128], BF16)
    dtxT = [sb(cx3a, f"dtxT{i}", [128, 512], BF16) for i in range(2)]
    bdT = [sb(cx3a, f"bdT{i}", [128, 512], BF16) for i in range(2)]
    stk = sb(cx3a, "stk", [128, 128], F32)
    trc = sb(cx3a, "trc", [128, 128], F32)
    dtT_bf = sb(cx3a, "dtT_bf", [128, NH], BF16)
    wT_f = sb(cx3a, "wT_f", [128, NH], F32)
    wT_bf = sb(cx3a, "wT_bf", [128, NH], BF16)
    cs_hm = sb(cx3a, "cs_hm", [NH, Q], F32)
    w_hm = sb(cx3a, "w_hm", [NH, Q], F32)

    P.emit("vector", lambda v: v.memset(h_f32[:], 0.0))
    P.emit("vector", lambda v: v.memset(h_bf[:], 0.0))
    P.emit("vector", lambda v: v.memset(logtot[:], 0.0), inc=(s_v, 1))

    def slab_load(dst, src, c, sem):
        for g8 in range(8):
            P.emit("sync", lambda e, dst=dst, src=src, c=c, g8=g8: e.dma_start(
                out=dst.ap().rearrange("p (hh t) -> p hh t", t=Q)[:, 8 * g8:8 * (g8 + 1), :],
                in_=src.rearrange("(hh p) t -> p hh t", p=128)[:, 8 * g8:8 * (g8 + 1), Q * c:Q * (c + 1)]),
                inc=(sem, 16), dma=True)

    def slab_T_load(dst, src, c, sem):
        for g8 in range(8):
            P.emit("sync", lambda e, dst=dst, src=src, c=c, g8=g8: e.dma_start_transpose(
                out=dst[:, 1024 * g8:1024 * (g8 + 1)],
                in_=src[1024 * g8:1024 * (g8 + 1), Q * c:Q * (c + 1)]).then_inc(sem.h, 16))
            sem.n += 16

    for s_ in (s_xo[0], s_xo[1], s_spxn[0], s_spxn[1], s_spg):
        P.wait("sync", s_, s_.n)
    slab_load(Xc[0], x_scr, 0, s_sl[0])
    slab_T_load(XTc[0], x_scr, 0, s_sl[0])
    slab_ld = {0: (s_sl[0], s_sl[0].n)}
    last_grp_v = {}
    last_grp_sp = {}
    last_grp_te = {}
    last_grp_sc = {}
    last_rt_sc = {}
    sc_ab_hist = {}
    last_cs_ld = {0: [], 1: []}
    prev_sp_cs = (s_spg, 0)

    for c in range(NCHUNK):
        pb = c % 2
        if c + 1 < NCHUNK:
            if c >= 1:
                P.wait("sync", s_v, last_grp_v.get((c - 1, 15), 0))
            slab_load(Xc[(c + 1) % 2], x_scr, c + 1, s_sl[(c + 1) % 2])
            slab_T_load(XTc[(c + 1) % 2], x_scr, c + 1, s_sl[(c + 1) % 2])
            slab_ld[c + 1] = (s_sl[(c + 1) % 2], s_sl[(c + 1) % 2].n)
        P.wait("vector", s_v, v_dtprep)
        if c >= 1:
            P.wait("vector", s_te, last_grp_te.get((c - 1, 15), 0))
            P.wait("vector", s_sc, last_grp_sc.get((c - 1, 15), 0))
            P.wait("vector", *prev_sp_cs)
        P.emit("vector", lambda v, c=c: v.tensor_tensor_scan(
            cs_hm[:, :], logdA[:, Q * c:Q * (c + 1)], logdA[:, Q * c:Q * (c + 1)],
            0.0, ALU.add, ALU.bypass))
        P.emit("vector", lambda v: v.tensor_add(logtot[:, :], logtot[:, :], cs_hm[:, Q - 1:Q]))
        P.emit("vector", lambda v, c=c: v.tensor_copy(stk[0:64, :], dt_sp[:, Q * c:Q * (c + 1)]))
        P.emit("vector", lambda v: v.tensor_copy(stk[64:128, :], cs_hm[:, :]), inc=(s_v, 1))
        v_stk = s_v.n
        P.wait("scalar", s_v, v_stk)
        for s_, v_ in last_cs_ld[pb]:
            P.wait("scalar", s_, v_)
        P.emit("scalar", lambda e, c=c: e.dma_start(out=cs_scr[c % 2], in_=cs_hm[:, :]), inc=(s_spcs[pb], 16), dma=True)
        P.emit("scalar", lambda e, c=c: e.dma_start(out=ce_scr[c % 2], in_=cs_hm[:, Q - 1:Q]), inc=(s_spcs[pb], 16), dma=True)
        sp_cs = (s_spcs[pb], s_spcs[pb].n)
        prev_sp_cs = sp_cs
        P.emit("scalar", lambda e: e.activation(
            w_hm[:, :], cs_hm[:, :], AF.Exp, bias=cs_hm[:, Q - 1:Q], scale=-1.0), inc=(s_sc, 1))
        sc_whm = s_sc.n
        P.wait("tensor", s_v, v_stk)
        P.emit("tensor", lambda t: t.transpose(pf[4].ap()[0:128, 0:128], stk[:, :], ident_f[:, :]), inc=(s_te, 1))
        P.wait("vector", s_te, s_te.n)
        P.emit("vector", lambda v: v.tensor_copy(trc[:, :], pf[4][0:128, 0:128]))
        P.emit("vector", lambda v: v.tensor_copy(dtT_bf[:, :], trc[:, 0:64]))
        P.emit("vector", lambda v: v.tensor_scalar_mul(trc[:, 64:128], trc[:, 64:128], -1.0), inc=(s_v, 1))
        v_trc = s_v.n
        P.wait("tensor", s_sc, sc_whm)
        P.emit("tensor", lambda t: t.transpose(pf[5].ap()[0:128, 0:64], w_hm[:, :], ident_f[0:64, 0:64]), inc=(s_te, 1))
        P.wait("vector", s_te, s_te.n)
        P.emit("vector", lambda v: v.tensor_copy(wT_f[:, :], pf[5][0:128, 0:64]))
        P.emit("vector", lambda v: v.tensor_copy(wT_bf[:, :], wT_f[:, :]), inc=(s_v, 1))
        v_trans = s_v.n
        P.wait("sync", *sp_cs)
        if c >= 2:
            P.wait("sync", s_sc, sc_ab_hist[c - 2])
        P.emit("sync", lambda e, c=c: e.dma_start(
            out=ab_t[c][:, :],
            in_=ce_scr[c % 2].rearrange("h o -> o h").broadcast_to((128, NH))), inc=(s_ab[pb], 16), dma=True)
        P.wait("scalar", s_ab[pb], s_ab[pb].n)
        P.emit("scalar", lambda e, c=c: e.activation(ab_t[c][:, :], ab_t[c][:, :], AF.Exp), inc=(s_sc, 1))
        sc_ab = s_sc.n
        sc_ab_hist[c] = sc_ab
        P.wait("tensor", s_v, v_trc)
        P.emit("tensor", lambda t, c=c: t.transpose(
            pbf[0][0:128, 0:128], bfm[:, Q * c:Q * (c + 1)], ident_bf[:, :]), inc=(s_te, 1))
        P.emit("tensor", lambda t, c=c: t.matmul(
            pf[4].ap()[0:128, 0:128], bfm[:, Q * c:Q * (c + 1)], cfm[:, Q * c:Q * (c + 1)],
            start=True, stop=True), inc=(s_te, 1))
        P.wait("vector", s_te, s_te.n)
        P.emit("vector", lambda v: v.tensor_copy(bt_b[:, :], pbf[0][0:128, 0:128]))
        P.emit("vector", lambda v: v.tensor_copy(gm_b[:, :], pf[4][0:128, 0:128]))
        for j in range(4):
            P.emit("vector", lambda v, c=c, j=j: v.tensor_copy(crep[:, 128 * j:128 * (j + 1)], cfm[:, Q * c:Q * (c + 1)]))
        for j in range(4):
            P.emit("vector", lambda v, j=j: v.tensor_copy(grep[:, 128 * j:128 * (j + 1)], gm_b[:, :]),
                   inc=(s_v, 1) if j == 3 else None)
        v_btg = s_v.n
        rt_ld = {}
        for g in range(2):
            P.wait("sync", *sp_cs)
            P.wait("sync", s_sc, last_rt_sc.get(g % 2, 0))
            P.emit("sync", lambda e, c=c, g=g: e.dma_start(
                out=rt_b[g % 2][:, :],
                in_=cs_scr[c % 2].rearrange("h t -> (h t)").unsqueeze(0)[0:1, 512 * g:512 * (g + 1)].broadcast_to((128, 512))),
                inc=(s_rt[g % 2], 16), dma=True)
            rt_ld[g] = (s_rt[g % 2], s_rt[g % 2].n)

        for g in range(16):
            gb = g % 2
            py = pf[0 + gb]
            pS = pf[2 + gb]
            P.wait("vector", *rt_ld[g])
            P.wait("vector", s_v, v_btg)
            P.wait("vector", s_sc, last_grp_sc.get((c, g - 2), 0))
            P.wait("vector", s_te, last_grp_te.get((c, g - 2), last_grp_te.get((c - 1, 14 + g), 0)))
            P.wait("vector", *last_grp_sp.get((c, g - 2), last_grp_sp.get((c - 1, 14 + g), (s_spg, 0))))
            P.emit("vector", lambda v, gb=gb: v.tensor_add(rtm_b[gb][:, :], rt_b[gb][:, :], mneg4[:, :]), inc=(s_v, 1))
            v_rtm = s_v.n
            P.wait("scalar", s_v, v_rtm)
            P.wait("scalar", s_v, v_trans)
            P.emit("scalar", lambda e, gb=gb: e.activation(edec_b[gb][:, :], rt_b[gb][:, :], AF.Exp), inc=(s_sc, 1))
            for hj in range(4):
                hh = 4 * g + hj
                P.emit("scalar", lambda e, gb=gb, hj=hj, hh=hh: e.activation(
                    l4_b[gb][:, 128 * hj:128 * (hj + 1)], rtm_b[gb][:, 128 * hj:128 * (hj + 1)],
                    AF.Exp, bias=trc[:, 64 + hh:65 + hh]), inc=(s_sc, 1) if hj == 3 else None)
            sc_exps = s_sc.n
            last_rt_sc[gb] = sc_exps
            last_grp_sc[(c, g)] = sc_exps
            if g + 2 < 16:
                P.wait("sync", s_sc, sc_exps)
                P.emit("sync", lambda e, c=c, g=g, gb=gb: e.dma_start(
                    out=rt_b[gb][:, :],
                    in_=cs_scr[c % 2].rearrange("h t -> (h t)").unsqueeze(0)[0:1, 512 * (g + 2):512 * (g + 3)].broadcast_to((128, 512))),
                    inc=(s_rt[gb], 16), dma=True)
                rt_ld[g + 2] = (s_rt[gb], s_rt[gb].n)
            P.wait("vector", s_sc, sc_exps)
            P.emit("vector", lambda v, gb=gb: v.tensor_mul(cdec_b[gb][:, :], crep[:, :], edec_b[gb][:, :]))
            P.emit("vector", lambda v, gb=gb: v.tensor_mul(m4_b[gb][:, :], grep[:, :], l4_b[gb][:, :]), inc=(s_v, 1))
            P.wait("vector", *slab_ld[c])
            for hj in range(4):
                hh = 4 * g + hj
                P.emit("vector", lambda v, pb=pb, gb=gb, hh=hh, hj=hj: v.scalar_tensor_tensor(
                    dtxT[gb][:, 128 * hj:128 * (hj + 1)], XTc[pb][:, 128 * hh:128 * (hh + 1)],
                    dtT_bf[:, hh:hh + 1], XTc[pb][:, 128 * hh:128 * (hh + 1)], ALU.mult, ALU.bypass))
                P.emit("vector", lambda v, gb=gb, hh=hh, hj=hj: v.scalar_tensor_tensor(
                    bdT[gb][:, 128 * hj:128 * (hj + 1)], bt_b[:, :], wT_bf[:, hh:hh + 1],
                    bt_b[:, :], ALU.mult, ALU.bypass), inc=(s_v, 1) if hj == 3 else None)
            v_heads = s_v.n
            P.wait("tensor", s_v, v_heads)
            P.wait("tensor", s_v, last_grp_v.get((c, g - 2), last_grp_v.get((c - 1, 14 + g), 0)))
            for hj in range(4):
                hh = 4 * g + hj
                P.emit("tensor", lambda t, gb=gb, hj=hj, py=py: t.matmul(
                    py.ap()[0:128, 128 * hj:128 * (hj + 1)], dtxT[gb][:, 128 * hj:128 * (hj + 1)],
                    m4_b[gb][:, 128 * hj:128 * (hj + 1)], start=True, stop=False))
                P.emit("tensor", lambda t, gb=gb, hj=hj, hh=hh, py=py: t.matmul(
                    py.ap()[0:128, 128 * hj:128 * (hj + 1)], h_bf[:, 128 * hh:128 * (hh + 1)],
                    cdec_b[gb][:, 128 * hj:128 * (hj + 1)], start=False, stop=True))
                P.emit("tensor", lambda t, gb=gb, hj=hj, pS=pS: t.matmul(
                    pS.ap()[0:128, 128 * hj:128 * (hj + 1)], bdT[gb][:, 128 * hj:128 * (hj + 1)],
                    dtxT[gb][:, 128 * hj:128 * (hj + 1)], start=True, stop=True),
                    inc=(s_te, 1) if hj == 3 else None)
            last_grp_te[(c, g)] = s_te.n
            P.wait("vector", s_te, last_grp_te[(c, g)])
            P.wait("vector", s_sc, sc_ab)
            for hj in range(4):
                hh = 4 * g + hj
                P.emit("vector", lambda v, c=c, hh=hh, hj=hj, pS=pS: v.scalar_tensor_tensor(
                    h_f32[:, 128 * hh:128 * (hh + 1)], h_f32[:, 128 * hh:128 * (hh + 1)],
                    ab_t[c][:, hh:hh + 1], pS[0:128, 128 * hj:128 * (hj + 1)], ALU.mult, ALU.add))
                P.emit("vector", lambda v, pb=pb, gb=gb, hh=hh, hj=hj, py=py: v.scalar_tensor_tensor(
                    y4_b[gb][:, 128 * hj:128 * (hj + 1)], Xc[pb][:, 128 * hh:128 * (hh + 1)],
                    dbc[:, hh:hh + 1], py[0:128, 128 * hj:128 * (hj + 1)], ALU.mult, ALU.add),
                    inc=(s_v, 1) if hj == 3 else None)
            last_grp_v[(c, g)] = s_v.n
            P.wait("scalar", s_v, last_grp_v[(c, g)])
            P.emit("scalar", lambda e, g=g: e.copy(
                h_bf[:, 512 * g:512 * (g + 1)], h_f32[:, 512 * g:512 * (g + 1)]), inc=(s_sc, 1))
            P.emit("scalar", lambda e, c=c, g=g, gb=gb: e.dma_start(
                out=y_scr.rearrange("(hh p) t -> p hh t", p=128)[:, 4 * g:4 * (g + 1), Q * c:Q * (c + 1)],
                in_=y4_b[gb].ap().rearrange("p (hh t) -> p hh t", t=Q)), inc=(s_y[gb], 16), dma=True)
            P.emit("scalar", lambda e, c=c, g=g, gb=gb: e.dma_start(
                out=cd_scr.rearrange("(hh n) t -> n hh t", n=128)[:, 4 * g:4 * (g + 1), Q * c:Q * (c + 1)],
                in_=cdec_b[gb].ap().rearrange("n (hh t) -> n hh t", t=Q)), inc=(s_y[gb], 16), dma=True)
            last_grp_sp[(c, g)] = (s_y[gb], s_y[gb].n)
        last_cs_ld[pb] = [rt_ld[14], rt_ld[15], (s_ab[pb], s_ab[pb].n)]
    v_p3a = s_v.n
    if debug:
        P.wait("scalar", s_v, v_p3a)
        P.emit("scalar", lambda e: e.dma_start(out=dbg["dbg_h"], in_=h_f32[:]), inc=(s_spg, 16), dma=True)

    P.barrier(s_bar)
    cx3a.close()

    cxX = contextlib.ExitStack()
    metag = sb(cxX, "metag", [8, NH], F32)
    w8 = sb(cxX, "w8", [8, NH], F32)
    pmcol = sb(cxX, "pmcol", [8, 1], F32)
    btT = sb(cxX, "btT", [8, 8], F32)
    meta_bc = sb(cxX, "meta_bc", [128, NH], F32)
    wbc = [sb(cxX, f"wbc{j}", [128, NH], F32) for j in range(8)]
    sg_b = [sb(cxX, f"sg_b{i}", [128, NH * 128], BF16) for i in range(2)]

    for g8 in range(8):
        P.emit("scalar", lambda e, g8=g8: e.dma_start(
            out=cc_s_in.ap().rearrange("(hh n) p -> n hh p", n=128)[:, 8 * g8:8 * (g8 + 1), :],
            in_=h_bf.ap().rearrange("n (hh p) -> n hh p", p=128)[:, 8 * g8:8 * (g8 + 1), :]), inc=(s_ccw, 16), dma=True)
    P.emit("scalar", lambda e: e.dma_start(out=small_scr[0:1, 0:64].rearrange("o h -> h o"),
                                           in_=logtot[:, :]), inc=(s_sm, 16), dma=True)
    P.wait("sync", s_sm, s_sm.n)
    P.emit("sync", lambda e: e.dma_start(out=meta_bc[:], in_=small_scr[0:1, 0:64].broadcast_to((128, NH))), inc=(s_meta, 16), dma=True)
    P.wait("scalar", s_meta, s_meta.n)
    P.emit("scalar", lambda e: e.dma_start(out=cc_m_in.ap(), in_=meta_bc[:]), inc=(s_ccw, 16), dma=True)
    P.wait("gpsimd", s_ccw, s_ccw.n)
    P.emit("gpsimd", lambda g: g.collective_compute(
        "AllGather", ALU.bypass, replica_groups=[[0, 1, 2, 3, 4, 5, 6, 7]],
        ins=[cc_s_in.ap().opt()], outs=[cc_s_out.ap().opt()]), inc=(s_cc, 1))
    P.emit("gpsimd", lambda g: g.collective_compute(
        "AllGather", ALU.bypass, replica_groups=[[0, 1, 2, 3, 4, 5, 6, 7]],
        ins=[cc_m_in.ap().opt()], outs=[cc_m_out.ap().opt()]), inc=(s_cc, 1))
    P.wait("sync", s_cc, 2)
    P.wait("tensor", s_cc, 2)

    P.emit("sync", lambda e: e.dma_start(
        out=metag[:, :],
        in_=cc_m_out.ap().rearrange("(j r) h -> j r h", r=128)[:, 0:1, :].rearrange("j o h -> j (o h)")), inc=(s_meta, 16), dma=True)
    P.emit("sync", lambda e: e.dma_start(out=btT[:, :], in_=betweenT), inc=(s_meta, 16), dma=True)
    P.emit("sync", lambda e: e.dma_start(out=pmcol[:, :], in_=prevmask), inc=(s_meta, 16), dma=True)
    P.wait("tensor", s_meta, s_meta.n)
    P.emit("tensor", lambda t: t.matmul(pf[4].ap()[0:8, 0:NH], btT[:, :], metag[:, :], start=True, stop=True), inc=(s_te, 1))
    P.wait("scalar", s_te, s_te.n)
    P.emit("scalar", lambda e: e.activation(w8[:, :], pf[4][0:8, 0:NH], AF.Exp), inc=(s_sc, 1))
    P.wait("vector", s_sc, s_sc.n)
    P.emit("vector", lambda v: v.scalar_tensor_tensor(
        w8[:, :], w8[:, :], pmcol[:, 0:1], w8[:, :], ALU.mult, ALU.bypass), inc=(s_v, 1))
    P.wait("scalar", s_v, s_v.n)
    P.emit("scalar", lambda e: e.dma_start(out=wj_scr.rearrange("o (j h) -> j h", j=8), in_=w8[:, :]), inc=(s_ccw, 16), dma=True)
    P.wait("sync", s_ccw, s_ccw.n)
    for j in range(8):
        P.emit("sync", lambda e, j=j: e.dma_start(
            out=wbc[j][:, :], in_=wj_scr[0:1, NH * j:NH * (j + 1)].broadcast_to((128, NH))), inc=(s_meta, 16), dma=True)
    P.emit("vector", lambda v: v.memset(h_f32[:], 0.0), inc=(s_v, 1))
    vj = {}
    for j in range(8):
        jb = j % 2
        if j >= 2:
            P.wait("sync", s_v, vj[j - 2])
        for g8 in range(8):
            P.emit("sync", lambda e, j=j, jb=jb, g8=g8: e.dma_start(
                out=sg_b[jb].ap().rearrange("n (hh p) -> n hh p", p=128)[:, 8 * g8:8 * (g8 + 1), :],
                in_=cc_s_out.ap().rearrange("(j hh n) p -> j n hh p", hh=NH, n=128)[j][:, 8 * g8:8 * (g8 + 1), :]),
                inc=(s_sl[jb], 16), dma=True)
        P.wait("vector", s_sl[jb], s_sl[jb].n)
        P.wait("vector", s_meta, s_meta.n)
        for hh in range(NH):
            P.emit("vector", lambda v, j=j, jb=jb, hh=hh: v.scalar_tensor_tensor(
                h_f32[:, 128 * hh:128 * (hh + 1)], sg_b[jb][:, 128 * hh:128 * (hh + 1)],
                wbc[j][:, hh:hh + 1], h_f32[:, 128 * hh:128 * (hh + 1)], ALU.mult, ALU.add),
                inc=(s_v, 1) if hh == NH - 1 else None)
        vj[j] = s_v.n
    P.wait("scalar", s_v, s_v.n)
    P.emit("scalar", lambda e: e.copy(h_bf[:, :], h_f32[:, :]), inc=(s_sc, 1))
    P.barrier(s_bar)
    cxX.close()

    cx3b = contextlib.ExitStack()
    Yc = [sb(cx3b, f"Yc{i}", [128, NH * 128], BF16) for i in range(2)]
    Zc = [sb(cx3b, f"Zc{i}", [128, NH * 128], BF16) for i in range(2)]
    Cc_ = [sb(cx3b, f"Cc_{i}", [128, NH * 128], BF16) for i in range(2)]
    o2_b = [sb(cx3b, f"o2_b{i}", [128, 512], F32) for i in range(2)]
    th_b = [sb(cx3b, f"th_b{i}", [128, 512], F32) for i in range(2)]
    o3_b = [sb(cx3b, f"o3_b{i}", [128, 512], BF16) for i in range(2)]
    sq_b = [sb(cx3b, f"sq_b{i}", [128, 512], BF16) for i in range(2)]
    o4_b = [sb(cx3b, f"o4_b{i}", [128, 512], BF16) for i in range(2)]
    rrow = sb(cx3b, "rrow", [1, Q], F32)

    for s_ in (s_y[0], s_y[1], s_z[0], s_z[1]):
        P.wait("sync", s_, s_.n)
    slab_load(Yc[0], y_scr, 0, s_sl[0])
    slab_load(Zc[0], z_scr, 0, s_sl[0])
    slab_load(Cc_[0], cd_scr, 0, s_sl[0])
    slab3b_ld = {0: (s_sl[0], s_sl[0].n)}
    bl_v = {}
    bl_sp = {}
    bl_te = {}
    bl_sc = {}
    bl_sqte = {}
    rrow_v = {}

    for c in range(NCHUNK):
        pb = c % 2
        if c + 1 < NCHUNK:
            if c >= 1:
                P.wait("sync", s_v, bl_v.get((c - 1, 15), 0))
                P.wait("sync", s_sc, bl_sc.get((c - 1, 15), 0))
                P.wait("sync", s_te, bl_te.get((c - 1, 15), 0))
            slab_load(Yc[(c + 1) % 2], y_scr, c + 1, s_sl[(c + 1) % 2])
            slab_load(Zc[(c + 1) % 2], z_scr, c + 1, s_sl[(c + 1) % 2])
            slab_load(Cc_[(c + 1) % 2], cd_scr, c + 1, s_sl[(c + 1) % 2])
            slab3b_ld[c + 1] = (s_sl[(c + 1) % 2], s_sl[(c + 1) % 2].n)
        for g in range(16):
            gb = g % 2
            py = pf[0 + gb]
            P.wait("tensor", *slab3b_ld[c])
            P.wait("tensor", s_v, bl_v.get((c, g - 2), bl_v.get((c - 1, 14 + g), 0)))
            for hj in range(4):
                hh = 4 * g + hj
                P.emit("tensor", lambda t, pb=pb, hh=hh, hj=hj, py=py: t.matmul(
                    py.ap()[0:128, 128 * hj:128 * (hj + 1)], h_bf[:, 128 * hh:128 * (hh + 1)],
                    Cc_[pb][:, 128 * hh:128 * (hh + 1)], start=True, stop=True),
                    inc=(s_te, 1) if hj == 3 else None)
            bl_te[(c, g)] = s_te.n
            P.wait("vector", s_te, bl_te[(c, g)])
            P.wait("vector", s_sc, bl_sc.get((c, g - 2), 0))
            P.emit("vector", lambda v, pb=pb, g=g, gb=gb, py=py: v.tensor_add(
                o2_b[gb][:, :], py[0:128, 0:512], Yc[pb][:, 512 * g:512 * (g + 1)]), inc=(s_v, 1))
            v_o2 = s_v.n
            P.wait("scalar", s_v, v_o2)
            P.wait("scalar", s_te, bl_sqte.get((c, g - 2), 0))
            P.emit("scalar", lambda e, pb=pb, g=g, gb=gb: e.activation(
                th_b[gb][:, :], Zc[pb][:, 512 * g:512 * (g + 1)], AF.Silu), inc=(s_sc, 1))
            P.wait("vector", s_sc, s_sc.n)
            P.emit("vector", lambda v, gb=gb: v.tensor_mul(o3_b[gb][:, :], o2_b[gb][:, :], th_b[gb][:, :]), inc=(s_v, 1))
            v_o3 = s_v.n
            P.wait("scalar", s_v, v_o3)
            P.emit("scalar", lambda e, gb=gb: e.activation(sq_b[gb][:, :], o3_b[gb][:, :], AF.Square), inc=(s_sc, 1))
            bl_sc[(c, g)] = s_sc.n
            P.wait("vector", *bl_sp.get((c, g - 2), bl_sp.get((c - 1, 14 + g), (s_spg, 0))))
            for hj in range(4):
                hh = 4 * g + hj
                P.emit("vector", lambda v, gb=gb, hh=hh, hj=hj: v.scalar_tensor_tensor(
                    o4_b[gb][:, 128 * hj:128 * (hj + 1)], o3_b[gb][:, 128 * hj:128 * (hj + 1)],
                    nwcols[:, hh:hh + 1], o3_b[gb][:, 128 * hj:128 * (hj + 1)], ALU.mult, ALU.bypass))
                P.emit("vector", lambda v, c=c, hh=hh: v.scalar_tensor_tensor(
                    h_f32[:, 128 * hh:128 * (hh + 1)], h_f32[:, 128 * hh:128 * (hh + 1)],
                    ab_t[c][:, hh:hh + 1], h_f32[:, 128 * hh:128 * (hh + 1)], ALU.mult, ALU.bypass),
                    inc=(s_v, 1) if hj == 3 else None)
            bl_v[(c, g)] = s_v.n
            P.wait("scalar", s_v, bl_v[(c, g)])
            P.emit("scalar", lambda e, g=g: e.copy(
                h_bf[:, 512 * g:512 * (g + 1)], h_f32[:, 512 * g:512 * (g + 1)]), inc=(s_sc, 1))
            P.wait("tensor", s_sc, bl_sc[(c, g)])
            if g == 0 and c >= 1:
                P.wait("tensor", s_v, rrow_v[c - 1])
            for hj in range(4):
                P.emit("tensor", lambda t, g=g, gb=gb, hj=hj: t.matmul(
                    pf[5].ap()[0:1, 0:Q], onescol_bf[:, 0:1], sq_b[gb][:, 128 * hj:128 * (hj + 1)],
                    start=(g == 0 and hj == 0), stop=(g == 15 and hj == 3), skip_group_check=True),
                    inc=(s_te, 1) if hj == 3 else None)
            bl_sqte[(c, g)] = s_te.n
            P.emit("scalar", lambda e, c=c, g=g, gb=gb: e.dma_start(
                out=yn_scr.rearrange("(hh p) t -> p hh t", p=128)[:, 4 * g:4 * (g + 1), Q * c:Q * (c + 1)],
                in_=o4_b[gb].ap().rearrange("p (hh t) -> p hh t", t=Q)), inc=(s_y[gb], 16), dma=True)
            bl_sp[(c, g)] = (s_y[gb], s_y[gb].n)
        P.wait("vector", s_te, bl_sqte[(c, 15)])
        P.wait("vector", s_spg, s_spg.n)
        P.emit("vector", lambda v: v.tensor_copy(rrow[:, :], pf[5][0:1, 0:Q]), inc=(s_v, 1))
        rrow_v[c] = s_v.n
        P.wait("scalar", s_v, s_v.n)
        P.emit("scalar", lambda e: e.activation(rrow[:, :], rrow[:, :], AF.Ln, bias=cbias[0:1, 3:4], scale=1.0 / DSSM))
        P.emit("scalar", lambda e: e.activation(rrow[:, :], rrow[:, :], AF.Exp, scale=-0.5), inc=(s_sc, 1))
        P.wait("scalar", s_sc, s_sc.n)
        P.emit("scalar", lambda e, c=c: e.dma_start(out=rstd_scr[c:c + 1, :], in_=rrow[:, :]), inc=(s_spg, 16), dma=True)
    P.barrier(s_bar)
    cx3b.close()
    cxH.close()

    cx4 = contextlib.ExitStack()
    wo_bf = [sb(cx4, f"wo_bf{i}", [128, NH * 512], BF16) for i in range(2)]
    yns = [sb(cx4, f"yns{i}", [128, SPAN], BF16) for i in range(2)]
    rst8 = sb(cx4, "rst8", [NCHUNK, Q], F32)
    rstT = sb(cx4, "rstT", [128, NCHUNK], F32)
    osc = [sb(cx4, f"osc{i}", [128, 512], F32) for i in range(2)]
    fin = [sb(cx4, f"fin{i}", [128, 512], F32) for i in range(2)]

    P.wait("sync", s_spg, s_spg.n)
    for s_ in (s_y[0], s_y[1]):
        P.wait("sync", s_, s_.n)
    P.emit("sync", lambda e: e.dma_start(out=rst8[:, :], in_=rstd_scr), inc=(s_meta, 16), dma=True)
    P.wait("tensor", s_meta, s_meta.n)
    P.emit("tensor", lambda t: t.transpose(pf[4].ap()[0:128, 0:8], rst8[:, :], ident_f[0:8, 0:8]), inc=(s_te, 1))
    P.wait("vector", s_te, s_te.n)
    P.emit("vector", lambda v: v.tensor_copy(rstT[:, :], pf[4][0:128, 0:8]), inc=(s_v, 1))
    v_rstT = s_v.n

    def emit_woload(dc):
        wb = dc % 2
        for gkc in range(4):
            P.emit("gpsimd", lambda g, wb=wb, dc=dc, gkc=gkc: g.dma_start(
                out=wo_bf[wb].ap().rearrange("p (kc c) -> p kc c", c=512)[:, 16 * gkc:16 * (gkc + 1), :],
                in_=w_out.rearrange("(kc p) c -> p kc c", p=128)[:, 16 * gkc:16 * (gkc + 1), 512 * dc:512 * (dc + 1)]),
                inc=(s_w[wb], 16), dma=True)
        wo_ld[dc] = (s_w[wb], s_w[wb].n)

    wo_ld = {}
    emit_woload(0)
    te_dc = {}
    te_ch = {}
    v_evac = {}
    st_evac = {}
    pview = [pf[i].ap()[0:128, 0:512] for i in range(6)] + \
            [pbf[i].ap().bitcast(F32)[0:128, 0:512] for i in range(2)]

    for dc in range(8):
        wb = dc % 2
        if dc + 1 < 8:
            if dc >= 1:
                P.wait("gpsimd", s_te, te_dc[dc - 1])
            emit_woload(dc + 1)
        P.wait("tensor", *wo_ld[dc])
        P.wait("tensor", s_v, v_rstT)
        if dc >= 1:
            P.wait("tensor", s_v, v_evac[dc - 1])
        for ch in range(NH):
            yb = ch % 2
            P.wait("sync", s_te, te_ch.get((dc, ch - 2), te_ch.get((dc - 1, NH - 2 + ch), 0)))
            P.emit("sync", lambda e, yb=yb, ch=ch: e.dma_start(
                out=yns[yb][:, :], in_=yn_scr[128 * ch:128 * (ch + 1), :]), inc=(s_sl[yb], 16), dma=True)
            P.wait("tensor", s_sl[yb], s_sl[yb].n)
            for tc in range(8):
                P.emit("tensor", lambda t, wb=wb, yb=yb, ch=ch, tc=tc: t.matmul(
                    pview[tc], yns[yb][:, 128 * tc:128 * (tc + 1)], wo_bf[wb][:, 512 * ch:512 * (ch + 1)],
                    start=(ch == 0), stop=(ch == NH - 1), skip_group_check=True),
                    inc=(s_te, 1) if tc == 7 else None)
            te_ch[(dc, ch)] = s_te.n
        te_dc[dc] = s_te.n
        P.wait("vector", s_te, te_dc[dc])
        for tc in range(8):
            ob = tc % 2
            if tc >= 2 or dc >= 1:
                P.wait("sync", s_v, v_evac.get((dc, tc - 2), v_evac.get((dc - 1, 6 + tc), 0)))
            P.emit("sync", lambda e, ob=ob, tc=tc, dc=dc: e.dma_start(
                out=osc[ob][:, :], in_=out[128 * tc:128 * (tc + 1), 512 * dc:512 * (dc + 1)]), inc=(s_os[ob], 16), dma=True)
            P.wait("vector", s_os[ob], s_os[ob].n)
            if tc >= 2 or dc >= 1:
                P.wait("vector", *st_evac.get((dc, tc - 2), st_evac.get((dc - 1, 6 + tc), (s_spg, 0))))
            P.emit("vector", lambda v, ob=ob, tc=tc: v.scalar_tensor_tensor(
                fin[ob][:, :], pview[tc], rstT[:, tc:tc + 1], osc[ob][:, :], ALU.mult, ALU.add), inc=(s_v, 1))
            v_evac[(dc, tc)] = s_v.n
            P.wait("sync", s_v, s_v.n)
            P.emit("sync", lambda e, ob=ob, tc=tc, dc=dc: e.dma_start(
                out=out[128 * tc:128 * (tc + 1), 512 * dc:512 * (dc + 1)], in_=fin[ob][:, :]), inc=(s_st[ob], 16), dma=True)
            st_evac[(dc, tc)] = (s_st[ob], s_st[ob].n)
        v_evac[dc] = s_v.n

    if debug:
        for s_ in (s_y[0], s_y[1], s_z[0], s_z[1], s_xo[0], s_xo[1], s_spg):
            P.wait("gpsimd", s_, s_.n)
        P.emit("gpsimd", lambda g: g.dma_start(out=dbg["dbg_z"], in_=z_scr), inc=(s_spg, 16), dma=True)
        P.emit("gpsimd", lambda g: g.dma_start(out=dbg["dbg_x"], in_=x_scr), inc=(s_spg, 16), dma=True)
        P.emit("gpsimd", lambda g: g.dma_start(out=dbg["dbg_y"], in_=y_scr), inc=(s_spg, 16), dma=True)
        P.emit("gpsimd", lambda g: g.dma_start(out=dbg["dbg_yn"], in_=yn_scr), inc=(s_spg, 16), dma=True)
        P.wait("gpsimd", s_spg, s_spg.n)

    P.wait("sync", s_st[0], s_st[0].n)
    P.wait("sync", s_st[1], s_st[1].n)
    P.barrier(s_bar)

    with nc.Block() as block:
        @block.sync
        def _(eng):
            for fn in P.q["sync"]:
                fn(eng)

        @block.scalar
        def _(eng):
            for fn in P.q["scalar"]:
                fn(eng)

        @block.gpsimd
        def _(eng):
            for fn in P.q["gpsimd"]:
                fn(eng)

        @block.vector
        def _(eng):
            for fn in P.q["vector"]:
                fn(eng)

        @block.tensor
        def _(eng):
            for fn in P.q["tensor"]:
                fn(eng)

    cx4.close()
    ctx.close()
    return nc


_NC_CACHE = {}


def kernel(**inputs):
    x = np.ascontiguousarray(inputs["L0_attn_hidden_states"], dtype=np.float32)
    w_in_a = np.ascontiguousarray(inputs["L0_attn_W_in"], dtype=np.float32)
    conv_w_a = np.ascontiguousarray(inputs["L0_attn_conv_w"], dtype=np.float32)
    dt_bias_a = np.ascontiguousarray(inputs["L0_attn_dt_bias"], dtype=np.float32).reshape(NH, 1)
    a_log_a = np.ascontiguousarray(inputs["L0_attn_A_log"], dtype=np.float32).reshape(NH, 1)
    d_par_a = np.ascontiguousarray(inputs["L0_attn_D_param"], dtype=np.float32).reshape(1, NH)
    nw_a = np.ascontiguousarray(inputs["L0_attn_norm_weight"], dtype=np.float32).reshape(NH, 128)
    w_out_a = np.ascontiguousarray(inputs["L0_attn_W_out"], dtype=np.float32)
    w_res_a = np.ascontiguousarray(inputs["L0_ffn_W_res"], dtype=np.float32)
    w_pre_a = np.ascontiguousarray(inputs["L0_ffn_W_pre"], dtype=np.float32)
    w_post_a = np.ascontiguousarray(inputs["L0_ffn_W_post"], dtype=np.float32)
    h_bases_a = np.concatenate([
        np.asarray(inputs["L0_ffn_H_res_base"], np.float32).reshape(-1),
        np.asarray(inputs["L0_ffn_H_pre_base"], np.float32).reshape(-1),
        np.asarray(inputs["L0_ffn_H_post_base"], np.float32).reshape(-1)]).reshape(1, 24)
    gate_a = np.ascontiguousarray(inputs["L0_ffn_gate"], dtype=np.float32).reshape(1, 1)
    w_br_a = np.ascontiguousarray(inputs["L0_ffn_W_branch"], dtype=np.float32)

    if "nc" not in _NC_CACHE:
        _NC_CACHE["nc"] = build_program(debug=False)
    nc = _NC_CACHE["nc"]

    in_maps = []
    for cid in range(8):
        b, s = cid // 4, cid % 4
        t0 = s * SPAN
        if s == 0:
            xh_a = np.concatenate([np.zeros((HALO, HD), np.float32), x[b, t0:t0 + SPAN]], 0)
        else:
            xh_a = x[b, t0 - HALO:t0 + SPAN]
        between = np.zeros((8, 8), np.float32)
        pmask = np.zeros((8, 1), np.float32)
        for j in range(8):
            jb, js = j // 4, j % 4
            if jb == b and js < s:
                pmask[j, 0] = 1.0
                for m in range(8):
                    mb, ms = m // 4, m % 4
                    if mb == b and js < ms < s:
                        between[m, j] = 1.0
        in_maps.append({
            "xh": np.ascontiguousarray(xh_a), "w_in": w_in_a, "conv_w": conv_w_a,
            "dt_bias": dt_bias_a, "a_log": a_log_a, "d_par": d_par_a, "nw_in": nw_a,
            "w_out": w_out_a, "w_res": w_res_a, "w_pre": w_pre_a, "w_post": w_post_a,
            "h_bases": h_bases_a, "gate": gate_a, "w_br": w_br_a,
            "betweenT": between, "prevmask": pmask,
        })
    res = run_bass_kernel_spmd(nc, in_maps, core_ids=list(range(8)))
    outp = np.zeros((2, 4096, HD), np.float32)
    for cid in range(8):
        b, s = cid // 4, cid % 4
        outp[b, s * SPAN:(s + 1) * SPAN] = res.results[cid]["out"]
    return outp


# version 03
# speedup vs baseline: 1.1067x; 1.1067x over previous
import contextlib
import numpy as np
import concourse.bass as bass
import concourse.mybir as mybir
from concourse.bass_utils import run_bass_kernel_spmd

F32 = mybir.dt.float32
BF16 = mybir.dt.bfloat16
AF = mybir.ActivationFunctionType
ALU = mybir.AluOpType
AX = mybir.AxisListType.X

HD = 4096
DSSM = 8192
NH = 64
KCONV = 4
CONV_DIM = DSSM + 2 * 128
PROJ = DSSM + CONV_DIM + NH
SPAN = 1024
HALO = 3
TT = SPAN + HALO
Q = 128
NCHUNK = 8
NTILE = 8
NDC = 32
EPS_LN = 1e-5
EPS_RMS = 1.1920929e-07
EPS_MAM = 1e-5
NEG_BIG = -30000.0

ENGS = ("sync", "scalar", "gpsimd", "vector", "tensor")


class Sem:
    def __init__(self, nc, ctx, name):
        self.h = ctx.enter_context(nc.semaphore(name))
        self.n = 0


class Plan:

    CHAINED = ("vector", "scalar", "gpsimd")

    def __init__(self, nc):
        self.nc = nc
        self.q = {e: [] for e in ENGS}
        self.nbar = 0
        self.chain = {}
        self.te_sem = None
        self.dma_sems = []

    def set_chain(self, eng, sem):
        self.chain[eng] = sem

    def _chain_wait(self, eng):
        cs = self.chain.get(eng)
        if cs is not None and cs.n > 0:
            v = cs.n
            self.q[eng].append(lambda e, cs=cs, v=v: e.wait_ge(cs.h, v))

    def emit(self, eng, fn, inc=None, dma=False):
        cs = self.chain.get(eng)
        if cs is not None:
            self._chain_wait(eng)
            if dma or (inc is not None and inc[0] is not cs):
                sem, k = inc
                sem.n += k
                self.q[eng].append(lambda e, fn=fn, sem=sem, k=k: fn(e).then_inc(sem.h, k))
            else:
                cs.n += 1
                self.q[eng].append(lambda e, fn=fn, cs=cs: fn(e).then_inc(cs.h, 1))
            return
        if inc is not None:
            sem, k = inc
            sem.n += k
            self.q[eng].append(lambda e, fn=fn, sem=sem, k=k: fn(e).then_inc(sem.h, k))
        else:
            self.q[eng].append(fn)

    def wait(self, eng, sem, val=None):
        v = sem.n if val is None else val
        if v > 0:
            self.q[eng].append(lambda e, sem=sem, v=v: e.wait_ge(sem.h, v))

    def barrier(self, s_bar):
        self.nbar += 1
        tgt = 5 * self.nbar
        for s_ in self.dma_sems:
            if s_.n > 0:
                self.wait("sync", s_, s_.n)
        for e in ENGS:
            self._chain_wait(e)
            if e == "tensor" and self.te_sem is not None and self.te_sem.n > 0:
                self.wait(e, self.te_sem, self.te_sem.n)
            self.q[e].append(lambda eng, s=s_bar: eng.sem_inc(s.h, 1))
        s_bar.n = tgt
        for e in ENGS:
            self.wait(e, s_bar, tgt)


def build_program(debug=False):
    nc = bass.Bass("TRN2", target_bir_lowering=False)
    ctx = contextlib.ExitStack()

    xh = nc.dram_tensor("xh", [TT, HD], F32, kind="ExternalInput").ap()
    w_in = nc.dram_tensor("w_in", [HD, PROJ], F32, kind="ExternalInput").ap()
    conv_w = nc.dram_tensor("conv_w", [CONV_DIM, KCONV], F32, kind="ExternalInput").ap()
    dt_bias = nc.dram_tensor("dt_bias", [NH, 1], F32, kind="ExternalInput").ap()
    a_log = nc.dram_tensor("a_log", [NH, 1], F32, kind="ExternalInput").ap()
    d_par = nc.dram_tensor("d_par", [1, NH], F32, kind="ExternalInput").ap()
    nw_in = nc.dram_tensor("nw_in", [NH, 128], F32, kind="ExternalInput").ap()
    w_out = nc.dram_tensor("w_out", [DSSM, HD], F32, kind="ExternalInput").ap()
    w_res = nc.dram_tensor("w_res", [16, HD], F32, kind="ExternalInput").ap()
    w_pre = nc.dram_tensor("w_pre", [4, HD], F32, kind="ExternalInput").ap()
    w_post = nc.dram_tensor("w_post", [4, HD], F32, kind="ExternalInput").ap()
    h_bases = nc.dram_tensor("h_bases", [1, 24], F32, kind="ExternalInput").ap()
    gate = nc.dram_tensor("gate", [1, 1], F32, kind="ExternalInput").ap()
    w_br = nc.dram_tensor("w_br", [1024, 1024], F32, kind="ExternalInput").ap()
    betweenT = nc.dram_tensor("betweenT", [8, 8], F32, kind="ExternalInput").ap()
    prevmask = nc.dram_tensor("prevmask", [8, 1], F32, kind="ExternalInput").ap()
    out = nc.dram_tensor("out", [SPAN, HD], F32, kind="ExternalOutput").ap()

    dbg = {}
    if debug:
        for nm, shp, dt in (("dbg_z", [DSSM, SPAN], BF16), ("dbg_x", [DSSM, SPAN], BF16),
                            ("dbg_dt", [NH, TT], F32), ("dbg_y", [DSSM, SPAN], BF16),
                            ("dbg_yn", [DSSM, SPAN], BF16), ("dbg_h", [128, NH * 128], F32),
                            ("dbg_hall", [128, NTILE * 24], F32),
                            ("dbg_xnfm", [128, NDC * TT], BF16),
                            ("dbg_bc", [128, 2 * SPAN], BF16)):
            dbg[nm] = nc.dram_tensor(nm, shp, dt, kind="ExternalOutput").ap()

    xn_scr = nc.dram_tensor("xn_scr", [SPAN, HD], BF16).ap()
    z_scr = nc.dram_tensor("z_scr", [DSSM, SPAN], BF16).ap()
    x_scr = nc.dram_tensor("x_scr", [DSSM, SPAN], BF16).ap()
    y_scr = nc.dram_tensor("y_scr", [DSSM, SPAN], BF16).ap()
    yn_scr = nc.dram_tensor("yn_scr", [DSSM, SPAN], BF16).ap()
    cd_scr = nc.dram_tensor("cd_scr", [DSSM, SPAN], BF16).ap()
    cs_scr = [nc.dram_tensor(f"cs_scr{i}", [NH, Q], F32).ap() for i in range(2)]
    ce_scr = [nc.dram_tensor(f"ce_scr{i}", [NH, 1], F32).ap() for i in range(2)]
    rstd_scr = nc.dram_tensor("rstd_scr", [NCHUNK, Q], F32).ap()
    small_scr = nc.dram_tensor("small_scr", [1, 64], F32).ap()
    wj_scr = nc.dram_tensor("wj_scr", [1, 8 * NH], F32).ap()
    cc_s_in = nc.dram_tensor("cc_s_in", [DSSM, 128], BF16)
    cc_s_out = nc.dram_tensor("cc_s_out", [8 * DSSM, 128], BF16)
    cc_m_in = nc.dram_tensor("cc_m_in", [128, NH], F32)
    cc_m_out = nc.dram_tensor("cc_m_out", [8 * 128, NH], F32)

    P = Plan(nc)
    s_st = [Sem(nc, ctx, f"s_st{i}") for i in range(2)]
    s_p0 = Sem(nc, ctx, "s_p0")
    s_x = [Sem(nc, ctx, f"s_x{i}") for i in range(2)]
    s_spxn = [Sem(nc, ctx, f"s_spxn{i}") for i in range(2)]
    s_ffn = Sem(nc, ctx, "s_ffn")
    s_wffn = Sem(nc, ctx, "s_wffn")
    s_gb = Sem(nc, ctx, "s_gb")
    s_sm = Sem(nc, ctx, "s_sm")
    s_w = [Sem(nc, ctx, f"s_w{i}") for i in range(2)]
    s_z = [Sem(nc, ctx, f"s_z{i}") for i in range(2)]
    s_xo = [Sem(nc, ctx, f"s_xo{i}") for i in range(2)]
    s_mx = [Sem(nc, ctx, f"s_mx{i}") for i in range(2)]
    s_sl = [Sem(nc, ctx, f"s_sl{i}") for i in range(2)]
    s_spcs = [Sem(nc, ctx, f"s_spcs{i}") for i in range(2)]
    s_ab = [Sem(nc, ctx, f"s_ab{i}") for i in range(2)]
    s_rt = [Sem(nc, ctx, f"s_rt{i}") for i in range(2)]
    s_y = [Sem(nc, ctx, f"s_y{i}") for i in range(2)]
    s_spg = Sem(nc, ctx, "s_spg")
    s_meta = Sem(nc, ctx, "s_meta")
    s_ccw = Sem(nc, ctx, "s_ccw")
    s_os = [Sem(nc, ctx, f"s_os{i}") for i in range(2)]
    P.dma_sems = [s_p0, s_ffn, s_wffn, s_gb, s_sm, s_spg, s_meta, s_ccw] + \
        s_st + s_x + s_spxn + s_w + s_z + s_xo + s_mx + s_sl + s_spcs + s_ab + s_rt + s_y + s_os
    s_v = Sem(nc, ctx, "s_v")
    s_sc = Sem(nc, ctx, "s_sc")
    s_te = Sem(nc, ctx, "s_te")
    s_gp = Sem(nc, ctx, "s_gp")
    s_cc = Sem(nc, ctx, "s_cc")
    s_bar = Sem(nc, ctx, "s_bar")
    P.set_chain("vector", s_v)
    P.set_chain("scalar", s_sc)
    P.set_chain("gpsimd", s_gp)
    P.te_sem = s_te

    def sb(cx, name, shape, dt):
        return cx.enter_context(nc.sbuf_tensor(name, shape, dt))


    pf = [ctx.enter_context(nc.psum_tensor(f"pf{i}", [128, 512], F32)) for i in range(6)]
    pbf = [ctx.enter_context(nc.psum_tensor(f"pbf{i}", [128, 1024], BF16)) for i in range(2)]

    ident_bf = sb(ctx, "ident_bf", [128, 128], BF16)
    ident_f = sb(ctx, "ident_f", [128, 128], F32)
    mneg4 = sb(ctx, "mneg4", [128, 512], F32)
    onescol_bf = sb(ctx, "onescol_bf", [128, 1], BF16)
    cbias = sb(ctx, "cbias", [128, 4], F32)
    convw = sb(ctx, "convw", [128, 66 * KCONV], F32)
    dtb_col = sb(ctx, "dtb_col", [NH, 1], F32)
    nega_col = sb(ctx, "nega_col", [NH, 1], F32)
    dbc = sb(ctx, "dbc", [128, NH], F32)
    nwcols = sb(ctx, "nwcols", [128, NH], BF16)
    bfm = sb(ctx, "bfm", [128, SPAN], BF16)
    cfm = sb(ctx, "cfm", [128, SPAN], BF16)
    dtraw = sb(ctx, "dtraw", [NH, TT], F32)
    dt_sp = sb(ctx, "dt_sp", [NH, SPAN], F32)
    logdA = sb(ctx, "logdA", [NH, SPAN], F32)
    logtot = sb(ctx, "logtot", [NH, 1], F32)
    h_all = sb(ctx, "h_all", [128, NTILE * 24], F32)
    rstdf_all = sb(ctx, "rstdf_all", [128, NTILE], F32)
    nw_tmp = sb(ctx, "nw_tmp", [NH, 128], F32)

    P.emit("gpsimd", lambda g: g.memset(ident_bf[:], 0.0))
    P.emit("gpsimd", lambda g: g.affine_select(
        out=ident_bf[:], in_=ident_bf[:], compare_op=ALU.not_equal, fill=1.0,
        base=0, pattern=[[-1, 128]], channel_multiplier=1))
    P.emit("gpsimd", lambda g: g.memset(ident_f[:], 0.0))
    P.emit("gpsimd", lambda g: g.affine_select(
        out=ident_f[:], in_=ident_f[:], compare_op=ALU.not_equal, fill=1.0,
        base=0, pattern=[[-1, 128]], channel_multiplier=1))
    P.emit("gpsimd", lambda g: g.memset(mneg4[:], 0.0))
    for j in range(4):
        P.emit("gpsimd", lambda g, j=j: g.affine_select(
            out=mneg4[:, 128 * j:128 * (j + 1)], in_=mneg4[:, 128 * j:128 * (j + 1)],
            compare_op=ALU.is_ge, fill=NEG_BIG, base=0,
            pattern=[[1, 128]], channel_multiplier=-1))
    P.emit("gpsimd", lambda g: g.memset(cbias[:, 0:1], EPS_LN))
    P.emit("gpsimd", lambda g: g.memset(cbias[:, 1:2], EPS_RMS))
    P.emit("gpsimd", lambda g: g.memset(cbias[:, 2:3], 1.0))
    P.emit("gpsimd", lambda g: g.memset(cbias[:, 3:4], EPS_MAM))
    P.emit("gpsimd", lambda g: g.memset(onescol_bf[:], 1.0), inc=(s_gp, 1))
    gp_p0 = s_gp.n

    P.emit("sync", lambda e: e.dma_start(
        out=convw.ap().rearrange("p (cc k) -> p cc k", k=KCONV),
        in_=conv_w.rearrange("(cc p) k -> p cc k", p=128)), inc=(s_p0, 16), dma=True)
    P.emit("sync", lambda e: e.dma_start(out=dtb_col[:], in_=dt_bias), inc=(s_p0, 16), dma=True)
    P.emit("sync", lambda e: e.dma_start(out=nega_col[:], in_=a_log), inc=(s_p0, 16), dma=True)
    P.emit("sync", lambda e: e.dma_start(out=dbc[:], in_=d_par.broadcast_to((128, NH))), inc=(s_p0, 16), dma=True)
    P.emit("sync", lambda e: e.dma_start(out=nw_tmp[:], in_=nw_in), inc=(s_p0, 16), dma=True)
    P.wait("scalar", s_p0, s_p0.n)
    P.emit("scalar", lambda e: e.activation(nega_col[:], nega_col[:], AF.Exp), inc=(s_sc, 1))
    P.wait("vector", s_sc, s_sc.n)
    P.emit("vector", lambda v: v.tensor_scalar_mul(nega_col[:], nega_col[:], -1.0), inc=(s_v, 1))

    P.wait("tensor", s_p0, s_p0.n)
    P.wait("tensor", s_gp, gp_p0)
    P.emit("tensor", lambda t: t.transpose(pf[0].ap()[0:128, 0:64], nw_tmp[:], ident_f[0:64, 0:64]), inc=(s_te, 1))
    P.wait("vector", s_te, s_te.n)
    P.emit("vector", lambda v: v.tensor_copy(nwcols[:], pf[0][0:128, 0:64]), inc=(s_v, 1))
    v_nw = s_v.n

    cxP12 = contextlib.ExitStack()
    xnfm = sb(cxP12, "xnfm", [128, NDC * TT], BF16)
    wbr_sb = sb(cxP12, "wbr_sb", [128, 8 * 1024], BF16)
    wcatT = sb(cxP12, "wcatT", [128, NDC * 24], BF16)
    hb_bc = sb(cxP12, "hb_bc", [128, 24], F32)
    gcol = sb(cxP12, "gcol", [128, 1], F32)
    gicol = sb(cxP12, "gicol", [128, 16], F32)
    logit_raw = sb(cxP12, "logit_raw", [128, NTILE * 24], F32)
    wcat_bf = sb(cxP12, "wcat_bf", [24, HD], BF16)
    cx1 = contextlib.ExitStack()
    xt_f = [sb(cx1, f"xt_f{i}", [128, HD], F32) for i in range(2)]
    xnf_t = sb(cx1, "xnf_t", [128, HD], F32)
    sqt = sb(cx1, "sqt", [128, HD], F32)
    xn_bf = [sb(cx1, f"xn_bf{i}", [128, HD], BF16) for i in range(2)]
    stat = sb(cx1, "stat", [128, 8], F32)

    te_tr_done = []
    sp_xn = []
    v_x_used = []
    sc_ffn = []
    for i in range(NTILE + 1):
        buf = i % 2
        is_halo = (i == NTILE)
        rows = HALO if is_halo else 128
        row0 = 0 if is_halo else HALO + 128 * i
        tcol = 0 if is_halo else HALO + 128 * i
        if i >= 2:
            P.wait("sync", s_v, v_x_used[i - 2])
        P.emit("sync", lambda e, buf=buf, rows=rows, row0=row0: e.dma_start(
            out=xt_f[buf][0:rows, :], in_=xh[row0:row0 + rows, :]), inc=(s_x[buf], 16), dma=True)
        P.wait("vector", s_x[buf], s_x[buf].n)
        if i >= 1:
            P.wait("vector", s_sc, sc_ffn[i - 1])
        P.emit("vector", lambda v, buf=buf, rows=rows: v.reduce_sum(
            stat[0:rows, 0:1], xt_f[buf][0:rows, :], axis=AX))
        P.emit("vector", lambda v, rows=rows: v.tensor_scalar_mul(
            stat[0:rows, 1:2], stat[0:rows, 0:1], -1.0 / HD))
        P.emit("vector", lambda v, buf=buf, rows=rows: v.scalar_tensor_tensor(
            xnf_t[0:rows, :], xt_f[buf][0:rows, :], stat[0:rows, 1:2],
            xt_f[buf][0:rows, :], ALU.add, ALU.bypass), inc=(s_v, 1))
        v_x_used.append(s_v.n)
        P.wait("scalar", s_v, s_v.n)
        P.wait("scalar", s_gp, gp_p0)
        P.emit("scalar", lambda e, rows=rows: e.activation(
            sqt[0:rows, :], xnf_t[0:rows, :], AF.Square, accum_out=stat[0:rows, 2:3]))
        P.emit("scalar", lambda e, rows=rows: e.activation(
            stat[0:rows, 3:4], stat[0:rows, 2:3], AF.Ln, bias=cbias[0:rows, 0:1], scale=1.0 / HD))
        P.emit("scalar", lambda e, rows=rows: e.activation(
            stat[0:rows, 4:5], stat[0:rows, 3:4], AF.Exp, scale=-0.5), inc=(s_sc, 1))
        P.wait("vector", s_sc, s_sc.n)
        P.emit("vector", lambda v, rows=rows: v.scalar_tensor_tensor(
            xnf_t[0:rows, :], xnf_t[0:rows, :], stat[0:rows, 4:5],
            xnf_t[0:rows, :], ALU.mult, ALU.bypass), inc=(s_v, 1))
        v_xn32 = s_v.n
        if i >= 2:
            P.wait("vector", s_te, te_tr_done[i - 2])
            P.wait("vector", *sp_xn[i - 2])
        P.emit("vector", lambda v, buf=buf, rows=rows: v.tensor_copy(
            xn_bf[buf][0:rows, :], xnf_t[0:rows, :]), inc=(s_v, 1))
        v_xnbf = s_v.n
        if not is_halo:
            P.wait("scalar", s_v, v_xn32)
            P.emit("scalar", lambda e: e.activation(
                sqt[:, :], xnf_t[:, :], AF.Square, accum_out=stat[0:128, 5:6]))
            P.emit("scalar", lambda e: e.activation(
                stat[0:128, 6:7], stat[0:128, 5:6], AF.Ln, bias=cbias[:, 1:2], scale=1.0 / HD))
            P.emit("scalar", lambda e, i=i: e.activation(
                rstdf_all[:, i:i + 1], stat[0:128, 6:7], AF.Exp, scale=-0.5), inc=(s_sc, 1))
            P.wait("scalar", s_v, v_xnbf)
            P.emit("scalar", lambda e, buf=buf, i=i: e.dma_start(
                out=xn_scr[128 * i:128 * (i + 1), :], in_=xn_bf[buf][:, :]), inc=(s_spxn[buf], 16), dma=True)
        sc_ffn.append(s_sc.n)
        sp_xn.append((s_spxn[buf], s_spxn[buf].n))
        P.wait("tensor", s_v, v_xnbf)
        tr_v = {}
        for dc in range(NDC):
            pd = pbf[dc % 2]
            if dc >= 2:
                P.wait("tensor", s_v, tr_v[dc - 2])
            P.emit("tensor", lambda t, buf=buf, dc=dc, rows=rows, pd=pd: t.transpose(
                pd[0:128, 0:rows], xn_bf[buf][0:rows, 128 * dc:128 * (dc + 1)],
                ident_bf[0:rows, 0:rows]), inc=(s_te, 1))
            P.wait("vector", s_te, s_te.n)
            P.emit("vector", lambda v, dc=dc, tcol=tcol, rows=rows, pd=pd: v.tensor_copy(
                xnfm[:, TT * dc + tcol:TT * dc + tcol + rows], pd[0:128, 0:rows]), inc=(s_v, 1))
            tr_v[dc] = s_v.n
        te_tr_done.append(s_te.n)
    v_xnfm_all = s_v.n
    if debug:
        P.wait("sync", s_v, v_xnfm_all)
        P.emit("sync", lambda e: e.dma_start(out=dbg["dbg_xnfm"], in_=xnfm[:, :]), inc=(s_spg, 16), dma=True)
        P.wait("sync", s_spg, s_spg.n)
    P.barrier(s_bar)
    cx1.close()

    cx2 = contextlib.ExitStack()
    g_tmp = sb(cx2, "g_tmp", [1, 2], F32)
    gi_row = sb(cx2, "gi_row", [1, 17], F32)

    P.emit("sync", lambda e: e.dma_start(out=hb_bc[:], in_=h_bases.broadcast_to((128, 24))), inc=(s_ffn, 16), dma=True)
    P.emit("sync", lambda e: e.dma_start(out=g_tmp[0:1, 0:1], in_=gate), inc=(s_ffn, 16), dma=True)
    P.emit("gpsimd", lambda g: g.dma_start(out=wcat_bf[0:16, :], in_=w_res), inc=(s_wffn, 16), dma=True)
    P.emit("gpsimd", lambda g: g.dma_start(out=wcat_bf[16:20, :], in_=w_pre), inc=(s_wffn, 16), dma=True)
    P.emit("gpsimd", lambda g: g.dma_start(out=wcat_bf[20:24, :], in_=w_post), inc=(s_wffn, 16), dma=True)
    P.emit("gpsimd", lambda g: g.dma_start(
        out=wbr_sb.ap().rearrange("p (kc c) -> p kc c", kc=8),
        in_=w_br.rearrange("(kc p) c -> p kc c", p=128)), inc=(s_wffn, 16), dma=True)
    P.wait("tensor", s_wffn, s_wffn.n)
    tr_v = {}
    for dc in range(NDC):
        pd = pbf[dc % 2]
        if dc >= 2:
            P.wait("tensor", s_v, tr_v[dc - 2])
        P.emit("tensor", lambda t, dc=dc, pd=pd: t.transpose(
            pd[0:128, 0:24], wcat_bf[:, 128 * dc:128 * (dc + 1)], ident_bf[0:24, 0:24]), inc=(s_te, 1))
        P.wait("vector", s_te, s_te.n)
        P.emit("vector", lambda v, dc=dc, pd=pd: v.tensor_copy(
            wcatT[:, 24 * dc:24 * (dc + 1)], pd[0:128, 0:24]), inc=(s_v, 1))
        tr_v[dc] = s_v.n
    v_wcatT = s_v.n

    P.wait("scalar", s_ffn, s_ffn.n)
    P.emit("scalar", lambda e: e.activation(g_tmp[0:1, 0:1], g_tmp[0:1, 0:1], AF.Sigmoid), inc=(s_sc, 1))
    P.wait("vector", s_sc, s_sc.n)
    P.emit("vector", lambda v: v.memset(gi_row[:], 0.0))
    P.emit("vector", lambda v: v.tensor_scalar(
        g_tmp[0:1, 1:2], g_tmp[0:1, 0:1], -1.0, 1.0, ALU.mult, ALU.add))
    for k in range(4):
        P.emit("vector", lambda v, k=k: v.tensor_copy(gi_row[0:1, 5 * k:5 * k + 1], g_tmp[0:1, 1:2]))
    P.emit("vector", lambda v: v.tensor_copy(gi_row[0:1, 16:17], g_tmp[0:1, 0:1]), inc=(s_v, 1))
    P.wait("scalar", s_v, s_v.n)
    P.emit("scalar", lambda e: e.dma_start(out=small_scr[0:1, 0:17], in_=gi_row[:]), inc=(s_sm, 16), dma=True)
    P.wait("sync", s_sm, s_sm.n)
    P.emit("sync", lambda e: e.dma_start(out=gicol[:], in_=small_scr[0:1, 0:16].broadcast_to((128, 16))), inc=(s_gb, 16), dma=True)
    P.emit("sync", lambda e: e.dma_start(out=gcol[:], in_=small_scr[0:1, 16:17].broadcast_to((128, 1))), inc=(s_gb, 16), dma=True)

    P.wait("tensor", s_v, v_wcatT)
    for i in range(NTILE):
        if i >= 1:
            P.wait("tensor", s_v, s_v.n)
        for dc in range(NDC):
            P.emit("tensor", lambda t, i=i, dc=dc: t.matmul(
                pf[4].ap()[0:128, 0:24],
                xnfm[:, TT * dc + HALO + 128 * i: TT * dc + HALO + 128 * (i + 1)],
                wcatT[:, 24 * dc:24 * (dc + 1)],
                start=(dc == 0), stop=(dc == NDC - 1)),
                inc=(s_te, 1) if dc == NDC - 1 else None)
        P.wait("vector", s_te, s_te.n)
        P.wait("vector", s_gb, s_gb.n)
        P.emit("vector", lambda v, i=i: v.scalar_tensor_tensor(
            logit_raw[:, 24 * i:24 * (i + 1)], pf[4][0:128, 0:24], rstdf_all[:, i:i + 1],
            hb_bc[:, :], ALU.mult, ALU.add))
        P.emit("vector", lambda v, i=i: v.tensor_scalar_mul(
            logit_raw[:, 24 * i:24 * i + 16], logit_raw[:, 24 * i:24 * i + 16], 10.0), inc=(s_v, 1))
    v_logits = s_v.n

    cx3 = contextlib.ExitStack()
    zv_b = sb(cx3, "zv_b", [128, 128], F32)
    u_b = sb(cx3, "u_b", [128, 32], F32)
    v_b = sb(cx3, "v_b", [128, 32], F32)
    m_b = sb(cx3, "m_b", [128, 32], F32)
    sst_b = sb(cx3, "sst_b", [128, 32], F32)

    Zr = logit_raw.ap().rearrange("p (t k) -> p t k", k=24)[:, :, 0:16].rearrange(
        "p t (i j) -> p t i j", j=4)
    zv4 = zv_b.ap().rearrange("p (t i j) -> p t i j", i=4, j=4)
    zv4T = zv_b.ap().rearrange("p (t i j) -> p t j i", i=4, j=4)
    uv = u_b.ap().rearrange("p (t i) -> p t i", i=4)
    vv = v_b.ap().rearrange("p (t j) -> p t j", j=4)
    mv = m_b.ap().rearrange("p (t k) -> p t k", k=4)
    sv = sst_b.ap().rearrange("p (t k) -> p t k", k=4)

    P.wait("vector", s_v, v_logits)
    P.emit("vector", lambda v: v.memset(u_b[:], 0.0))
    P.emit("vector", lambda v: v.memset(v_b[:], 0.0))
    for it in range(50):
        for i in range(4):
            P.emit("vector", lambda v, i=i: v.tensor_add(zv4[:, :, i, :], Zr[:, :, i, :], vv[:, :, :]))
        P.emit("vector", lambda v: v.reduce_max(mv[:, :, :], zv4[:, :, :, :], axis=AX))
        for j in range(4):
            P.emit("vector", lambda v, j=j: v.tensor_sub(zv4[:, :, :, j], zv4[:, :, :, j], mv[:, :, :]),
                   inc=(s_v, 1) if j == 3 else None)
        P.wait("scalar", s_v, s_v.n)
        P.emit("scalar", lambda e: e.activation(zv_b[:], zv_b[:], AF.Exp), inc=(s_sc, 1))
        P.wait("vector", s_sc, s_sc.n)
        P.emit("vector", lambda v: v.reduce_sum(sv[:, :, :], zv4[:, :, :, :], axis=AX), inc=(s_v, 1))
        P.wait("scalar", s_v, s_v.n)
        P.emit("scalar", lambda e: e.activation(sst_b[:], sst_b[:], AF.Ln), inc=(s_sc, 1))
        P.wait("vector", s_sc, s_sc.n)
        P.emit("vector", lambda v: v.scalar_tensor_tensor(
            u_b[:], sst_b[:], -1.0, m_b[:], ALU.mult, ALU.subtract))
        for j in range(4):
            P.emit("vector", lambda v, j=j: v.tensor_add(zv4[:, :, :, j], Zr[:, :, :, j], uv[:, :, :]))
        P.emit("vector", lambda v: v.reduce_max(mv[:, :, :], zv4T[:, :, :, :], axis=AX))
        for i in range(4):
            P.emit("vector", lambda v, i=i: v.tensor_sub(zv4[:, :, i, :], zv4[:, :, i, :], mv[:, :, :]),
                   inc=(s_v, 1) if i == 3 else None)
        P.wait("scalar", s_v, s_v.n)
        P.emit("scalar", lambda e: e.activation(zv_b[:], zv_b[:], AF.Exp), inc=(s_sc, 1))
        P.wait("vector", s_sc, s_sc.n)
        P.emit("vector", lambda v: v.reduce_sum(sv[:, :, :], zv4T[:, :, :, :], axis=AX), inc=(s_v, 1))
        P.wait("scalar", s_v, s_v.n)
        P.emit("scalar", lambda e: e.activation(sst_b[:], sst_b[:], AF.Ln), inc=(s_sc, 1))
        P.wait("vector", s_sc, s_sc.n)
        P.emit("vector", lambda v: v.scalar_tensor_tensor(
            v_b[:], sst_b[:], -1.0, m_b[:], ALU.mult, ALU.subtract))
    for i in range(4):
        P.emit("vector", lambda v, i=i: v.tensor_add(zv4[:, :, i, :], Zr[:, :, i, :], vv[:, :, :]))
    for i in range(4):
        for j in range(4):
            P.emit("vector", lambda v, i=i, j=j: v.tensor_add(
                zv4[:, :, i, j:j + 1], zv4[:, :, i, j:j + 1], uv[:, :, i:i + 1]))
    P.emit("vector", lambda v: v.tensor_copy(m_b[:], m_b[:]), inc=(s_v, 1))
    P.wait("scalar", s_v, s_v.n)
    P.emit("scalar", lambda e: e.activation(zv_b[:], zv_b[:], AF.Exp), inc=(s_sc, 1))
    P.wait("vector", s_sc, s_sc.n)
    P.wait("vector", s_gb, s_gb.n)
    for i in range(NTILE):
        P.emit("vector", lambda v, i=i: v.scalar_tensor_tensor(
            h_all[:, 24 * i:24 * i + 16], zv_b[:, 16 * i:16 * (i + 1)],
            gcol[:, 0:1], gicol[:, :], ALU.mult, ALU.add))
    mm8 = m_b.ap()[0:128, 0:8].rearrange("p (t o) -> p t o", o=1)
    ss8 = sst_b.ap()[0:128, 0:8].rearrange("p (t o) -> p t o", o=1)
    for base in (16, 20):
        pl = logit_raw.ap().rearrange("p (t k) -> p t k", k=24)[:, :, base:base + 4]
        hv = h_all.ap().rearrange("p (t k) -> p t k", k=24)[:, :, base:base + 4]
        tmp = zv_b.ap()[0:128, 0:32].rearrange("p (t j) -> p t j", j=4)
        P.emit("vector", lambda v, pl=pl: v.reduce_max(mm8[:, :, :], pl[:, :, :], axis=AX))
        for j in range(4):
            P.emit("vector", lambda v, pl=pl, tmp=tmp, j=j: v.tensor_sub(
                tmp[:, :, j:j + 1], pl[:, :, j:j + 1], mm8[:, :, :]), inc=(s_v, 1) if j == 3 else None)
        P.wait("scalar", s_v, s_v.n)
        P.emit("scalar", lambda e: e.activation(zv_b[0:128, 0:32], zv_b[0:128, 0:32], AF.Exp), inc=(s_sc, 1))
        P.wait("vector", s_sc, s_sc.n)
        P.emit("vector", lambda v, tmp=tmp: v.reduce_sum(ss8[:, :, :], tmp[:, :, :], axis=AX))
        P.emit("vector", lambda v: v.reciprocal(sst_b[0:128, 0:8], sst_b[0:128, 0:8]))
        for j in range(4):
            P.emit("vector", lambda v, hv=hv, tmp=tmp, j=j: v.tensor_mul(
                hv[:, :, j:j + 1], tmp[:, :, j:j + 1], ss8[:, :, :]))
    P.emit("vector", lambda v: v.tensor_copy(m_b[:], m_b[:]), inc=(s_v, 1))
    v_hall = s_v.n
    if debug:
        P.wait("scalar", s_v, v_hall)
        P.emit("scalar", lambda e: e.dma_start(out=dbg["dbg_hall"], in_=h_all[:]), inc=(s_spg, 16), dma=True)

    cxW = contextlib.ExitStack()
    wt_bf = [sb(cxW, f"wt_bf{i}", [128, NDC * 128], BF16) for i in range(2)]
    xbc_bf = [sb(cxW, f"xbc_bf{i}", [128, TT], BF16) for i in range(2)]
    cvb = [sb(cxW, f"cvb{i}", [128, SPAN], BF16) for i in range(2)]
    xout = [sb(cxW, f"xout{i}", [128, SPAN], BF16) for i in range(2)]
    zeb = [sb(cxW, f"zeb{i}", [128, 1024], BF16) for i in range(2)]
    xnm_bf = [sb(cxW, f"xnm_bf{i}", [128, HD], BF16) for i in range(2)]
    xpre_tm = sb(cxW, "xpre_tm", [128, 1024], BF16)
    xpre_fm = sb(cxW, "xpre_fm", [128, 8 * 128], BF16)
    ybr_bf = sb(cxW, "ybr_bf", [128, 1024], BF16)
    ffn_acc = sb(cxW, "ffn_acc", [128, HD], BF16)
    xrl_f = [sb(cxW, f"xrl_f{i}", [128, HD], F32) for i in range(2)]

    p2_ccs = list(range(64, 131)) + list(range(0, 64))
    idx_dt = p2_ccs.index(130)

    w_loaded = {}
    te_chunk = {}
    v_chunk = {}
    sc_chunk = {}
    sp_chunk = {}
    mix_done = {}
    mix_st = {}
    mix_te = {}
    mix_ybr = {}

    def emit_wload(idx):
        cc = p2_ccs[idx]
        wbuf = idx % 2
        width = 64 if cc == 130 else 128
        c0 = 128 * cc
        if idx >= 2:
            P.wait("gpsimd", s_te, te_chunk[idx - 2])
        for gdc in range(4):
            P.emit("gpsimd", lambda g, wbuf=wbuf, gdc=gdc, c0=c0, width=width: g.dma_start(
                out=wt_bf[wbuf].ap().rearrange("p (dc c) -> p dc c", c=128)[:, 8 * gdc:8 * (gdc + 1), 0:width],
                in_=w_in.rearrange("(dc p) c -> p dc c", p=128)[:, 8 * gdc:8 * (gdc + 1), c0:c0 + width]),
                inc=(s_w[wbuf], 16), dma=True)
        w_loaded[idx] = (s_w[wbuf], s_w[wbuf].n)

    emit_wload(0)
    emit_wload(1)

    def plan_mix_tile(i):
        mb = i % 2
        if i >= 2:
            P.wait("sync", s_v, mix_done[i - 2])
            P.wait("sync", *mix_st[i - 2])
        P.wait("sync", *sp_xn[i])
        P.emit("sync", lambda e, mb=mb, i=i: e.dma_start(
            out=xnm_bf[mb][:, :], in_=xn_scr[128 * i:128 * (i + 1), :]), inc=(s_mx[mb], 16), dma=True)
        P.emit("sync", lambda e, mb=mb, i=i: e.dma_start(
            out=xrl_f[mb][:, :], in_=xh[HALO + 128 * i:HALO + 128 * (i + 1), :]), inc=(s_mx[mb], 16), dma=True)
        P.wait("vector", s_mx[mb], s_mx[mb].n)
        P.wait("vector", s_v, v_hall)
        if i >= 1:
            P.wait("vector", s_te, mix_te[i - 1])
        for n in range(4):
            P.emit("vector", lambda v, mb=mb, i=i, n=n: v.scalar_tensor_tensor(
                xpre_tm[:, :], xnm_bf[mb][:, 1024 * n:1024 * (n + 1)],
                h_all[:, 24 * i + 16 + n:24 * i + 17 + n],
                xpre_tm[:, :] if n else xnm_bf[mb][:, 0:1024],
                ALU.mult, ALU.add if n else ALU.bypass), inc=(s_v, 1) if n == 3 else None)
        v_xpre = s_v.n
        P.wait("tensor", s_v, v_xpre)
        P.wait("tensor", s_v, v_chunk[idx_dt])
        P.wait("tensor", s_v, mix_ybr.get(i - 1, 0))
        tr_v = {}
        for kc in range(8):
            pd = pbf[kc % 2]
            if kc >= 2:
                P.wait("tensor", s_v, tr_v[kc - 2])
            P.emit("tensor", lambda t, kc=kc, pd=pd: t.transpose(
                pd[0:128, 0:128], xpre_tm[:, 128 * kc:128 * (kc + 1)], ident_bf[:, :]), inc=(s_te, 1))
            P.wait("vector", s_te, s_te.n)
            P.emit("vector", lambda v, kc=kc, pd=pd: v.tensor_copy(
                xpre_fm[:, 128 * kc:128 * (kc + 1)], pd[0:128, 0:128]), inc=(s_v, 1))
            tr_v[kc] = s_v.n
        v_xprefm = s_v.n
        P.wait("tensor", s_v, v_xprefm)
        for half in range(2):
            for kc in range(8):
                P.emit("tensor", lambda t, half=half, kc=kc: t.matmul(
                    pf[4 + half].ap()[0:128, 0:512], xpre_fm[:, 128 * kc:128 * (kc + 1)],
                    wbr_sb[:, 1024 * kc + 512 * half:1024 * kc + 512 * (half + 1)],
                    start=(kc == 0), stop=(kc == 7)), inc=(s_te, 1) if kc == 7 else None)
        mix_te[i] = s_te.n
        P.wait("vector", s_te, mix_te[i])
        for half in range(2):
            P.emit("vector", lambda v, half=half: v.tensor_copy(
                ybr_bf[:, 512 * half:512 * (half + 1)], pf[4 + half][0:128, 0:512]), inc=(s_v, 1))
        mix_ybr[i] = s_v.n
        for ib in range(4):
            for jb in range(4):
                P.emit("vector", lambda v, mb=mb, i=i, ib=ib, jb=jb: v.scalar_tensor_tensor(
                    ffn_acc[:, 1024 * ib:1024 * (ib + 1)],
                    xnm_bf[mb][:, 1024 * jb:1024 * (jb + 1)],
                    h_all[:, 24 * i + 4 * ib + jb:24 * i + 4 * ib + jb + 1],
                    ffn_acc[:, 1024 * ib:1024 * (ib + 1)] if jb else xnm_bf[mb][:, 0:1024],
                    ALU.mult, ALU.add if jb else ALU.bypass))
            P.emit("vector", lambda v, i=i, ib=ib: v.scalar_tensor_tensor(
                ffn_acc[:, 1024 * ib:1024 * (ib + 1)], ybr_bf[:, :],
                h_all[:, 24 * i + 20 + ib:24 * i + 21 + ib],
                ffn_acc[:, 1024 * ib:1024 * (ib + 1)], ALU.mult, ALU.add))
        P.emit("vector", lambda v, mb=mb: v.tensor_add(xrl_f[mb][:, :], xrl_f[mb][:, :], ffn_acc[:, :]), inc=(s_v, 1))
        mix_done[i] = s_v.n
        P.wait("sync", s_v, mix_done[i])
        P.emit("sync", lambda e, mb=mb, i=i: e.dma_start(
            out=out[128 * i:128 * (i + 1), :], in_=xrl_f[mb][:, :]), inc=(s_st[mb], 16), dma=True)
        mix_st[i] = (s_st[mb], s_st[mb].n)

    P.wait("tensor", s_v, v_xnfm_all)
    for idx, cc in enumerate(p2_ccs):
        pb = idx % 2
        is_dt = (cc == 130)
        is_z = (cc < 64)
        is_x = (64 <= cc < 128)
        width = 64 if is_dt else 128
        tiles = [(HALO, 512), (HALO + 512, 512)] if is_z else [(0, 512), (512, 512), (1024, HALO)]
        P.wait("tensor", *w_loaded[idx])
        if idx >= 2:
            P.wait("tensor", s_v, v_chunk[idx - 2])
        else:
            P.wait("tensor", s_v, v_logits)
        for tb, (t0, tw) in enumerate(tiles):
            pt = pf[2 * pb + tb] if tb < 2 else pf[4 + pb]
            for dc in range(NDC):
                P.emit("tensor", lambda t, pb=pb, dc=dc, t0=t0, tw=tw, pt=pt, width=width: t.matmul(
                    pt.ap()[0:width, 0:tw], wt_bf[pb][:, 128 * dc:128 * dc + width],
                    xnfm[:, TT * dc + t0:TT * dc + t0 + tw],
                    start=(dc == 0), stop=(dc == NDC - 1)),
                    inc=(s_te, 1) if dc == NDC - 1 else None)
        te_chunk[idx] = s_te.n
        if idx + 2 < len(p2_ccs):
            emit_wload(idx + 2)
        P.wait("vector", s_te, te_chunk[idx])
        if is_dt:
            for tb, (t0, tw) in enumerate(tiles):
                pt = pf[2 * pb + tb] if tb < 2 else pf[4 + pb]
                P.emit("vector", lambda v, t0=t0, tw=tw, pt=pt: v.tensor_copy(
                    dtraw[:, t0:t0 + tw], pt[0:64, 0:tw]), inc=(s_v, 1))
            v_chunk[idx] = s_v.n
            sc_chunk[idx] = s_sc.n
            sp_chunk[idx] = (s_spg, s_spg.n)
        elif is_z:
            if idx >= 2:
                P.wait("vector", *sp_chunk[idx - 2])
            for tb, (t0, tw) in enumerate(tiles):
                pt = pf[2 * pb + tb]
                P.emit("vector", lambda v, pb=pb, tb=tb, pt=pt: v.tensor_copy(
                    zeb[pb][:, 512 * tb:512 * (tb + 1)], pt[0:128, 0:512]), inc=(s_v, 1))
            v_chunk[idx] = s_v.n
            sc_chunk[idx] = s_sc.n
            P.wait("scalar", s_v, v_chunk[idx])
            P.emit("scalar", lambda e, pb=pb, cc=cc: e.dma_start(
                out=z_scr[128 * cc:128 * (cc + 1), :], in_=zeb[pb][:, :]), inc=(s_z[pb], 16), dma=True)
            sp_chunk[idx] = (s_z[pb], s_z[pb].n)
        else:
            if idx >= 2:
                P.wait("vector", s_sc, sc_chunk[idx - 2])
            for tb, (t0, tw) in enumerate(tiles):
                pt = pf[2 * pb + tb] if tb < 2 else pf[4 + pb]
                P.emit("vector", lambda v, pb=pb, t0=t0, tw=tw, pt=pt: v.tensor_copy(
                    xbc_bf[pb][:, t0:t0 + tw], pt[0:128, 0:tw]))
            ci = cc - 64
            for k in range(KCONV):
                P.emit("vector", lambda v, pb=pb, ci=ci, k=k: v.scalar_tensor_tensor(
                    cvb[pb][:, :], xbc_bf[pb][:, k:k + SPAN], convw[:, 4 * ci + k:4 * ci + k + 1],
                    cvb[pb][:, :] if k else xbc_bf[pb][:, 0:SPAN],
                    ALU.mult, ALU.add if k else ALU.bypass), inc=(s_v, 1) if k == 3 else None)
            v_chunk[idx] = s_v.n
            P.wait("scalar", s_v, v_chunk[idx])
            if idx >= 2:
                P.wait("scalar", *sp_chunk[idx - 2])
            P.emit("scalar", lambda e, pb=pb: e.activation(
                xout[pb][:, :], cvb[pb][:, :], AF.Silu), inc=(s_sc, 1))
            sc_chunk[idx] = s_sc.n
            if is_x:
                P.emit("scalar", lambda e, pb=pb, ci=ci: e.dma_start(
                    out=x_scr[128 * ci:128 * (ci + 1), :], in_=xout[pb][:, :]), inc=(s_xo[pb], 16), dma=True)
                sp_chunk[idx] = (s_xo[pb], s_xo[pb].n)
            else:
                dst = bfm if cc == 128 else cfm
                P.wait("vector", s_sc, sc_chunk[idx])
                P.emit("vector", lambda v, pb=pb, dst=dst: v.tensor_copy(dst[:, :], xout[pb][:, :]), inc=(s_v, 1))
                v_chunk[idx] = s_v.n
                sp_chunk[idx] = (s_spg, s_spg.n)
        if is_z and (cc % 8 == 7):
            plan_mix_tile(cc // 8)

    P.wait("vector", s_v, v_chunk[idx_dt])
    P.wait("vector", s_p0, s_p0.n)
    P.emit("vector", lambda v: v.scalar_tensor_tensor(
        dt_sp[:, :], dtraw[:, HALO:], dtb_col[:, 0:1], dtraw[:, HALO:], ALU.add, ALU.bypass), inc=(s_v, 1))
    P.wait("scalar", s_v, s_v.n)
    P.emit("scalar", lambda e: e.activation(logdA[:, :], dt_sp[:, :], AF.Abs))
    P.emit("scalar", lambda e: e.activation(logdA[:, :], logdA[:, :], AF.Exp, scale=-1.0))
    P.emit("scalar", lambda e: e.activation(logdA[:, :], logdA[:, :], AF.Ln, bias=cbias[0:NH, 2:3]), inc=(s_sc, 1))
    P.wait("vector", s_sc, s_sc.n)
    P.emit("vector", lambda v: v.tensor_relu(dt_sp[:, :], dt_sp[:, :]))
    P.emit("vector", lambda v: v.tensor_add(dt_sp[:, :], dt_sp[:, :], logdA[:, :]))
    P.emit("vector", lambda v: v.scalar_tensor_tensor(
        logdA[:, :], dt_sp[:, :], nega_col[:, 0:1], dt_sp[:, :], ALU.mult, ALU.bypass), inc=(s_v, 1))
    v_dtprep = s_v.n
    if debug:
        P.wait("scalar", s_v, v_dtprep)
        P.emit("scalar", lambda e: e.dma_start(out=dbg["dbg_dt"], in_=dtraw[:]), inc=(s_spg, 16), dma=True)

    if debug:
        P.wait("sync", s_v, s_v.n)
        P.emit("sync", lambda e: e.dma_start(out=dbg["dbg_bc"].rearrange("p (a t) -> p a t", a=2)[:, 0:1, :], in_=bfm[:, :].unsqueeze(1)), inc=(s_spg, 16), dma=True)
        P.emit("sync", lambda e: e.dma_start(out=dbg["dbg_bc"].rearrange("p (a t) -> p a t", a=2)[:, 1:2, :], in_=cfm[:, :].unsqueeze(1)), inc=(s_spg, 16), dma=True)
        P.wait("sync", s_spg, s_spg.n)
    P.barrier(s_bar)
    cxW.close()
    cx3.close()
    cx2.close()
    cxP12.close()

    cxH = contextlib.ExitStack()
    ab_t = [sb(cxH, f"ab{c}", [128, NH], F32) for c in range(NCHUNK)]
    h_f32 = sb(cxH, "h_f32", [128, NH * 128], F32)
    h_bf = sb(cxH, "h_bf", [128, NH * 128], BF16)
    cx3a = contextlib.ExitStack()
    Xc = [sb(cx3a, f"Xc{i}", [128, NH * 128], BF16) for i in range(2)]
    XTc = [sb(cx3a, f"XTc{i}", [128, NH * 128], BF16) for i in range(2)]
    rt_b = [sb(cx3a, f"rt_b{i}", [128, 512], F32) for i in range(2)]
    rtm_b = [sb(cx3a, f"rtm_b{i}", [128, 512], F32) for i in range(2)]
    edec_b = [sb(cx3a, f"edec_b{i}", [128, 512], BF16) for i in range(2)]
    l4_b = [sb(cx3a, f"l4_b{i}", [128, 512], BF16) for i in range(2)]
    cdec_b = [sb(cx3a, f"cdec_b{i}", [128, 512], BF16) for i in range(2)]
    m4_b = [sb(cx3a, f"m4_b{i}", [128, 512], BF16) for i in range(2)]
    crep = sb(cx3a, "crep", [128, 512], BF16)
    grep = sb(cx3a, "grep", [128, 512], BF16)
    y4_b = [sb(cx3a, f"y4_b{i}", [128, 512], BF16) for i in range(2)]
    gm_b = sb(cx3a, "gm_b", [128, 128], BF16)
    bt_b = sb(cx3a, "bt_b", [128, 128], BF16)
    dtxT = [sb(cx3a, f"dtxT{i}", [128, 512], BF16) for i in range(2)]
    bdT = [sb(cx3a, f"bdT{i}", [128, 512], BF16) for i in range(2)]
    stk = sb(cx3a, "stk", [128, 128], F32)
    trc = sb(cx3a, "trc", [128, 128], F32)
    dtT_bf = sb(cx3a, "dtT_bf", [128, NH], BF16)
    wT_f = sb(cx3a, "wT_f", [128, NH], F32)
    wT_bf = sb(cx3a, "wT_bf", [128, NH], BF16)
    cs_hm = sb(cx3a, "cs_hm", [NH, Q], F32)
    w_hm = sb(cx3a, "w_hm", [NH, Q], F32)

    P.emit("vector", lambda v: v.memset(h_f32[:], 0.0))
    P.emit("vector", lambda v: v.memset(h_bf[:], 0.0))
    P.emit("vector", lambda v: v.memset(logtot[:], 0.0), inc=(s_v, 1))

    def slab_load(dst, src, c, sem):
        for g8 in range(8):
            P.emit("sync", lambda e, dst=dst, src=src, c=c, g8=g8: e.dma_start(
                out=dst.ap().rearrange("p (hh t) -> p hh t", t=Q)[:, 8 * g8:8 * (g8 + 1), :],
                in_=src.rearrange("(hh p) t -> p hh t", p=128)[:, 8 * g8:8 * (g8 + 1), Q * c:Q * (c + 1)]),
                inc=(sem, 16), dma=True)

    def slab_T_load(dst, src, c, sem):
        for g8 in range(8):
            P.emit("sync", lambda e, dst=dst, src=src, c=c, g8=g8: e.dma_start_transpose(
                out=dst[:, 1024 * g8:1024 * (g8 + 1)],
                in_=src[1024 * g8:1024 * (g8 + 1), Q * c:Q * (c + 1)]).then_inc(sem.h, 16))
            sem.n += 16

    for s_ in (s_xo[0], s_xo[1], s_spxn[0], s_spxn[1], s_spg):
        P.wait("sync", s_, s_.n)
    slab_load(Xc[0], x_scr, 0, s_sl[0])
    slab_T_load(XTc[0], x_scr, 0, s_sl[0])
    slab_ld = {0: (s_sl[0], s_sl[0].n)}
    last_grp_v = {}
    last_grp_sp = {}
    last_grp_te = {}
    last_grp_sc = {}
    last_rt_sc = {}
    sc_ab_hist = {}
    last_cs_ld = {0: [], 1: []}
    prev_sp_cs = (s_spg, 0)

    for c in range(NCHUNK):
        pb = c % 2
        if c + 1 < NCHUNK:
            if c >= 1:
                P.wait("sync", s_v, last_grp_v.get((c - 1, 15), 0))
            slab_load(Xc[(c + 1) % 2], x_scr, c + 1, s_sl[(c + 1) % 2])
            slab_T_load(XTc[(c + 1) % 2], x_scr, c + 1, s_sl[(c + 1) % 2])
            slab_ld[c + 1] = (s_sl[(c + 1) % 2], s_sl[(c + 1) % 2].n)
        P.wait("vector", s_v, v_dtprep)
        if c >= 1:
            P.wait("vector", s_te, last_grp_te.get((c - 1, 15), 0))
            P.wait("vector", s_sc, last_grp_sc.get((c - 1, 15), 0))
            P.wait("vector", *prev_sp_cs)
        P.emit("vector", lambda v, c=c: v.tensor_tensor_scan(
            cs_hm[:, :], logdA[:, Q * c:Q * (c + 1)], logdA[:, Q * c:Q * (c + 1)],
            0.0, ALU.add, ALU.bypass))
        P.emit("vector", lambda v: v.tensor_add(logtot[:, :], logtot[:, :], cs_hm[:, Q - 1:Q]))
        P.emit("vector", lambda v, c=c: v.tensor_copy(stk[0:64, :], dt_sp[:, Q * c:Q * (c + 1)]))
        P.emit("vector", lambda v: v.tensor_copy(stk[64:128, :], cs_hm[:, :]), inc=(s_v, 1))
        v_stk = s_v.n
        P.wait("scalar", s_v, v_stk)
        for s_, v_ in last_cs_ld[pb]:
            P.wait("scalar", s_, v_)
        P.emit("scalar", lambda e, c=c: e.dma_start(out=cs_scr[c % 2], in_=cs_hm[:, :]), inc=(s_spcs[pb], 16), dma=True)
        P.emit("scalar", lambda e, c=c: e.dma_start(out=ce_scr[c % 2], in_=cs_hm[:, Q - 1:Q]), inc=(s_spcs[pb], 16), dma=True)
        sp_cs = (s_spcs[pb], s_spcs[pb].n)
        prev_sp_cs = sp_cs
        P.emit("scalar", lambda e: e.activation(
            w_hm[:, :], cs_hm[:, :], AF.Exp, bias=cs_hm[:, Q - 1:Q], scale=-1.0), inc=(s_sc, 1))
        sc_whm = s_sc.n
        P.wait("tensor", s_v, v_stk)
        P.emit("tensor", lambda t: t.transpose(pf[4].ap()[0:128, 0:128], stk[:, :], ident_f[:, :]), inc=(s_te, 1))
        P.wait("vector", s_te, s_te.n)
        P.emit("vector", lambda v: v.tensor_copy(trc[:, :], pf[4][0:128, 0:128]))
        P.emit("vector", lambda v: v.tensor_copy(dtT_bf[:, :], trc[:, 0:64]))
        P.emit("vector", lambda v: v.tensor_scalar_mul(trc[:, 64:128], trc[:, 64:128], -1.0), inc=(s_v, 1))
        v_trc = s_v.n
        P.wait("tensor", s_sc, sc_whm)
        P.emit("tensor", lambda t: t.transpose(pf[5].ap()[0:128, 0:64], w_hm[:, :], ident_f[0:64, 0:64]), inc=(s_te, 1))
        P.wait("vector", s_te, s_te.n)
        P.emit("vector", lambda v: v.tensor_copy(wT_f[:, :], pf[5][0:128, 0:64]))
        P.emit("vector", lambda v: v.tensor_copy(wT_bf[:, :], wT_f[:, :]), inc=(s_v, 1))
        v_trans = s_v.n
        P.wait("sync", *sp_cs)
        if c >= 2:
            P.wait("sync", s_sc, sc_ab_hist[c - 2])
        P.emit("sync", lambda e, c=c: e.dma_start(
            out=ab_t[c][:, :],
            in_=ce_scr[c % 2].rearrange("h o -> o h").broadcast_to((128, NH))), inc=(s_ab[pb], 16), dma=True)
        P.wait("scalar", s_ab[pb], s_ab[pb].n)
        P.emit("scalar", lambda e, c=c: e.activation(ab_t[c][:, :], ab_t[c][:, :], AF.Exp), inc=(s_sc, 1))
        sc_ab = s_sc.n
        sc_ab_hist[c] = sc_ab
        P.wait("tensor", s_v, v_trc)
        P.emit("tensor", lambda t, c=c: t.transpose(
            pbf[0][0:128, 0:128], bfm[:, Q * c:Q * (c + 1)], ident_bf[:, :]), inc=(s_te, 1))
        P.emit("tensor", lambda t, c=c: t.matmul(
            pf[4].ap()[0:128, 0:128], bfm[:, Q * c:Q * (c + 1)], cfm[:, Q * c:Q * (c + 1)],
            start=True, stop=True), inc=(s_te, 1))
        P.wait("vector", s_te, s_te.n)
        P.emit("vector", lambda v: v.tensor_copy(bt_b[:, :], pbf[0][0:128, 0:128]))
        P.emit("vector", lambda v: v.tensor_copy(gm_b[:, :], pf[4][0:128, 0:128]))
        for j in range(4):
            P.emit("vector", lambda v, c=c, j=j: v.tensor_copy(crep[:, 128 * j:128 * (j + 1)], cfm[:, Q * c:Q * (c + 1)]))
        for j in range(4):
            P.emit("vector", lambda v, j=j: v.tensor_copy(grep[:, 128 * j:128 * (j + 1)], gm_b[:, :]),
                   inc=(s_v, 1) if j == 3 else None)
        v_btg = s_v.n
        rt_ld = {}
        for g in range(2):
            P.wait("sync", *sp_cs)
            P.wait("sync", s_sc, last_rt_sc.get(g % 2, 0))
            P.emit("sync", lambda e, c=c, g=g: e.dma_start(
                out=rt_b[g % 2][:, :],
                in_=cs_scr[c % 2].rearrange("h t -> (h t)").unsqueeze(0)[0:1, 512 * g:512 * (g + 1)].broadcast_to((128, 512))),
                inc=(s_rt[g % 2], 16), dma=True)
            rt_ld[g] = (s_rt[g % 2], s_rt[g % 2].n)

        def stage_E(g):
            gb = g % 2
            py = pf[0 + gb]
            pS = pf[2 + gb]
            P.wait("vector", s_te, last_grp_te[(c, g)])
            P.wait("vector", s_sc, sc_ab)
            P.wait("vector", *last_grp_sp.get((c, g - 2), last_grp_sp.get((c - 1, 14 + g), (s_spg, 0))))
            for hj in range(4):
                hh = 4 * g + hj
                P.emit("vector", lambda v, c=c, hh=hh, hj=hj, pS=pS: v.scalar_tensor_tensor(
                    h_f32[:, 128 * hh:128 * (hh + 1)], h_f32[:, 128 * hh:128 * (hh + 1)],
                    ab_t[c][:, hh:hh + 1], pS[0:128, 128 * hj:128 * (hj + 1)], ALU.mult, ALU.add))
                P.emit("vector", lambda v, pb=pb, gb=gb, hh=hh, hj=hj, py=py: v.scalar_tensor_tensor(
                    y4_b[gb][:, 128 * hj:128 * (hj + 1)], Xc[pb][:, 128 * hh:128 * (hh + 1)],
                    dbc[:, hh:hh + 1], py[0:128, 128 * hj:128 * (hj + 1)], ALU.mult, ALU.add),
                    inc=(s_v, 1) if hj == 3 else None)
            last_grp_v[(c, g)] = s_v.n
            P.wait("scalar", s_v, last_grp_v[(c, g)])
            P.emit("scalar", lambda e, g=g: e.copy(
                h_bf[:, 512 * g:512 * (g + 1)], h_f32[:, 512 * g:512 * (g + 1)]), inc=(s_sc, 1))
            P.emit("scalar", lambda e, c=c, g=g, gb=gb: e.dma_start(
                out=y_scr.rearrange("(hh p) t -> p hh t", p=128)[:, 4 * g:4 * (g + 1), Q * c:Q * (c + 1)],
                in_=y4_b[gb].ap().rearrange("p (hh t) -> p hh t", t=Q)), inc=(s_y[gb], 16), dma=True)
            P.emit("scalar", lambda e, c=c, g=g, gb=gb: e.dma_start(
                out=cd_scr.rearrange("(hh n) t -> n hh t", n=128)[:, 4 * g:4 * (g + 1), Q * c:Q * (c + 1)],
                in_=cdec_b[gb].ap().rearrange("n (hh t) -> n hh t", t=Q)), inc=(s_y[gb], 16), dma=True)
            last_grp_sp[(c, g)] = (s_y[gb], s_y[gb].n)

        pend = None
        for g in range(16):
            gb = g % 2
            py = pf[0 + gb]
            pS = pf[2 + gb]
            P.wait("vector", *rt_ld[g])
            P.wait("vector", s_v, v_btg)
            P.wait("vector", s_sc, last_grp_sc.get((c, g - 2), 0))
            P.wait("vector", s_te, last_grp_te.get((c, g - 2), last_grp_te.get((c - 1, 14 + g), 0)))
            P.emit("vector", lambda v, gb=gb: v.tensor_add(rtm_b[gb][:, :], rt_b[gb][:, :], mneg4[:, :]), inc=(s_v, 1))
            v_rtm = s_v.n
            P.wait("scalar", s_v, v_rtm)
            P.wait("scalar", s_v, v_trans)
            P.emit("scalar", lambda e, gb=gb: e.activation(edec_b[gb][:, :], rt_b[gb][:, :], AF.Exp), inc=(s_sc, 1))
            for hj in range(4):
                hh = 4 * g + hj
                P.emit("scalar", lambda e, gb=gb, hj=hj, hh=hh: e.activation(
                    l4_b[gb][:, 128 * hj:128 * (hj + 1)], rtm_b[gb][:, 128 * hj:128 * (hj + 1)],
                    AF.Exp, bias=trc[:, 64 + hh:65 + hh]), inc=(s_sc, 1) if hj == 3 else None)
            sc_exps = s_sc.n
            last_rt_sc[gb] = sc_exps
            last_grp_sc[(c, g)] = sc_exps
            if g + 2 < 16:
                P.wait("sync", s_sc, sc_exps)
                P.emit("sync", lambda e, c=c, g=g, gb=gb: e.dma_start(
                    out=rt_b[gb][:, :],
                    in_=cs_scr[c % 2].rearrange("h t -> (h t)").unsqueeze(0)[0:1, 512 * (g + 2):512 * (g + 3)].broadcast_to((128, 512))),
                    inc=(s_rt[gb], 16), dma=True)
                rt_ld[g + 2] = (s_rt[gb], s_rt[gb].n)
            P.wait("vector", s_sc, sc_exps)
            P.wait("vector", *last_grp_sp.get((c, g - 2), last_grp_sp.get((c - 1, 14 + g), (s_spg, 0))))
            P.emit("vector", lambda v, gb=gb: v.tensor_mul(cdec_b[gb][:, :], crep[:, :], edec_b[gb][:, :]))
            P.emit("vector", lambda v, gb=gb: v.tensor_mul(m4_b[gb][:, :], grep[:, :], l4_b[gb][:, :]), inc=(s_v, 1))
            P.wait("vector", *slab_ld[c])
            for hj in range(4):
                hh = 4 * g + hj
                P.emit("vector", lambda v, pb=pb, gb=gb, hh=hh, hj=hj: v.scalar_tensor_tensor(
                    dtxT[gb][:, 128 * hj:128 * (hj + 1)], XTc[pb][:, 128 * hh:128 * (hh + 1)],
                    dtT_bf[:, hh:hh + 1], XTc[pb][:, 128 * hh:128 * (hh + 1)], ALU.mult, ALU.bypass))
                P.emit("vector", lambda v, gb=gb, hh=hh, hj=hj: v.scalar_tensor_tensor(
                    bdT[gb][:, 128 * hj:128 * (hj + 1)], bt_b[:, :], wT_bf[:, hh:hh + 1],
                    bt_b[:, :], ALU.mult, ALU.bypass), inc=(s_v, 1) if hj == 3 else None)
            v_heads = s_v.n
            P.wait("tensor", s_v, v_heads)
            P.wait("tensor", s_v, last_grp_v.get((c, g - 2), last_grp_v.get((c - 1, 14 + g), 0)))
            for hj in range(4):
                hh = 4 * g + hj
                P.emit("tensor", lambda t, gb=gb, hj=hj, py=py: t.matmul(
                    py.ap()[0:128, 128 * hj:128 * (hj + 1)], dtxT[gb][:, 128 * hj:128 * (hj + 1)],
                    m4_b[gb][:, 128 * hj:128 * (hj + 1)], start=True, stop=False))
                P.emit("tensor", lambda t, gb=gb, hj=hj, hh=hh, py=py: t.matmul(
                    py.ap()[0:128, 128 * hj:128 * (hj + 1)], h_bf[:, 128 * hh:128 * (hh + 1)],
                    cdec_b[gb][:, 128 * hj:128 * (hj + 1)], start=False, stop=True))
                P.emit("tensor", lambda t, gb=gb, hj=hj, pS=pS: t.matmul(
                    pS.ap()[0:128, 128 * hj:128 * (hj + 1)], bdT[gb][:, 128 * hj:128 * (hj + 1)],
                    dtxT[gb][:, 128 * hj:128 * (hj + 1)], start=True, stop=True),
                    inc=(s_te, 1) if hj == 3 else None)
            last_grp_te[(c, g)] = s_te.n
            if pend is not None:
                stage_E(pend)
            pend = g
        stage_E(pend)
        last_cs_ld[pb] = [rt_ld[14], rt_ld[15], (s_ab[pb], s_ab[pb].n)]
    v_p3a = s_v.n
    if debug:
        P.wait("scalar", s_v, v_p3a)
        P.emit("scalar", lambda e: e.dma_start(out=dbg["dbg_h"], in_=h_f32[:]), inc=(s_spg, 16), dma=True)

    P.barrier(s_bar)
    cx3a.close()

    cxX = contextlib.ExitStack()
    metag = sb(cxX, "metag", [8, NH], F32)
    w8 = sb(cxX, "w8", [8, NH], F32)
    pmcol = sb(cxX, "pmcol", [8, 1], F32)
    btT = sb(cxX, "btT", [8, 8], F32)
    meta_bc = sb(cxX, "meta_bc", [128, NH], F32)
    wbc = [sb(cxX, f"wbc{j}", [128, NH], F32) for j in range(8)]
    sg_b = [sb(cxX, f"sg_b{i}", [128, NH * 128], BF16) for i in range(2)]

    for g8 in range(8):
        P.emit("scalar", lambda e, g8=g8: e.dma_start(
            out=cc_s_in.ap().rearrange("(hh n) p -> n hh p", n=128)[:, 8 * g8:8 * (g8 + 1), :],
            in_=h_bf.ap().rearrange("n (hh p) -> n hh p", p=128)[:, 8 * g8:8 * (g8 + 1), :]), inc=(s_ccw, 16), dma=True)
    P.emit("scalar", lambda e: e.dma_start(out=small_scr[0:1, 0:64].rearrange("o h -> h o"),
                                           in_=logtot[:, :]), inc=(s_sm, 16), dma=True)
    P.wait("sync", s_sm, s_sm.n)
    P.emit("sync", lambda e: e.dma_start(out=meta_bc[:], in_=small_scr[0:1, 0:64].broadcast_to((128, NH))), inc=(s_meta, 16), dma=True)
    P.wait("scalar", s_meta, s_meta.n)
    P.emit("scalar", lambda e: e.dma_start(out=cc_m_in.ap(), in_=meta_bc[:]), inc=(s_ccw, 16), dma=True)
    P.wait("gpsimd", s_ccw, s_ccw.n)
    P.emit("gpsimd", lambda g: g.collective_compute(
        "AllGather", ALU.bypass, replica_groups=[[0, 1, 2, 3, 4, 5, 6, 7]],
        ins=[cc_s_in.ap().opt()], outs=[cc_s_out.ap().opt()]), inc=(s_cc, 1))
    P.emit("gpsimd", lambda g: g.collective_compute(
        "AllGather", ALU.bypass, replica_groups=[[0, 1, 2, 3, 4, 5, 6, 7]],
        ins=[cc_m_in.ap().opt()], outs=[cc_m_out.ap().opt()]), inc=(s_cc, 1))
    P.wait("sync", s_cc, 2)
    P.wait("tensor", s_cc, 2)

    P.emit("sync", lambda e: e.dma_start(
        out=metag[:, :],
        in_=cc_m_out.ap().rearrange("(j r) h -> j r h", r=128)[:, 0:1, :].rearrange("j o h -> j (o h)")), inc=(s_meta, 16), dma=True)
    P.emit("sync", lambda e: e.dma_start(out=btT[:, :], in_=betweenT), inc=(s_meta, 16), dma=True)
    P.emit("sync", lambda e: e.dma_start(out=pmcol[:, :], in_=prevmask), inc=(s_meta, 16), dma=True)
    P.wait("tensor", s_meta, s_meta.n)
    P.emit("tensor", lambda t: t.matmul(pf[4].ap()[0:8, 0:NH], btT[:, :], metag[:, :], start=True, stop=True), inc=(s_te, 1))
    P.wait("scalar", s_te, s_te.n)
    P.emit("scalar", lambda e: e.activation(w8[:, :], pf[4][0:8, 0:NH], AF.Exp), inc=(s_sc, 1))
    P.wait("vector", s_sc, s_sc.n)
    P.emit("vector", lambda v: v.scalar_tensor_tensor(
        w8[:, :], w8[:, :], pmcol[:, 0:1], w8[:, :], ALU.mult, ALU.bypass), inc=(s_v, 1))
    P.wait("scalar", s_v, s_v.n)
    P.emit("scalar", lambda e: e.dma_start(out=wj_scr.rearrange("o (j h) -> j h", j=8), in_=w8[:, :]), inc=(s_ccw, 16), dma=True)
    P.wait("sync", s_ccw, s_ccw.n)
    for j in range(8):
        P.emit("sync", lambda e, j=j: e.dma_start(
            out=wbc[j][:, :], in_=wj_scr[0:1, NH * j:NH * (j + 1)].broadcast_to((128, NH))), inc=(s_meta, 16), dma=True)
    P.emit("vector", lambda v: v.memset(h_f32[:], 0.0), inc=(s_v, 1))
    vj = {}
    for j in range(8):
        jb = j % 2
        if j >= 2:
            P.wait("sync", s_v, vj[j - 2])
        for g8 in range(8):
            P.emit("sync", lambda e, j=j, jb=jb, g8=g8: e.dma_start(
                out=sg_b[jb].ap().rearrange("n (hh p) -> n hh p", p=128)[:, 8 * g8:8 * (g8 + 1), :],
                in_=cc_s_out.ap().rearrange("(j hh n) p -> j n hh p", hh=NH, n=128)[j][:, 8 * g8:8 * (g8 + 1), :]),
                inc=(s_sl[jb], 16), dma=True)
        P.wait("vector", s_sl[jb], s_sl[jb].n)
        P.wait("vector", s_meta, s_meta.n)
        for hh in range(NH):
            P.emit("vector", lambda v, j=j, jb=jb, hh=hh: v.scalar_tensor_tensor(
                h_f32[:, 128 * hh:128 * (hh + 1)], sg_b[jb][:, 128 * hh:128 * (hh + 1)],
                wbc[j][:, hh:hh + 1], h_f32[:, 128 * hh:128 * (hh + 1)], ALU.mult, ALU.add),
                inc=(s_v, 1) if hh == NH - 1 else None)
        vj[j] = s_v.n
    P.wait("scalar", s_v, s_v.n)
    P.emit("scalar", lambda e: e.copy(h_bf[:, :], h_f32[:, :]), inc=(s_sc, 1))
    P.barrier(s_bar)
    cxX.close()

    cx3b = contextlib.ExitStack()
    Yc = [sb(cx3b, f"Yc{i}", [128, NH * 128], BF16) for i in range(2)]
    Zc = [sb(cx3b, f"Zc{i}", [128, NH * 128], BF16) for i in range(2)]
    Cc_ = [sb(cx3b, f"Cc_{i}", [128, NH * 128], BF16) for i in range(2)]
    o2_b = [sb(cx3b, f"o2_b{i}", [128, 512], F32) for i in range(2)]
    th_b = [sb(cx3b, f"th_b{i}", [128, 512], F32) for i in range(2)]
    o3_b = [sb(cx3b, f"o3_b{i}", [128, 512], BF16) for i in range(2)]
    sq_b = [sb(cx3b, f"sq_b{i}", [128, 512], BF16) for i in range(2)]
    o4_b = [sb(cx3b, f"o4_b{i}", [128, 512], BF16) for i in range(2)]
    rrow = sb(cx3b, "rrow", [1, Q], F32)

    for s_ in (s_y[0], s_y[1], s_z[0], s_z[1]):
        P.wait("sync", s_, s_.n)
    slab_load(Yc[0], y_scr, 0, s_sl[0])
    slab_load(Zc[0], z_scr, 0, s_sl[0])
    slab_load(Cc_[0], cd_scr, 0, s_sl[0])
    slab3b_ld = {0: (s_sl[0], s_sl[0].n)}
    bl_v = {}
    bl_sp = {}
    bl_te = {}
    bl_sc = {}
    bl_sqte = {}
    bl_silu = {}
    rrow_v = {}

    for c in range(NCHUNK):
        pb = c % 2
        if c + 1 < NCHUNK:
            if c >= 1:
                P.wait("sync", s_v, bl_v.get((c - 1, 15), 0))
                P.wait("sync", s_sc, bl_sc.get((c - 1, 15), 0))
                P.wait("sync", s_te, bl_te.get((c - 1, 15), 0))
            slab_load(Yc[(c + 1) % 2], y_scr, c + 1, s_sl[(c + 1) % 2])
            slab_load(Zc[(c + 1) % 2], z_scr, c + 1, s_sl[(c + 1) % 2])
            slab_load(Cc_[(c + 1) % 2], cd_scr, c + 1, s_sl[(c + 1) % 2])
            slab3b_ld[c + 1] = (s_sl[(c + 1) % 2], s_sl[(c + 1) % 2].n)
        def stage_F(g):
            gb = g % 2
            py = pf[0 + gb]
            P.wait("vector", s_te, bl_te[(c, g)])
            P.wait("vector", s_sc, bl_sc.get((c, g - 2), 0))
            P.emit("vector", lambda v, pb=pb, g=g, gb=gb, py=py: v.tensor_add(
                o2_b[gb][:, :], py[0:128, 0:512], Yc[pb][:, 512 * g:512 * (g + 1)]))
            P.wait("vector", s_sc, bl_silu[(c, g)])
            P.emit("vector", lambda v, gb=gb: v.tensor_mul(o3_b[gb][:, :], o2_b[gb][:, :], th_b[gb][:, :]), inc=(s_v, 1))
            v_o3 = s_v.n
            P.wait("scalar", s_v, v_o3)
            P.wait("scalar", s_te, bl_sqte.get((c, g - 2), 0))
            P.emit("scalar", lambda e, gb=gb: e.activation(sq_b[gb][:, :], o3_b[gb][:, :], AF.Square), inc=(s_sc, 1))
            bl_sc[(c, g)] = s_sc.n
            P.wait("vector", *bl_sp.get((c, g - 2), bl_sp.get((c - 1, 14 + g), (s_spg, 0))))
            for hj in range(4):
                hh = 4 * g + hj
                P.emit("vector", lambda v, gb=gb, hh=hh, hj=hj: v.scalar_tensor_tensor(
                    o4_b[gb][:, 128 * hj:128 * (hj + 1)], o3_b[gb][:, 128 * hj:128 * (hj + 1)],
                    nwcols[:, hh:hh + 1], o3_b[gb][:, 128 * hj:128 * (hj + 1)], ALU.mult, ALU.bypass))
                P.emit("vector", lambda v, c=c, hh=hh: v.scalar_tensor_tensor(
                    h_f32[:, 128 * hh:128 * (hh + 1)], h_f32[:, 128 * hh:128 * (hh + 1)],
                    ab_t[c][:, hh:hh + 1], h_f32[:, 128 * hh:128 * (hh + 1)], ALU.mult, ALU.bypass),
                    inc=(s_v, 1) if hj == 3 else None)
            bl_v[(c, g)] = s_v.n
            P.wait("scalar", s_v, bl_v[(c, g)])
            P.emit("scalar", lambda e, g=g: e.copy(
                h_bf[:, 512 * g:512 * (g + 1)], h_f32[:, 512 * g:512 * (g + 1)]), inc=(s_sc, 1))
            P.wait("tensor", s_sc, bl_sc[(c, g)])
            if g == 0 and c >= 1:
                P.wait("tensor", s_v, rrow_v[c - 1])
            for hj in range(4):
                P.emit("tensor", lambda t, g=g, gb=gb, hj=hj: t.matmul(
                    pf[5].ap()[0:1, 0:Q], onescol_bf[:, 0:1], sq_b[gb][:, 128 * hj:128 * (hj + 1)],
                    start=(g == 0 and hj == 0), stop=(g == 15 and hj == 3), skip_group_check=True),
                    inc=(s_te, 1) if hj == 3 else None)
            bl_sqte[(c, g)] = s_te.n
            P.emit("scalar", lambda e, c=c, g=g, gb=gb: e.dma_start(
                out=yn_scr.rearrange("(hh p) t -> p hh t", p=128)[:, 4 * g:4 * (g + 1), Q * c:Q * (c + 1)],
                in_=o4_b[gb].ap().rearrange("p (hh t) -> p hh t", t=Q)), inc=(s_y[gb], 16), dma=True)
            bl_sp[(c, g)] = (s_y[gb], s_y[gb].n)

        pend = None
        for g in range(16):
            gb = g % 2
            py = pf[0 + gb]
            P.wait("tensor", *slab3b_ld[c])
            P.wait("tensor", s_v, bl_v.get((c, g - 2), bl_v.get((c - 1, 14 + g), 0)))
            for hj in range(4):
                hh = 4 * g + hj
                P.emit("tensor", lambda t, pb=pb, hh=hh, hj=hj, py=py: t.matmul(
                    py.ap()[0:128, 128 * hj:128 * (hj + 1)], h_bf[:, 128 * hh:128 * (hh + 1)],
                    Cc_[pb][:, 128 * hh:128 * (hh + 1)], start=True, stop=True),
                    inc=(s_te, 1) if hj == 3 else None)
            bl_te[(c, g)] = s_te.n
            P.wait("scalar", *slab3b_ld[c])
            P.wait("scalar", s_v, bl_v.get((c, g - 2), bl_v.get((c - 1, 14 + g), 0)))
            P.emit("scalar", lambda e, pb=pb, g=g, gb=gb: e.activation(
                th_b[gb][:, :], Zc[pb][:, 512 * g:512 * (g + 1)], AF.Silu), inc=(s_sc, 1))
            bl_silu[(c, g)] = s_sc.n
            if pend is not None:
                stage_F(pend)
            pend = g
        stage_F(pend)
        P.wait("vector", s_te, bl_sqte[(c, 15)])
        P.wait("vector", s_spg, s_spg.n)
        P.emit("vector", lambda v: v.tensor_copy(rrow[:, :], pf[5][0:1, 0:Q]), inc=(s_v, 1))
        rrow_v[c] = s_v.n
        P.wait("scalar", s_v, s_v.n)
        P.emit("scalar", lambda e: e.activation(rrow[:, :], rrow[:, :], AF.Ln, bias=cbias[0:1, 3:4], scale=1.0 / DSSM))
        P.emit("scalar", lambda e: e.activation(rrow[:, :], rrow[:, :], AF.Exp, scale=-0.5), inc=(s_sc, 1))
        P.wait("scalar", s_sc, s_sc.n)
        P.emit("scalar", lambda e, c=c: e.dma_start(out=rstd_scr[c:c + 1, :], in_=rrow[:, :]), inc=(s_spg, 16), dma=True)
    P.barrier(s_bar)
    cx3b.close()
    cxH.close()

    cx4 = contextlib.ExitStack()
    wo_bf = [sb(cx4, f"wo_bf{i}", [128, NH * 512], BF16) for i in range(2)]
    yns = [sb(cx4, f"yns{i}", [128, 4 * SPAN], BF16) for i in range(2)]
    rst8 = sb(cx4, "rst8", [NCHUNK, Q], F32)
    rstT = sb(cx4, "rstT", [128, NCHUNK], F32)
    osc = [sb(cx4, f"osc{i}", [128, 8 * 512], F32) for i in range(2)]
    fin = [sb(cx4, f"fin{i}", [128, 512], F32) for i in range(2)]

    P.wait("sync", s_spg, s_spg.n)
    for s_ in (s_y[0], s_y[1]):
        P.wait("sync", s_, s_.n)
    P.emit("sync", lambda e: e.dma_start(out=rst8[:, :], in_=rstd_scr), inc=(s_meta, 16), dma=True)
    P.wait("tensor", s_meta, s_meta.n)
    P.emit("tensor", lambda t: t.transpose(pf[4].ap()[0:128, 0:8], rst8[:, :], ident_f[0:8, 0:8]), inc=(s_te, 1))
    P.wait("vector", s_te, s_te.n)
    P.emit("vector", lambda v: v.tensor_copy(rstT[:, :], pf[4][0:128, 0:8]), inc=(s_v, 1))
    v_rstT = s_v.n

    def emit_woload(dc):
        wb = dc % 2
        for gkc in range(4):
            P.emit("gpsimd", lambda g, wb=wb, dc=dc, gkc=gkc: g.dma_start(
                out=wo_bf[wb].ap().rearrange("p (kc c) -> p kc c", c=512)[:, 16 * gkc:16 * (gkc + 1), :],
                in_=w_out.rearrange("(kc p) c -> p kc c", p=128)[:, 16 * gkc:16 * (gkc + 1), 512 * dc:512 * (dc + 1)]),
                inc=(s_w[wb], 16), dma=True)
        wo_ld[dc] = (s_w[wb], s_w[wb].n)

    wo_ld = {}
    emit_woload(0)
    te_dc = {}
    te_blk = {}
    v_evac = {}
    st_evac = {}
    osc_ld = {}
    pview = [pf[i].ap()[0:128, 0:512] for i in range(6)] + \
            [pbf[i].ap().bitcast(F32)[0:128, 0:512] for i in range(2)]

    for dc in range(8):
        wb = dc % 2
        ob = dc % 2
        if dc >= 2:
            P.wait("sync", s_v, v_evac[dc - 2])
        if dc >= 1:
            P.wait("sync", *st_evac.get(dc - 1, (s_spg, 0)))
        P.emit("sync", lambda e, ob=ob, dc=dc: e.dma_start(
            out=osc[ob].ap().rearrange("p (tc c) -> p tc c", c=512),
            in_=out[:, 512 * dc:512 * (dc + 1)].rearrange("(tc p) c -> p tc c", p=128)),
            inc=(s_os[ob], 16), dma=True)
        osc_ld[dc] = (s_os[ob], s_os[ob].n)
        if dc + 1 < 8:
            if dc >= 1:
                P.wait("gpsimd", s_te, te_dc[dc - 1])
            emit_woload(dc + 1)
        P.wait("tensor", *wo_ld[dc])
        P.wait("tensor", s_v, v_rstT)
        if dc >= 1:
            P.wait("tensor", s_v, v_evac[dc - 1])
        for blk in range(16):
            yb = blk % 2
            P.wait("sync", s_te, te_blk.get((dc, blk - 2), te_blk.get((dc - 1, 14 + blk), 0)))
            P.emit("sync", lambda e, yb=yb, blk=blk: e.dma_start(
                out=yns[yb].ap().rearrange("p (k t) -> p k t", k=4),
                in_=yn_scr[512 * blk:512 * (blk + 1), :].rearrange("(k p) t -> p k t", p=128)),
                inc=(s_sl[yb], 16), dma=True)
            P.wait("tensor", s_sl[yb], s_sl[yb].n)
            for k in range(4):
                ch = 4 * blk + k
                for tc in range(8):
                    P.emit("tensor", lambda t, wb=wb, yb=yb, k=k, ch=ch, tc=tc: t.matmul(
                        pview[tc], yns[yb][:, 1024 * k + 128 * tc:1024 * k + 128 * (tc + 1)],
                        wo_bf[wb][:, 512 * ch:512 * (ch + 1)],
                        start=(ch == 0), stop=(ch == NH - 1), skip_group_check=True),
                        inc=(s_te, 1) if tc == 7 else None)
            te_blk[(dc, blk)] = s_te.n
        te_dc[dc] = s_te.n
        P.wait("vector", s_te, te_dc[dc])
        P.wait("vector", *osc_ld[dc])
        for tc in range(8):
            fb = tc % 2
            if tc >= 2 or dc >= 1:
                P.wait("vector", *st_evac.get((dc, tc - 2), st_evac.get((dc - 1, 6 + tc), (s_spg, 0))))
            P.emit("vector", lambda v, ob=ob, fb=fb, tc=tc: v.scalar_tensor_tensor(
                fin[fb][:, :], pview[tc], rstT[:, tc:tc + 1], osc[ob][:, 512 * tc:512 * (tc + 1)],
                ALU.mult, ALU.add), inc=(s_v, 1))
            v_evac[(dc, tc)] = s_v.n
            P.wait("sync", s_v, s_v.n)
            P.emit("sync", lambda e, fb=fb, tc=tc, dc=dc: e.dma_start(
                out=out[128 * tc:128 * (tc + 1), 512 * dc:512 * (dc + 1)], in_=fin[fb][:, :]), inc=(s_st[fb], 16), dma=True)
            st_evac[(dc, tc)] = (s_st[fb], s_st[fb].n)
        v_evac[dc] = s_v.n
        st_evac[dc] = (s_st[0], s_st[0].n)

    if debug:
        for s_ in (s_y[0], s_y[1], s_z[0], s_z[1], s_xo[0], s_xo[1], s_spg):
            P.wait("gpsimd", s_, s_.n)
        P.emit("gpsimd", lambda g: g.dma_start(out=dbg["dbg_z"], in_=z_scr), inc=(s_spg, 16), dma=True)
        P.emit("gpsimd", lambda g: g.dma_start(out=dbg["dbg_x"], in_=x_scr), inc=(s_spg, 16), dma=True)
        P.emit("gpsimd", lambda g: g.dma_start(out=dbg["dbg_y"], in_=y_scr), inc=(s_spg, 16), dma=True)
        P.emit("gpsimd", lambda g: g.dma_start(out=dbg["dbg_yn"], in_=yn_scr), inc=(s_spg, 16), dma=True)
        P.wait("gpsimd", s_spg, s_spg.n)

    P.wait("sync", s_st[0], s_st[0].n)
    P.wait("sync", s_st[1], s_st[1].n)
    P.barrier(s_bar)

    with nc.Block() as block:
        @block.sync
        def _(eng):
            for fn in P.q["sync"]:
                fn(eng)

        @block.scalar
        def _(eng):
            for fn in P.q["scalar"]:
                fn(eng)

        @block.gpsimd
        def _(eng):
            for fn in P.q["gpsimd"]:
                fn(eng)

        @block.vector
        def _(eng):
            for fn in P.q["vector"]:
                fn(eng)

        @block.tensor
        def _(eng):
            for fn in P.q["tensor"]:
                fn(eng)

    cx4.close()
    ctx.close()
    return nc


_NC_CACHE = {}


def kernel(**inputs):
    x = np.ascontiguousarray(inputs["L0_attn_hidden_states"], dtype=np.float32)
    w_in_a = np.ascontiguousarray(inputs["L0_attn_W_in"], dtype=np.float32)
    conv_w_a = np.ascontiguousarray(inputs["L0_attn_conv_w"], dtype=np.float32)
    dt_bias_a = np.ascontiguousarray(inputs["L0_attn_dt_bias"], dtype=np.float32).reshape(NH, 1)
    a_log_a = np.ascontiguousarray(inputs["L0_attn_A_log"], dtype=np.float32).reshape(NH, 1)
    d_par_a = np.ascontiguousarray(inputs["L0_attn_D_param"], dtype=np.float32).reshape(1, NH)
    nw_a = np.ascontiguousarray(inputs["L0_attn_norm_weight"], dtype=np.float32).reshape(NH, 128)
    w_out_a = np.ascontiguousarray(inputs["L0_attn_W_out"], dtype=np.float32)
    w_res_a = np.ascontiguousarray(inputs["L0_ffn_W_res"], dtype=np.float32)
    w_pre_a = np.ascontiguousarray(inputs["L0_ffn_W_pre"], dtype=np.float32)
    w_post_a = np.ascontiguousarray(inputs["L0_ffn_W_post"], dtype=np.float32)
    h_bases_a = np.concatenate([
        np.asarray(inputs["L0_ffn_H_res_base"], np.float32).reshape(-1),
        np.asarray(inputs["L0_ffn_H_pre_base"], np.float32).reshape(-1),
        np.asarray(inputs["L0_ffn_H_post_base"], np.float32).reshape(-1)]).reshape(1, 24)
    gate_a = np.ascontiguousarray(inputs["L0_ffn_gate"], dtype=np.float32).reshape(1, 1)
    w_br_a = np.ascontiguousarray(inputs["L0_ffn_W_branch"], dtype=np.float32)

    if "nc" not in _NC_CACHE:
        _NC_CACHE["nc"] = build_program(debug=False)
    nc = _NC_CACHE["nc"]

    in_maps = []
    for cid in range(8):
        b, s = cid // 4, cid % 4
        t0 = s * SPAN
        if s == 0:
            xh_a = np.concatenate([np.zeros((HALO, HD), np.float32), x[b, t0:t0 + SPAN]], 0)
        else:
            xh_a = x[b, t0 - HALO:t0 + SPAN]
        between = np.zeros((8, 8), np.float32)
        pmask = np.zeros((8, 1), np.float32)
        for j in range(8):
            jb, js = j // 4, j % 4
            if jb == b and js < s:
                pmask[j, 0] = 1.0
                for m in range(8):
                    mb, ms = m // 4, m % 4
                    if mb == b and js < ms < s:
                        between[m, j] = 1.0
        in_maps.append({
            "xh": np.ascontiguousarray(xh_a), "w_in": w_in_a, "conv_w": conv_w_a,
            "dt_bias": dt_bias_a, "a_log": a_log_a, "d_par": d_par_a, "nw_in": nw_a,
            "w_out": w_out_a, "w_res": w_res_a, "w_pre": w_pre_a, "w_post": w_post_a,
            "h_bases": h_bases_a, "gate": gate_a, "w_br": w_br_a,
            "betweenT": between, "prevmask": pmask,
        })
    res = run_bass_kernel_spmd(nc, in_maps, core_ids=list(range(8)))
    outp = np.zeros((2, 4096, HD), np.float32)
    for cid in range(8):
        b, s = cid // 4, cid % 4
        outp[b, s * SPAN:(s + 1) * SPAN] = res.results[cid]["out"]
    return outp


# version 04 (final)
# speedup vs baseline: 1.2513x; 1.1307x over previous
import contextlib
import numpy as np
import concourse.bass as bass
import concourse.mybir as mybir
from concourse.bass_utils import run_bass_kernel_spmd

F32 = mybir.dt.float32
BF16 = mybir.dt.bfloat16
AF = mybir.ActivationFunctionType
ALU = mybir.AluOpType
AX = mybir.AxisListType.X

HD = 4096
DSSM = 8192
NH = 64
KCONV = 4
CONV_DIM = DSSM + 2 * 128
PROJ = DSSM + CONV_DIM + NH
SPAN = 1024
HALO = 3
TT = SPAN + HALO
Q = 128
NCHUNK = 8
NTILE = 8
NDC = 32
EPS_LN = 1e-5
EPS_RMS = 1.1920929e-07
EPS_MAM = 1e-5
NEG_BIG = -30000.0

ENGS = ("sync", "scalar", "gpsimd", "vector", "tensor")


class Sem:
    def __init__(self, nc, ctx, name):
        self.h = ctx.enter_context(nc.semaphore(name))
        self.n = 0


class Plan:

    CHAINED = ("vector", "scalar", "gpsimd")

    def __init__(self, nc):
        self.nc = nc
        self.q = {e: [] for e in ENGS}
        self.nbar = 0
        self.chain = {}
        self.te_sem = None
        self.dma_sems = []

    def set_chain(self, eng, sem):
        self.chain[eng] = sem

    def _chain_wait(self, eng):
        cs = self.chain.get(eng)
        if cs is not None and cs.n > 0:
            v = cs.n
            self.q[eng].append(lambda e, cs=cs, v=v: e.wait_ge(cs.h, v))

    def emit(self, eng, fn, inc=None, dma=False, chain=True):
        cs = self.chain.get(eng)
        if cs is not None:
            if chain is True or chain == 'start':
                self._chain_wait(eng)
            if dma or (inc is not None and inc[0] is not cs):
                sem, k = inc
                sem.n += k
                self.q[eng].append(lambda e, fn=fn, sem=sem, k=k: fn(e).then_inc(sem.h, k))
            elif chain is True or chain == 'end':
                cs.n += 1
                self.q[eng].append(lambda e, fn=fn, cs=cs: fn(e).then_inc(cs.h, 1))
            else:
                self.q[eng].append(fn)
            return
        if inc is not None:
            sem, k = inc
            sem.n += k
            self.q[eng].append(lambda e, fn=fn, sem=sem, k=k: fn(e).then_inc(sem.h, k))
        else:
            self.q[eng].append(fn)

    def wait(self, eng, sem, val=None):
        v = sem.n if val is None else val
        if v > 0:
            self.q[eng].append(lambda e, sem=sem, v=v: e.wait_ge(sem.h, v))

    def barrier(self, s_bar):
        self.nbar += 1
        tgt = 5 * self.nbar
        for s_ in self.dma_sems:
            if s_.n > 0:
                self.wait("sync", s_, s_.n)
        for e in ENGS:
            self._chain_wait(e)
            if e == "tensor" and self.te_sem is not None and self.te_sem.n > 0:
                self.wait(e, self.te_sem, self.te_sem.n)
            self.q[e].append(lambda eng, s=s_bar: eng.sem_inc(s.h, 1))
        s_bar.n = tgt
        for e in ENGS:
            self.wait(e, s_bar, tgt)


def build_program(debug=False):
    nc = bass.Bass("TRN2", target_bir_lowering=False)
    ctx = contextlib.ExitStack()

    xh = nc.dram_tensor("xh", [TT, HD], F32, kind="ExternalInput").ap()
    w_in = nc.dram_tensor("w_in", [HD, PROJ], F32, kind="ExternalInput").ap()
    conv_w = nc.dram_tensor("conv_w", [CONV_DIM, KCONV], F32, kind="ExternalInput").ap()
    dt_bias = nc.dram_tensor("dt_bias", [NH, 1], F32, kind="ExternalInput").ap()
    a_log = nc.dram_tensor("a_log", [NH, 1], F32, kind="ExternalInput").ap()
    d_par = nc.dram_tensor("d_par", [1, NH], F32, kind="ExternalInput").ap()
    nw_in = nc.dram_tensor("nw_in", [NH, 128], F32, kind="ExternalInput").ap()
    w_out = nc.dram_tensor("w_out", [DSSM, HD], F32, kind="ExternalInput").ap()
    w_res = nc.dram_tensor("w_res", [16, HD], F32, kind="ExternalInput").ap()
    w_pre = nc.dram_tensor("w_pre", [4, HD], F32, kind="ExternalInput").ap()
    w_post = nc.dram_tensor("w_post", [4, HD], F32, kind="ExternalInput").ap()
    h_bases = nc.dram_tensor("h_bases", [1, 24], F32, kind="ExternalInput").ap()
    gate = nc.dram_tensor("gate", [1, 1], F32, kind="ExternalInput").ap()
    w_br = nc.dram_tensor("w_br", [1024, 1024], F32, kind="ExternalInput").ap()
    betweenT = nc.dram_tensor("betweenT", [8, 8], F32, kind="ExternalInput").ap()
    prevmask = nc.dram_tensor("prevmask", [8, 1], F32, kind="ExternalInput").ap()
    out = nc.dram_tensor("out", [SPAN, HD], F32, kind="ExternalOutput").ap()

    dbg = {}
    if debug:
        for nm, shp, dt in (("dbg_z", [DSSM, SPAN], BF16), ("dbg_x", [DSSM, SPAN], BF16),
                            ("dbg_dt", [NH, TT], F32), ("dbg_y", [DSSM, SPAN], BF16),
                            ("dbg_yn", [DSSM, SPAN], BF16), ("dbg_h", [128, NH * 128], F32),
                            ("dbg_hall", [128, NTILE * 24], F32),
                            ("dbg_xnfm", [128, NDC * TT], BF16),
                            ("dbg_bc", [128, 2 * SPAN], BF16)):
            dbg[nm] = nc.dram_tensor(nm, shp, dt, kind="ExternalOutput").ap()

    xn_scr = nc.dram_tensor("xn_scr", [SPAN, HD], BF16).ap()
    z_scr = nc.dram_tensor("z_scr", [DSSM, SPAN], BF16).ap()
    x_scr = nc.dram_tensor("x_scr", [DSSM, SPAN], BF16).ap()
    y_scr = nc.dram_tensor("y_scr", [DSSM, SPAN], BF16).ap()
    yn_scr = nc.dram_tensor("yn_scr", [DSSM, SPAN], BF16).ap()
    cd_scr = nc.dram_tensor("cd_scr", [DSSM, SPAN], BF16).ap()
    cs_scr = [nc.dram_tensor(f"cs_scr{i}", [NH, Q], F32).ap() for i in range(2)]
    ce_scr = [nc.dram_tensor(f"ce_scr{i}", [NH, 1], F32).ap() for i in range(2)]
    rstd_scr = nc.dram_tensor("rstd_scr", [NCHUNK, Q], F32).ap()
    small_scr = nc.dram_tensor("small_scr", [1, 64], F32).ap()
    wj_scr = nc.dram_tensor("wj_scr", [1, 8 * NH], F32).ap()
    cc_s_in = nc.dram_tensor("cc_s_in", [DSSM, 128], BF16)
    cc_s_out = nc.dram_tensor("cc_s_out", [8 * DSSM, 128], BF16)
    cc_m_in = nc.dram_tensor("cc_m_in", [128, NH], F32)
    cc_m_out = nc.dram_tensor("cc_m_out", [8 * 128, NH], F32)

    P = Plan(nc)
    s_st = [Sem(nc, ctx, f"s_st{i}") for i in range(2)]
    s_p0 = Sem(nc, ctx, "s_p0")
    s_x = [Sem(nc, ctx, f"s_x{i}") for i in range(2)]
    s_spxn = [Sem(nc, ctx, f"s_spxn{i}") for i in range(2)]
    s_ffn = Sem(nc, ctx, "s_ffn")
    s_wffn = Sem(nc, ctx, "s_wffn")
    s_gb = Sem(nc, ctx, "s_gb")
    s_sm = Sem(nc, ctx, "s_sm")
    s_w = [Sem(nc, ctx, f"s_w{i}") for i in range(2)]
    s_z = [Sem(nc, ctx, f"s_z{i}") for i in range(2)]
    s_xo = [Sem(nc, ctx, f"s_xo{i}") for i in range(2)]
    s_mx = [Sem(nc, ctx, f"s_mx{i}") for i in range(2)]
    s_sl = [Sem(nc, ctx, f"s_sl{i}") for i in range(2)]
    s_spcs = [Sem(nc, ctx, f"s_spcs{i}") for i in range(2)]
    s_ab = [Sem(nc, ctx, f"s_ab{i}") for i in range(2)]
    s_rt = [Sem(nc, ctx, f"s_rt{i}") for i in range(2)]
    s_y = [Sem(nc, ctx, f"s_y{i}") for i in range(2)]
    s_spg = Sem(nc, ctx, "s_spg")
    s_meta = Sem(nc, ctx, "s_meta")
    s_ccw = Sem(nc, ctx, "s_ccw")
    s_os = [Sem(nc, ctx, f"s_os{i}") for i in range(2)]
    P.dma_sems = [s_p0, s_ffn, s_wffn, s_gb, s_sm, s_spg, s_meta, s_ccw] + \
        s_st + s_x + s_spxn + s_w + s_z + s_xo + s_mx + s_sl + s_spcs + s_ab + s_rt + s_y + s_os
    s_v = Sem(nc, ctx, "s_v")
    s_sc = Sem(nc, ctx, "s_sc")
    s_te = Sem(nc, ctx, "s_te")
    s_gp = Sem(nc, ctx, "s_gp")
    s_cc = Sem(nc, ctx, "s_cc")
    s_bar = Sem(nc, ctx, "s_bar")
    P.set_chain("vector", s_v)
    P.set_chain("scalar", s_sc)
    P.set_chain("gpsimd", s_gp)
    P.te_sem = s_te

    def sb(cx, name, shape, dt):
        return cx.enter_context(nc.sbuf_tensor(name, shape, dt))


    pf = [ctx.enter_context(nc.psum_tensor(f"pf{i}", [128, 512], F32)) for i in range(6)]
    pbf = [ctx.enter_context(nc.psum_tensor(f"pbf{i}", [128, 1024], BF16)) for i in range(2)]

    ident_bf = sb(ctx, "ident_bf", [128, 128], BF16)
    ident_f = sb(ctx, "ident_f", [128, 128], F32)
    mneg4 = sb(ctx, "mneg4", [128, 512], F32)
    onescol_bf = sb(ctx, "onescol_bf", [128, 1], BF16)
    cbias = sb(ctx, "cbias", [128, 4], F32)
    convw = sb(ctx, "convw", [128, 66 * KCONV], F32)
    dtb_col = sb(ctx, "dtb_col", [NH, 1], F32)
    nega_col = sb(ctx, "nega_col", [NH, 1], F32)
    dbc = sb(ctx, "dbc", [128, NH], F32)
    nwcols = sb(ctx, "nwcols", [128, NH], BF16)
    bfm = sb(ctx, "bfm", [128, SPAN], BF16)
    cfm = sb(ctx, "cfm", [128, SPAN], BF16)
    dtraw = sb(ctx, "dtraw", [NH, TT], F32)
    dt_sp = sb(ctx, "dt_sp", [NH, SPAN], F32)
    logdA = sb(ctx, "logdA", [NH, SPAN], F32)
    logtot = sb(ctx, "logtot", [NH, 1], F32)
    h_all = sb(ctx, "h_all", [128, NTILE * 24], F32)
    rstdf_all = sb(ctx, "rstdf_all", [128, NTILE], F32)
    nw_tmp = sb(ctx, "nw_tmp", [NH, 128], F32)

    P.emit("gpsimd", lambda g: g.memset(ident_bf[:], 0.0))
    P.emit("gpsimd", lambda g: g.affine_select(
        out=ident_bf[:], in_=ident_bf[:], compare_op=ALU.not_equal, fill=1.0,
        base=0, pattern=[[-1, 128]], channel_multiplier=1))
    P.emit("gpsimd", lambda g: g.memset(ident_f[:], 0.0))
    P.emit("gpsimd", lambda g: g.affine_select(
        out=ident_f[:], in_=ident_f[:], compare_op=ALU.not_equal, fill=1.0,
        base=0, pattern=[[-1, 128]], channel_multiplier=1))
    P.emit("gpsimd", lambda g: g.memset(mneg4[:], 0.0))
    for j in range(4):
        P.emit("gpsimd", lambda g, j=j: g.affine_select(
            out=mneg4[:, 128 * j:128 * (j + 1)], in_=mneg4[:, 128 * j:128 * (j + 1)],
            compare_op=ALU.is_ge, fill=NEG_BIG, base=0,
            pattern=[[1, 128]], channel_multiplier=-1))
    P.emit("gpsimd", lambda g: g.memset(cbias[:, 0:1], EPS_LN))
    P.emit("gpsimd", lambda g: g.memset(cbias[:, 1:2], EPS_RMS))
    P.emit("gpsimd", lambda g: g.memset(cbias[:, 2:3], 1.0))
    P.emit("gpsimd", lambda g: g.memset(cbias[:, 3:4], EPS_MAM))
    P.emit("gpsimd", lambda g: g.memset(onescol_bf[:], 1.0), inc=(s_gp, 1))
    gp_p0 = s_gp.n

    P.emit("sync", lambda e: e.dma_start(
        out=convw.ap().rearrange("p (cc k) -> p cc k", k=KCONV),
        in_=conv_w.rearrange("(cc p) k -> p cc k", p=128)), inc=(s_p0, 16), dma=True)
    P.emit("sync", lambda e: e.dma_start(out=dtb_col[:], in_=dt_bias), inc=(s_p0, 16), dma=True)
    P.emit("sync", lambda e: e.dma_start(out=nega_col[:], in_=a_log), inc=(s_p0, 16), dma=True)
    P.emit("sync", lambda e: e.dma_start(out=dbc[:], in_=d_par.broadcast_to((128, NH))), inc=(s_p0, 16), dma=True)
    P.emit("sync", lambda e: e.dma_start(out=nw_tmp[:], in_=nw_in), inc=(s_p0, 16), dma=True)
    P.wait("scalar", s_p0, s_p0.n)
    P.emit("scalar", lambda e: e.activation(nega_col[:], nega_col[:], AF.Exp), inc=(s_sc, 1))
    P.wait("vector", s_sc, s_sc.n)
    P.emit("vector", lambda v: v.tensor_scalar_mul(nega_col[:], nega_col[:], -1.0), inc=(s_v, 1))

    P.wait("tensor", s_p0, s_p0.n)
    P.wait("tensor", s_gp, gp_p0)
    P.emit("tensor", lambda t: t.transpose(pf[0].ap()[0:128, 0:64], nw_tmp[:], ident_f[0:64, 0:64]), inc=(s_te, 1))
    P.wait("vector", s_te, s_te.n)
    P.emit("vector", lambda v: v.tensor_copy(nwcols[:], pf[0][0:128, 0:64]), inc=(s_v, 1))
    v_nw = s_v.n

    cxP12 = contextlib.ExitStack()
    xnfm = sb(cxP12, "xnfm", [128, NDC * TT], BF16)
    wbr_sb = sb(cxP12, "wbr_sb", [128, 8 * 1024], BF16)
    wcatT = sb(cxP12, "wcatT", [128, NDC * 24], BF16)
    hb_bc = sb(cxP12, "hb_bc", [128, 24], F32)
    gcol = sb(cxP12, "gcol", [128, 1], F32)
    gicol = sb(cxP12, "gicol", [128, 16], F32)
    logit_raw = sb(cxP12, "logit_raw", [128, NTILE * 24], F32)
    wcat_bf = sb(cxP12, "wcat_bf", [24, HD], BF16)
    cx1 = contextlib.ExitStack()
    xt_f = [sb(cx1, f"xt_f{i}", [128, HD], F32) for i in range(2)]
    xnf_t = sb(cx1, "xnf_t", [128, HD], F32)
    sqt = sb(cx1, "sqt", [128, HD], F32)
    xn_bf = [sb(cx1, f"xn_bf{i}", [128, HD], BF16) for i in range(2)]
    stat = sb(cx1, "stat", [128, 8], F32)

    te_tr_done = []
    sp_xn = []
    v_x_used = []
    sc_ffn = []
    for i in range(NTILE + 1):
        buf = i % 2
        is_halo = (i == NTILE)
        rows = HALO if is_halo else 128
        row0 = 0 if is_halo else HALO + 128 * i
        tcol = 0 if is_halo else HALO + 128 * i
        if i >= 2:
            P.wait("sync", s_v, v_x_used[i - 2])
        P.emit("sync", lambda e, buf=buf, rows=rows, row0=row0: e.dma_start(
            out=xt_f[buf][0:rows, :], in_=xh[row0:row0 + rows, :]), inc=(s_x[buf], 16), dma=True)
        P.wait("vector", s_x[buf], s_x[buf].n)
        if i >= 1:
            P.wait("vector", s_sc, sc_ffn[i - 1])
        P.emit("vector", lambda v, buf=buf, rows=rows: v.reduce_sum(
            stat[0:rows, 0:1], xt_f[buf][0:rows, :], axis=AX))
        P.emit("vector", lambda v, rows=rows: v.tensor_scalar_mul(
            stat[0:rows, 1:2], stat[0:rows, 0:1], -1.0 / HD))
        P.emit("vector", lambda v, buf=buf, rows=rows: v.scalar_tensor_tensor(
            xnf_t[0:rows, :], xt_f[buf][0:rows, :], stat[0:rows, 1:2],
            xt_f[buf][0:rows, :], ALU.add, ALU.bypass), inc=(s_v, 1))
        v_x_used.append(s_v.n)
        P.wait("scalar", s_v, s_v.n)
        P.wait("scalar", s_gp, gp_p0)
        P.emit("scalar", lambda e, rows=rows: e.activation(
            sqt[0:rows, :], xnf_t[0:rows, :], AF.Square, accum_out=stat[0:rows, 2:3]))
        P.emit("scalar", lambda e, rows=rows: e.activation(
            stat[0:rows, 3:4], stat[0:rows, 2:3], AF.Ln, bias=cbias[0:rows, 0:1], scale=1.0 / HD))
        P.emit("scalar", lambda e, rows=rows: e.activation(
            stat[0:rows, 4:5], stat[0:rows, 3:4], AF.Exp, scale=-0.5), inc=(s_sc, 1))
        P.wait("vector", s_sc, s_sc.n)
        P.emit("vector", lambda v, rows=rows: v.scalar_tensor_tensor(
            xnf_t[0:rows, :], xnf_t[0:rows, :], stat[0:rows, 4:5],
            xnf_t[0:rows, :], ALU.mult, ALU.bypass), inc=(s_v, 1))
        v_xn32 = s_v.n
        if i >= 2:
            P.wait("vector", s_te, te_tr_done[i - 2])
            P.wait("vector", *sp_xn[i - 2])
        P.emit("vector", lambda v, buf=buf, rows=rows: v.tensor_copy(
            xn_bf[buf][0:rows, :], xnf_t[0:rows, :]), inc=(s_v, 1))
        v_xnbf = s_v.n
        if not is_halo:
            P.wait("scalar", s_v, v_xn32)
            P.emit("scalar", lambda e: e.activation(
                sqt[:, :], xnf_t[:, :], AF.Square, accum_out=stat[0:128, 5:6]))
            P.emit("scalar", lambda e: e.activation(
                stat[0:128, 6:7], stat[0:128, 5:6], AF.Ln, bias=cbias[:, 1:2], scale=1.0 / HD))
            P.emit("scalar", lambda e, i=i: e.activation(
                rstdf_all[:, i:i + 1], stat[0:128, 6:7], AF.Exp, scale=-0.5), inc=(s_sc, 1))
            P.wait("scalar", s_v, v_xnbf)
            P.emit("scalar", lambda e, buf=buf, i=i: e.dma_start(
                out=xn_scr[128 * i:128 * (i + 1), :], in_=xn_bf[buf][:, :]), inc=(s_spxn[buf], 16), dma=True)
        sc_ffn.append(s_sc.n)
        sp_xn.append((s_spxn[buf], s_spxn[buf].n))
        P.wait("tensor", s_v, v_xnbf)
        tr_v = {}
        for dc in range(NDC):
            pd = pbf[dc % 2]
            if dc >= 2:
                P.wait("tensor", s_v, tr_v[dc - 2])
            P.emit("tensor", lambda t, buf=buf, dc=dc, rows=rows, pd=pd: t.transpose(
                pd[0:128, 0:rows], xn_bf[buf][0:rows, 128 * dc:128 * (dc + 1)],
                ident_bf[0:rows, 0:rows]), inc=(s_te, 1))
            P.wait("vector", s_te, s_te.n)
            P.emit("vector", lambda v, dc=dc, tcol=tcol, rows=rows, pd=pd: v.tensor_copy(
                xnfm[:, TT * dc + tcol:TT * dc + tcol + rows], pd[0:128, 0:rows]), inc=(s_v, 1))
            tr_v[dc] = s_v.n
        te_tr_done.append(s_te.n)
    v_xnfm_all = s_v.n
    if debug:
        P.wait("sync", s_v, v_xnfm_all)
        P.emit("sync", lambda e: e.dma_start(out=dbg["dbg_xnfm"], in_=xnfm[:, :]), inc=(s_spg, 16), dma=True)
        P.wait("sync", s_spg, s_spg.n)
    P.barrier(s_bar)
    cx1.close()

    cx2 = contextlib.ExitStack()
    g_tmp = sb(cx2, "g_tmp", [1, 2], F32)
    gi_row = sb(cx2, "gi_row", [1, 17], F32)

    P.emit("sync", lambda e: e.dma_start(out=hb_bc[:], in_=h_bases.broadcast_to((128, 24))), inc=(s_ffn, 16), dma=True)
    P.emit("sync", lambda e: e.dma_start(out=g_tmp[0:1, 0:1], in_=gate), inc=(s_ffn, 16), dma=True)
    P.emit("gpsimd", lambda g: g.dma_start(out=wcat_bf[0:16, :], in_=w_res), inc=(s_wffn, 16), dma=True)
    P.emit("gpsimd", lambda g: g.dma_start(out=wcat_bf[16:20, :], in_=w_pre), inc=(s_wffn, 16), dma=True)
    P.emit("gpsimd", lambda g: g.dma_start(out=wcat_bf[20:24, :], in_=w_post), inc=(s_wffn, 16), dma=True)
    P.emit("gpsimd", lambda g: g.dma_start(
        out=wbr_sb.ap().rearrange("p (kc c) -> p kc c", kc=8),
        in_=w_br.rearrange("(kc p) c -> p kc c", p=128)), inc=(s_wffn, 16), dma=True)
    P.wait("tensor", s_wffn, s_wffn.n)
    tr_v = {}
    for dc in range(NDC):
        pd = pbf[dc % 2]
        if dc >= 2:
            P.wait("tensor", s_v, tr_v[dc - 2])
        P.emit("tensor", lambda t, dc=dc, pd=pd: t.transpose(
            pd[0:128, 0:24], wcat_bf[:, 128 * dc:128 * (dc + 1)], ident_bf[0:24, 0:24]), inc=(s_te, 1))
        P.wait("vector", s_te, s_te.n)
        P.emit("vector", lambda v, dc=dc, pd=pd: v.tensor_copy(
            wcatT[:, 24 * dc:24 * (dc + 1)], pd[0:128, 0:24]), inc=(s_v, 1))
        tr_v[dc] = s_v.n
    v_wcatT = s_v.n

    P.wait("scalar", s_ffn, s_ffn.n)
    P.emit("scalar", lambda e: e.activation(g_tmp[0:1, 0:1], g_tmp[0:1, 0:1], AF.Sigmoid), inc=(s_sc, 1))
    P.wait("vector", s_sc, s_sc.n)
    P.emit("vector", lambda v: v.memset(gi_row[:], 0.0))
    P.emit("vector", lambda v: v.tensor_scalar(
        g_tmp[0:1, 1:2], g_tmp[0:1, 0:1], -1.0, 1.0, ALU.mult, ALU.add))
    for k in range(4):
        P.emit("vector", lambda v, k=k: v.tensor_copy(gi_row[0:1, 5 * k:5 * k + 1], g_tmp[0:1, 1:2]))
    P.emit("vector", lambda v: v.tensor_copy(gi_row[0:1, 16:17], g_tmp[0:1, 0:1]), inc=(s_v, 1))
    P.wait("scalar", s_v, s_v.n)
    P.emit("scalar", lambda e: e.dma_start(out=small_scr[0:1, 0:17], in_=gi_row[:]), inc=(s_sm, 16), dma=True)
    P.wait("sync", s_sm, s_sm.n)
    P.emit("sync", lambda e: e.dma_start(out=gicol[:], in_=small_scr[0:1, 0:16].broadcast_to((128, 16))), inc=(s_gb, 16), dma=True)
    P.emit("sync", lambda e: e.dma_start(out=gcol[:], in_=small_scr[0:1, 16:17].broadcast_to((128, 1))), inc=(s_gb, 16), dma=True)

    P.wait("tensor", s_v, v_wcatT)
    for i in range(NTILE):
        if i >= 1:
            P.wait("tensor", s_v, s_v.n)
        for dc in range(NDC):
            P.emit("tensor", lambda t, i=i, dc=dc: t.matmul(
                pf[4].ap()[0:128, 0:24],
                xnfm[:, TT * dc + HALO + 128 * i: TT * dc + HALO + 128 * (i + 1)],
                wcatT[:, 24 * dc:24 * (dc + 1)],
                start=(dc == 0), stop=(dc == NDC - 1)),
                inc=(s_te, 1) if dc == NDC - 1 else None)
        P.wait("vector", s_te, s_te.n)
        P.wait("vector", s_gb, s_gb.n)
        P.emit("vector", lambda v, i=i: v.scalar_tensor_tensor(
            logit_raw[:, 24 * i:24 * (i + 1)], pf[4][0:128, 0:24], rstdf_all[:, i:i + 1],
            hb_bc[:, :], ALU.mult, ALU.add))
        P.emit("vector", lambda v, i=i: v.tensor_scalar_mul(
            logit_raw[:, 24 * i:24 * i + 16], logit_raw[:, 24 * i:24 * i + 16], 10.0), inc=(s_v, 1))
    v_logits = s_v.n

    cx3 = contextlib.ExitStack()
    zv_b = sb(cx3, "zv_b", [128, 128], F32)
    u_b = sb(cx3, "u_b", [128, 32], F32)
    v_b = sb(cx3, "v_b", [128, 32], F32)
    m_b = sb(cx3, "m_b", [128, 32], F32)
    sst_b = sb(cx3, "sst_b", [128, 32], F32)

    Zr = logit_raw.ap().rearrange("p (t k) -> p t k", k=24)[:, :, 0:16].rearrange(
        "p t (i j) -> p t i j", j=4)
    zv4 = zv_b.ap().rearrange("p (t i j) -> p t i j", i=4, j=4)
    zv4T = zv_b.ap().rearrange("p (t i j) -> p t j i", i=4, j=4)
    uv = u_b.ap().rearrange("p (t i) -> p t i", i=4)
    vv = v_b.ap().rearrange("p (t j) -> p t j", j=4)
    mv = m_b.ap().rearrange("p (t k) -> p t k", k=4)
    sv = sst_b.ap().rearrange("p (t k) -> p t k", k=4)

    P.wait("vector", s_v, v_logits)
    P.emit("vector", lambda v: v.memset(u_b[:], 0.0))
    P.emit("vector", lambda v: v.memset(v_b[:], 0.0))
    for it in range(50):
        for i in range(4):
            P.emit("vector", lambda v, i=i: v.tensor_add(zv4[:, :, i, :], Zr[:, :, i, :], vv[:, :, :]),
                   chain=('start' if i == 0 else ('end' if i == 3 else 'mid')))
        P.emit("vector", lambda v: v.reduce_max(mv[:, :, :], zv4[:, :, :, :], axis=AX))
        for j in range(4):
            P.emit("vector", lambda v, j=j: v.tensor_sub(zv4[:, :, :, j], zv4[:, :, :, j], mv[:, :, :]),
                   chain=('start' if j == 0 else ('end' if j == 3 else 'mid')))
        P.wait("scalar", s_v, s_v.n)
        P.emit("scalar", lambda e: e.activation(zv_b[:], zv_b[:], AF.Exp), inc=(s_sc, 1))
        P.wait("vector", s_sc, s_sc.n)
        P.emit("vector", lambda v: v.reduce_sum(sv[:, :, :], zv4[:, :, :, :], axis=AX), inc=(s_v, 1))
        P.wait("scalar", s_v, s_v.n)
        P.emit("scalar", lambda e: e.activation(sst_b[:], sst_b[:], AF.Ln), inc=(s_sc, 1))
        P.wait("vector", s_sc, s_sc.n)
        P.emit("vector", lambda v: v.scalar_tensor_tensor(
            u_b[:], sst_b[:], -1.0, m_b[:], ALU.mult, ALU.subtract))
        for j in range(4):
            P.emit("vector", lambda v, j=j: v.tensor_add(zv4[:, :, :, j], Zr[:, :, :, j], uv[:, :, :]),
                   chain=('start' if j == 0 else ('end' if j == 3 else 'mid')))
        P.emit("vector", lambda v: v.reduce_max(mv[:, :, :], zv4T[:, :, :, :], axis=AX))
        for i in range(4):
            P.emit("vector", lambda v, i=i: v.tensor_sub(zv4[:, :, i, :], zv4[:, :, i, :], mv[:, :, :]),
                   chain=('start' if i == 0 else ('end' if i == 3 else 'mid')))
        P.wait("scalar", s_v, s_v.n)
        P.emit("scalar", lambda e: e.activation(zv_b[:], zv_b[:], AF.Exp), inc=(s_sc, 1))
        P.wait("vector", s_sc, s_sc.n)
        P.emit("vector", lambda v: v.reduce_sum(sv[:, :, :], zv4T[:, :, :, :], axis=AX), inc=(s_v, 1))
        P.wait("scalar", s_v, s_v.n)
        P.emit("scalar", lambda e: e.activation(sst_b[:], sst_b[:], AF.Ln), inc=(s_sc, 1))
        P.wait("vector", s_sc, s_sc.n)
        P.emit("vector", lambda v: v.scalar_tensor_tensor(
            v_b[:], sst_b[:], -1.0, m_b[:], ALU.mult, ALU.subtract))
    for i in range(4):
        P.emit("vector", lambda v, i=i: v.tensor_add(zv4[:, :, i, :], Zr[:, :, i, :], vv[:, :, :]),
               chain=('start' if i == 0 else ('end' if i == 3 else 'mid')))
    for i in range(4):
        for j in range(4):
            P.emit("vector", lambda v, i=i, j=j: v.tensor_add(
                zv4[:, :, i, j:j + 1], zv4[:, :, i, j:j + 1], uv[:, :, i:i + 1]),
                chain=('start' if (i == 0 and j == 0) else ('end' if (i == 3 and j == 3) else 'mid')))
    P.emit("vector", lambda v: v.tensor_copy(m_b[:], m_b[:]), inc=(s_v, 1))
    P.wait("scalar", s_v, s_v.n)
    P.emit("scalar", lambda e: e.activation(zv_b[:], zv_b[:], AF.Exp), inc=(s_sc, 1))
    P.wait("vector", s_sc, s_sc.n)
    P.wait("vector", s_gb, s_gb.n)
    for i in range(NTILE):
        P.emit("vector", lambda v, i=i: v.scalar_tensor_tensor(
            h_all[:, 24 * i:24 * i + 16], zv_b[:, 16 * i:16 * (i + 1)],
            gcol[:, 0:1], gicol[:, :], ALU.mult, ALU.add),
            chain=('start' if i == 0 else ('end' if i == NTILE - 1 else 'mid')))
    mm8 = m_b.ap()[0:128, 0:8].rearrange("p (t o) -> p t o", o=1)
    ss8 = sst_b.ap()[0:128, 0:8].rearrange("p (t o) -> p t o", o=1)
    for base in (16, 20):
        pl = logit_raw.ap().rearrange("p (t k) -> p t k", k=24)[:, :, base:base + 4]
        hv = h_all.ap().rearrange("p (t k) -> p t k", k=24)[:, :, base:base + 4]
        tmp = zv_b.ap()[0:128, 0:32].rearrange("p (t j) -> p t j", j=4)
        P.emit("vector", lambda v, pl=pl: v.reduce_max(mm8[:, :, :], pl[:, :, :], axis=AX))
        for j in range(4):
            P.emit("vector", lambda v, pl=pl, tmp=tmp, j=j: v.tensor_sub(
                tmp[:, :, j:j + 1], pl[:, :, j:j + 1], mm8[:, :, :]), inc=(s_v, 1) if j == 3 else None)
        P.wait("scalar", s_v, s_v.n)
        P.emit("scalar", lambda e: e.activation(zv_b[0:128, 0:32], zv_b[0:128, 0:32], AF.Exp), inc=(s_sc, 1))
        P.wait("vector", s_sc, s_sc.n)
        P.emit("vector", lambda v, tmp=tmp: v.reduce_sum(ss8[:, :, :], tmp[:, :, :], axis=AX))
        P.emit("vector", lambda v: v.reciprocal(sst_b[0:128, 0:8], sst_b[0:128, 0:8]))
        for j in range(4):
            P.emit("vector", lambda v, hv=hv, tmp=tmp, j=j: v.tensor_mul(
                hv[:, :, j:j + 1], tmp[:, :, j:j + 1], ss8[:, :, :]))
    P.emit("vector", lambda v: v.tensor_copy(m_b[:], m_b[:]), inc=(s_v, 1))
    v_hall = s_v.n
    if debug:
        P.wait("scalar", s_v, v_hall)
        P.emit("scalar", lambda e: e.dma_start(out=dbg["dbg_hall"], in_=h_all[:]), inc=(s_spg, 16), dma=True)

    cxW = contextlib.ExitStack()
    wt_bf = [sb(cxW, f"wt_bf{i}", [128, NDC * 128], BF16) for i in range(2)]
    xbc_bf = [sb(cxW, f"xbc_bf{i}", [128, TT], BF16) for i in range(2)]
    cvb = [sb(cxW, f"cvb{i}", [128, SPAN], BF16) for i in range(2)]
    xout = [sb(cxW, f"xout{i}", [128, SPAN], BF16) for i in range(2)]
    zeb = [sb(cxW, f"zeb{i}", [128, 1024], BF16) for i in range(2)]
    xnm_bf = [sb(cxW, f"xnm_bf{i}", [128, HD], BF16) for i in range(2)]
    xpre_tm = sb(cxW, "xpre_tm", [128, 1024], BF16)
    xpre_fm = sb(cxW, "xpre_fm", [128, 8 * 128], BF16)
    ybr_bf = sb(cxW, "ybr_bf", [128, 1024], BF16)
    ffn_acc = sb(cxW, "ffn_acc", [128, HD], BF16)
    xrl_f = [sb(cxW, f"xrl_f{i}", [128, HD], F32) for i in range(2)]

    p2_ccs = list(range(64, 131)) + list(range(0, 64))
    idx_dt = p2_ccs.index(130)

    w_loaded = {}
    te_chunk = {}
    v_chunk = {}
    sc_chunk = {}
    sp_chunk = {}
    mix_done = {}
    mix_st = {}
    mix_te = {}
    mix_ybr = {}

    def emit_wload(idx):
        cc = p2_ccs[idx]
        wbuf = idx % 2
        width = 64 if cc == 130 else 128
        c0 = 128 * cc
        if idx >= 2:
            P.wait("gpsimd", s_te, te_chunk[idx - 2])
        for gdc in range(4):
            P.emit("gpsimd", lambda g, wbuf=wbuf, gdc=gdc, c0=c0, width=width: g.dma_start(
                out=wt_bf[wbuf].ap().rearrange("p (dc c) -> p dc c", c=128)[:, 8 * gdc:8 * (gdc + 1), 0:width],
                in_=w_in.rearrange("(dc p) c -> p dc c", p=128)[:, 8 * gdc:8 * (gdc + 1), c0:c0 + width]),
                inc=(s_w[wbuf], 16), dma=True)
        w_loaded[idx] = (s_w[wbuf], s_w[wbuf].n)

    emit_wload(0)
    emit_wload(1)

    def plan_mix_tile(i):
        mb = i % 2
        if i >= 2:
            P.wait("sync", s_v, mix_done[i - 2])
            P.wait("sync", *mix_st[i - 2])
        P.wait("sync", *sp_xn[i])
        P.emit("sync", lambda e, mb=mb, i=i: e.dma_start(
            out=xnm_bf[mb][:, :], in_=xn_scr[128 * i:128 * (i + 1), :]), inc=(s_mx[mb], 16), dma=True)
        P.emit("sync", lambda e, mb=mb, i=i: e.dma_start(
            out=xrl_f[mb][:, :], in_=xh[HALO + 128 * i:HALO + 128 * (i + 1), :]), inc=(s_mx[mb], 16), dma=True)
        P.wait("vector", s_mx[mb], s_mx[mb].n)
        P.wait("vector", s_v, v_hall)
        if i >= 1:
            P.wait("vector", s_te, mix_te[i - 1])
        for n in range(4):
            P.emit("vector", lambda v, mb=mb, i=i, n=n: v.scalar_tensor_tensor(
                xpre_tm[:, :], xnm_bf[mb][:, 1024 * n:1024 * (n + 1)],
                h_all[:, 24 * i + 16 + n:24 * i + 17 + n],
                xpre_tm[:, :] if n else xnm_bf[mb][:, 0:1024],
                ALU.mult, ALU.add if n else ALU.bypass), inc=(s_v, 1) if n == 3 else None)
        v_xpre = s_v.n
        P.wait("tensor", s_v, v_xpre)
        P.wait("tensor", s_v, v_chunk[idx_dt])
        P.wait("tensor", s_v, mix_ybr.get(i - 1, 0))
        tr_v = {}
        for kc in range(8):
            pd = pbf[kc % 2]
            if kc >= 2:
                P.wait("tensor", s_v, tr_v[kc - 2])
            P.emit("tensor", lambda t, kc=kc, pd=pd: t.transpose(
                pd[0:128, 0:128], xpre_tm[:, 128 * kc:128 * (kc + 1)], ident_bf[:, :]), inc=(s_te, 1))
            P.wait("vector", s_te, s_te.n)
            P.emit("vector", lambda v, kc=kc, pd=pd: v.tensor_copy(
                xpre_fm[:, 128 * kc:128 * (kc + 1)], pd[0:128, 0:128]), inc=(s_v, 1))
            tr_v[kc] = s_v.n
        v_xprefm = s_v.n
        P.wait("tensor", s_v, v_xprefm)
        for half in range(2):
            for kc in range(8):
                P.emit("tensor", lambda t, half=half, kc=kc: t.matmul(
                    pf[4 + half].ap()[0:128, 0:512], xpre_fm[:, 128 * kc:128 * (kc + 1)],
                    wbr_sb[:, 1024 * kc + 512 * half:1024 * kc + 512 * (half + 1)],
                    start=(kc == 0), stop=(kc == 7)), inc=(s_te, 1) if kc == 7 else None)
        mix_te[i] = s_te.n
        P.wait("vector", s_te, mix_te[i])
        for half in range(2):
            P.emit("vector", lambda v, half=half: v.tensor_copy(
                ybr_bf[:, 512 * half:512 * (half + 1)], pf[4 + half][0:128, 0:512]), inc=(s_v, 1))
        mix_ybr[i] = s_v.n
        for jb in range(4):
            for ib in range(4):
                P.emit("vector", lambda v, mb=mb, i=i, ib=ib, jb=jb: v.scalar_tensor_tensor(
                    ffn_acc[:, 1024 * ib:1024 * (ib + 1)],
                    xnm_bf[mb][:, 1024 * jb:1024 * (jb + 1)],
                    h_all[:, 24 * i + 4 * ib + jb:24 * i + 4 * ib + jb + 1],
                    ffn_acc[:, 1024 * ib:1024 * (ib + 1)] if jb else xnm_bf[mb][:, 0:1024],
                    ALU.mult, ALU.add if jb else ALU.bypass),
                    chain=('start' if ib == 0 else ('end' if ib == 3 else 'mid')))
        for ib in range(4):
            P.emit("vector", lambda v, i=i, ib=ib: v.scalar_tensor_tensor(
                ffn_acc[:, 1024 * ib:1024 * (ib + 1)], ybr_bf[:, :],
                h_all[:, 24 * i + 20 + ib:24 * i + 21 + ib],
                ffn_acc[:, 1024 * ib:1024 * (ib + 1)], ALU.mult, ALU.add),
                chain=('start' if ib == 0 else ('end' if ib == 3 else 'mid')))
        P.emit("vector", lambda v, mb=mb: v.tensor_add(xrl_f[mb][:, :], xrl_f[mb][:, :], ffn_acc[:, :]), inc=(s_v, 1))
        mix_done[i] = s_v.n
        P.wait("sync", s_v, mix_done[i])
        P.emit("sync", lambda e, mb=mb, i=i: e.dma_start(
            out=out[128 * i:128 * (i + 1), :], in_=xrl_f[mb][:, :]), inc=(s_st[mb], 16), dma=True)
        mix_st[i] = (s_st[mb], s_st[mb].n)

    P.wait("tensor", s_v, v_xnfm_all)
    for idx, cc in enumerate(p2_ccs):
        pb = idx % 2
        is_dt = (cc == 130)
        is_z = (cc < 64)
        is_x = (64 <= cc < 128)
        width = 64 if is_dt else 128
        tiles = [(HALO, 512), (HALO + 512, 512)] if is_z else [(0, 512), (512, 512), (1024, HALO)]
        P.wait("tensor", *w_loaded[idx])
        if idx >= 2:
            P.wait("tensor", s_v, v_chunk[idx - 2])
        else:
            P.wait("tensor", s_v, v_logits)
        for tb, (t0, tw) in enumerate(tiles):
            pt = pf[2 * pb + tb] if tb < 2 else pf[4 + pb]
            for dc in range(NDC):
                P.emit("tensor", lambda t, pb=pb, dc=dc, t0=t0, tw=tw, pt=pt, width=width: t.matmul(
                    pt.ap()[0:width, 0:tw], wt_bf[pb][:, 128 * dc:128 * dc + width],
                    xnfm[:, TT * dc + t0:TT * dc + t0 + tw],
                    start=(dc == 0), stop=(dc == NDC - 1)),
                    inc=(s_te, 1) if dc == NDC - 1 else None)
        te_chunk[idx] = s_te.n
        if idx + 2 < len(p2_ccs):
            emit_wload(idx + 2)
        P.wait("vector", s_te, te_chunk[idx])
        if is_dt:
            for tb, (t0, tw) in enumerate(tiles):
                pt = pf[2 * pb + tb] if tb < 2 else pf[4 + pb]
                P.emit("vector", lambda v, t0=t0, tw=tw, pt=pt: v.tensor_copy(
                    dtraw[:, t0:t0 + tw], pt[0:64, 0:tw]), inc=(s_v, 1))
            v_chunk[idx] = s_v.n
            sc_chunk[idx] = s_sc.n
            sp_chunk[idx] = (s_spg, s_spg.n)
        elif is_z:
            if idx >= 2:
                P.wait("vector", *sp_chunk[idx - 2])
            for tb, (t0, tw) in enumerate(tiles):
                pt = pf[2 * pb + tb]
                P.emit("vector", lambda v, pb=pb, tb=tb, pt=pt: v.tensor_copy(
                    zeb[pb][:, 512 * tb:512 * (tb + 1)], pt[0:128, 0:512]),
                    chain=('start' if tb == 0 else 'end'))
            v_chunk[idx] = s_v.n
            sc_chunk[idx] = s_sc.n
            P.wait("scalar", s_v, v_chunk[idx])
            P.emit("scalar", lambda e, pb=pb, cc=cc: e.dma_start(
                out=z_scr[128 * cc:128 * (cc + 1), :], in_=zeb[pb][:, :]), inc=(s_z[pb], 16), dma=True)
            sp_chunk[idx] = (s_z[pb], s_z[pb].n)
        else:
            if idx >= 2:
                P.wait("vector", s_sc, sc_chunk[idx - 2])
            for tb, (t0, tw) in enumerate(tiles):
                pt = pf[2 * pb + tb] if tb < 2 else pf[4 + pb]
                P.emit("vector", lambda v, pb=pb, t0=t0, tw=tw, pt=pt: v.tensor_copy(
                    xbc_bf[pb][:, t0:t0 + tw], pt[0:128, 0:tw]),
                    chain=('start' if tb == 0 else ('end' if tb == 2 else 'mid')))
            ci = cc - 64
            for k in range(KCONV):
                P.emit("vector", lambda v, pb=pb, ci=ci, k=k: v.scalar_tensor_tensor(
                    cvb[pb][:, :], xbc_bf[pb][:, k:k + SPAN], convw[:, 4 * ci + k:4 * ci + k + 1],
                    cvb[pb][:, :] if k else xbc_bf[pb][:, 0:SPAN],
                    ALU.mult, ALU.add if k else ALU.bypass), inc=(s_v, 1) if k == 3 else None)
            v_chunk[idx] = s_v.n
            P.wait("scalar", s_v, v_chunk[idx])
            if idx >= 2:
                P.wait("scalar", *sp_chunk[idx - 2])
            P.emit("scalar", lambda e, pb=pb: e.activation(
                xout[pb][:, :], cvb[pb][:, :], AF.Silu), inc=(s_sc, 1))
            sc_chunk[idx] = s_sc.n
            if is_x:
                P.emit("scalar", lambda e, pb=pb, ci=ci: e.dma_start(
                    out=x_scr[128 * ci:128 * (ci + 1), :], in_=xout[pb][:, :]), inc=(s_xo[pb], 16), dma=True)
                sp_chunk[idx] = (s_xo[pb], s_xo[pb].n)
            else:
                dst = bfm if cc == 128 else cfm
                P.wait("vector", s_sc, sc_chunk[idx])
                P.emit("vector", lambda v, pb=pb, dst=dst: v.tensor_copy(dst[:, :], xout[pb][:, :]), inc=(s_v, 1))
                v_chunk[idx] = s_v.n
                sp_chunk[idx] = (s_spg, s_spg.n)
        if is_z and (cc % 8 == 7):
            plan_mix_tile(cc // 8)

    P.wait("vector", s_v, v_chunk[idx_dt])
    P.wait("vector", s_p0, s_p0.n)
    P.emit("vector", lambda v: v.scalar_tensor_tensor(
        dt_sp[:, :], dtraw[:, HALO:], dtb_col[:, 0:1], dtraw[:, HALO:], ALU.add, ALU.bypass), inc=(s_v, 1))
    P.wait("scalar", s_v, s_v.n)
    P.emit("scalar", lambda e: e.activation(logdA[:, :], dt_sp[:, :], AF.Abs))
    P.emit("scalar", lambda e: e.activation(logdA[:, :], logdA[:, :], AF.Exp, scale=-1.0))
    P.emit("scalar", lambda e: e.activation(logdA[:, :], logdA[:, :], AF.Ln, bias=cbias[0:NH, 2:3]), inc=(s_sc, 1))
    P.wait("vector", s_sc, s_sc.n)
    P.emit("vector", lambda v: v.tensor_relu(dt_sp[:, :], dt_sp[:, :]))
    P.emit("vector", lambda v: v.tensor_add(dt_sp[:, :], dt_sp[:, :], logdA[:, :]))
    P.emit("vector", lambda v: v.scalar_tensor_tensor(
        logdA[:, :], dt_sp[:, :], nega_col[:, 0:1], dt_sp[:, :], ALU.mult, ALU.bypass), inc=(s_v, 1))
    v_dtprep = s_v.n
    if debug:
        P.wait("scalar", s_v, v_dtprep)
        P.emit("scalar", lambda e: e.dma_start(out=dbg["dbg_dt"], in_=dtraw[:]), inc=(s_spg, 16), dma=True)

    if debug:
        P.wait("sync", s_v, s_v.n)
        P.emit("sync", lambda e: e.dma_start(out=dbg["dbg_bc"].rearrange("p (a t) -> p a t", a=2)[:, 0:1, :], in_=bfm[:, :].unsqueeze(1)), inc=(s_spg, 16), dma=True)
        P.emit("sync", lambda e: e.dma_start(out=dbg["dbg_bc"].rearrange("p (a t) -> p a t", a=2)[:, 1:2, :], in_=cfm[:, :].unsqueeze(1)), inc=(s_spg, 16), dma=True)
        P.wait("sync", s_spg, s_spg.n)
    P.barrier(s_bar)
    cxW.close()
    cx3.close()
    cx2.close()
    cxP12.close()

    cxH = contextlib.ExitStack()
    ab_t = [sb(cxH, f"ab{c}", [128, NH], F32) for c in range(NCHUNK)]
    h_f32 = sb(cxH, "h_f32", [128, NH * 128], F32)
    h_bf = sb(cxH, "h_bf", [128, NH * 128], BF16)
    cx3a = contextlib.ExitStack()
    Xc = [sb(cx3a, f"Xc{i}", [128, NH * 128], BF16) for i in range(2)]
    XTc = [sb(cx3a, f"XTc{i}", [128, NH * 128], BF16) for i in range(2)]
    rt_b = [sb(cx3a, f"rt_b{i}", [128, 512], F32) for i in range(2)]
    rtm_b = [sb(cx3a, f"rtm_b{i}", [128, 512], F32) for i in range(2)]
    edec_b = [sb(cx3a, f"edec_b{i}", [128, 512], BF16) for i in range(2)]
    l4_b = [sb(cx3a, f"l4_b{i}", [128, 512], BF16) for i in range(2)]
    cdec_b = [sb(cx3a, f"cdec_b{i}", [128, 512], BF16) for i in range(2)]
    m4_b = [sb(cx3a, f"m4_b{i}", [128, 512], BF16) for i in range(2)]
    crep = sb(cx3a, "crep", [128, 512], BF16)
    grep = sb(cx3a, "grep", [128, 512], BF16)
    y4_b = [sb(cx3a, f"y4_b{i}", [128, 512], BF16) for i in range(2)]
    gm_b = sb(cx3a, "gm_b", [128, 128], BF16)
    bt_b = sb(cx3a, "bt_b", [128, 128], BF16)
    dtxT = [sb(cx3a, f"dtxT{i}", [128, 512], BF16) for i in range(2)]
    bdT = [sb(cx3a, f"bdT{i}", [128, 512], BF16) for i in range(2)]
    stk = sb(cx3a, "stk", [128, 128], F32)
    trc = sb(cx3a, "trc", [128, 128], F32)
    dtT_bf = sb(cx3a, "dtT_bf", [128, NH], BF16)
    wT_f = sb(cx3a, "wT_f", [128, NH], F32)
    wT_bf = sb(cx3a, "wT_bf", [128, NH], BF16)
    cs_hm = sb(cx3a, "cs_hm", [NH, Q], F32)
    w_hm = sb(cx3a, "w_hm", [NH, Q], F32)

    P.emit("vector", lambda v: v.memset(h_f32[:], 0.0))
    P.emit("vector", lambda v: v.memset(h_bf[:], 0.0))
    P.emit("vector", lambda v: v.memset(logtot[:], 0.0), inc=(s_v, 1))

    def slab_load(dst, src, c, sem):
        for g8 in range(8):
            P.emit("sync", lambda e, dst=dst, src=src, c=c, g8=g8: e.dma_start(
                out=dst.ap().rearrange("p (hh t) -> p hh t", t=Q)[:, 8 * g8:8 * (g8 + 1), :],
                in_=src.rearrange("(hh p) t -> p hh t", p=128)[:, 8 * g8:8 * (g8 + 1), Q * c:Q * (c + 1)]),
                inc=(sem, 16), dma=True)

    def slab_T_load(dst, src, c, sem):
        for g8 in range(8):
            P.emit("sync", lambda e, dst=dst, src=src, c=c, g8=g8: e.dma_start_transpose(
                out=dst[:, 1024 * g8:1024 * (g8 + 1)],
                in_=src[1024 * g8:1024 * (g8 + 1), Q * c:Q * (c + 1)]).then_inc(sem.h, 16))
            sem.n += 16

    for s_ in (s_xo[0], s_xo[1], s_spxn[0], s_spxn[1], s_spg):
        P.wait("sync", s_, s_.n)
    slab_load(Xc[0], x_scr, 0, s_sl[0])
    slab_T_load(XTc[0], x_scr, 0, s_sl[0])
    slab_ld = {0: (s_sl[0], s_sl[0].n)}
    last_grp_v = {}
    last_grp_sp = {}
    last_grp_te = {}
    last_grp_sc = {}
    last_rt_sc = {}
    sc_ab_hist = {}
    last_cs_ld = {0: [], 1: []}
    prev_sp_cs = (s_spg, 0)

    for c in range(NCHUNK):
        pb = c % 2
        if c + 1 < NCHUNK:
            if c >= 1:
                P.wait("sync", s_v, last_grp_v.get((c - 1, 15), 0))
            slab_load(Xc[(c + 1) % 2], x_scr, c + 1, s_sl[(c + 1) % 2])
            slab_T_load(XTc[(c + 1) % 2], x_scr, c + 1, s_sl[(c + 1) % 2])
            slab_ld[c + 1] = (s_sl[(c + 1) % 2], s_sl[(c + 1) % 2].n)
        P.wait("vector", s_v, v_dtprep)
        if c >= 1:
            P.wait("vector", s_te, last_grp_te.get((c - 1, 15), 0))
            P.wait("vector", s_sc, last_grp_sc.get((c - 1, 15), 0))
            P.wait("vector", *prev_sp_cs)
        P.emit("vector", lambda v, c=c: v.tensor_tensor_scan(
            cs_hm[:, :], logdA[:, Q * c:Q * (c + 1)], logdA[:, Q * c:Q * (c + 1)],
            0.0, ALU.add, ALU.bypass))
        P.emit("vector", lambda v: v.tensor_add(logtot[:, :], logtot[:, :], cs_hm[:, Q - 1:Q]))
        P.emit("vector", lambda v, c=c: v.tensor_copy(stk[0:64, :], dt_sp[:, Q * c:Q * (c + 1)]))
        P.emit("vector", lambda v: v.tensor_copy(stk[64:128, :], cs_hm[:, :]), inc=(s_v, 1))
        v_stk = s_v.n
        P.wait("scalar", s_v, v_stk)
        for s_, v_ in last_cs_ld[pb]:
            P.wait("scalar", s_, v_)
        P.emit("scalar", lambda e, c=c: e.dma_start(out=cs_scr[c % 2], in_=cs_hm[:, :]), inc=(s_spcs[pb], 16), dma=True)
        P.emit("scalar", lambda e, c=c: e.dma_start(out=ce_scr[c % 2], in_=cs_hm[:, Q - 1:Q]), inc=(s_spcs[pb], 16), dma=True)
        sp_cs = (s_spcs[pb], s_spcs[pb].n)
        prev_sp_cs = sp_cs
        P.emit("scalar", lambda e: e.activation(
            w_hm[:, :], cs_hm[:, :], AF.Exp, bias=cs_hm[:, Q - 1:Q], scale=-1.0), inc=(s_sc, 1))
        sc_whm = s_sc.n
        P.wait("tensor", s_v, v_stk)
        P.emit("tensor", lambda t: t.transpose(pf[4].ap()[0:128, 0:128], stk[:, :], ident_f[:, :]), inc=(s_te, 1))
        P.wait("vector", s_te, s_te.n)
        P.emit("vector", lambda v: v.tensor_copy(trc[:, :], pf[4][0:128, 0:128]))
        P.emit("vector", lambda v: v.tensor_copy(dtT_bf[:, :], trc[:, 0:64]))
        P.emit("vector", lambda v: v.tensor_scalar_mul(trc[:, 64:128], trc[:, 64:128], -1.0), inc=(s_v, 1))
        v_trc = s_v.n
        P.wait("tensor", s_sc, sc_whm)
        P.emit("tensor", lambda t: t.transpose(pf[5].ap()[0:128, 0:64], w_hm[:, :], ident_f[0:64, 0:64]), inc=(s_te, 1))
        P.wait("vector", s_te, s_te.n)
        P.emit("vector", lambda v: v.tensor_copy(wT_f[:, :], pf[5][0:128, 0:64]))
        P.emit("vector", lambda v: v.tensor_copy(wT_bf[:, :], wT_f[:, :]), inc=(s_v, 1))
        v_trans = s_v.n
        P.wait("sync", *sp_cs)
        if c >= 2:
            P.wait("sync", s_sc, sc_ab_hist[c - 2])
        P.emit("sync", lambda e, c=c: e.dma_start(
            out=ab_t[c][:, :],
            in_=ce_scr[c % 2].rearrange("h o -> o h").broadcast_to((128, NH))), inc=(s_ab[pb], 16), dma=True)
        P.wait("scalar", s_ab[pb], s_ab[pb].n)
        P.emit("scalar", lambda e, c=c: e.activation(ab_t[c][:, :], ab_t[c][:, :], AF.Exp), inc=(s_sc, 1))
        sc_ab = s_sc.n
        sc_ab_hist[c] = sc_ab
        P.wait("tensor", s_v, v_trc)
        P.emit("tensor", lambda t, c=c: t.transpose(
            pbf[0][0:128, 0:128], bfm[:, Q * c:Q * (c + 1)], ident_bf[:, :]), inc=(s_te, 1))
        P.emit("tensor", lambda t, c=c: t.matmul(
            pf[4].ap()[0:128, 0:128], bfm[:, Q * c:Q * (c + 1)], cfm[:, Q * c:Q * (c + 1)],
            start=True, stop=True), inc=(s_te, 1))
        P.wait("vector", s_te, s_te.n)
        P.emit("vector", lambda v: v.tensor_copy(bt_b[:, :], pbf[0][0:128, 0:128]))
        P.emit("vector", lambda v: v.tensor_copy(gm_b[:, :], pf[4][0:128, 0:128]))
        for j in range(4):
            P.emit("vector", lambda v, c=c, j=j: v.tensor_copy(crep[:, 128 * j:128 * (j + 1)], cfm[:, Q * c:Q * (c + 1)]))
        for j in range(4):
            P.emit("vector", lambda v, j=j: v.tensor_copy(grep[:, 128 * j:128 * (j + 1)], gm_b[:, :]),
                   inc=(s_v, 1) if j == 3 else None)
        v_btg = s_v.n
        rt_ld = {}
        for g in range(2):
            P.wait("sync", *sp_cs)
            P.wait("sync", s_sc, last_rt_sc.get(g % 2, 0))
            P.emit("sync", lambda e, c=c, g=g: e.dma_start(
                out=rt_b[g % 2][:, :],
                in_=cs_scr[c % 2].rearrange("h t -> (h t)").unsqueeze(0)[0:1, 512 * g:512 * (g + 1)].broadcast_to((128, 512))),
                inc=(s_rt[g % 2], 16), dma=True)
            rt_ld[g] = (s_rt[g % 2], s_rt[g % 2].n)

        def stage_E(g):
            gb = g % 2
            py = pf[0 + gb]
            pS = pf[2 + gb]
            P.wait("vector", s_te, last_grp_te[(c, g)])
            P.wait("vector", s_sc, sc_ab)
            P.wait("vector", *last_grp_sp.get((c, g - 2), last_grp_sp.get((c - 1, 14 + g), (s_spg, 0))))
            for hj in range(4):
                hh = 4 * g + hj
                P.emit("vector", lambda v, c=c, hh=hh, hj=hj, pS=pS: v.scalar_tensor_tensor(
                    h_f32[:, 128 * hh:128 * (hh + 1)], h_f32[:, 128 * hh:128 * (hh + 1)],
                    ab_t[c][:, hh:hh + 1], pS[0:128, 128 * hj:128 * (hj + 1)], ALU.mult, ALU.add),
                    chain='start' if hj == 0 else 'mid')
                P.emit("vector", lambda v, pb=pb, gb=gb, hh=hh, hj=hj, py=py: v.scalar_tensor_tensor(
                    y4_b[gb][:, 128 * hj:128 * (hj + 1)], Xc[pb][:, 128 * hh:128 * (hh + 1)],
                    dbc[:, hh:hh + 1], py[0:128, 128 * hj:128 * (hj + 1)], ALU.mult, ALU.add),
                    chain='end' if hj == 3 else 'mid')
            last_grp_v[(c, g)] = s_v.n
            P.wait("scalar", s_v, last_grp_v[(c, g)])
            P.emit("scalar", lambda e, g=g: e.copy(
                h_bf[:, 512 * g:512 * (g + 1)], h_f32[:, 512 * g:512 * (g + 1)]), inc=(s_sc, 1))
            P.emit("scalar", lambda e, c=c, g=g, gb=gb: e.dma_start(
                out=y_scr.rearrange("(hh p) t -> p hh t", p=128)[:, 4 * g:4 * (g + 1), Q * c:Q * (c + 1)],
                in_=y4_b[gb].ap().rearrange("p (hh t) -> p hh t", t=Q)), inc=(s_y[gb], 16), dma=True)
            P.emit("scalar", lambda e, c=c, g=g, gb=gb: e.dma_start(
                out=cd_scr.rearrange("(hh n) t -> n hh t", n=128)[:, 4 * g:4 * (g + 1), Q * c:Q * (c + 1)],
                in_=cdec_b[gb].ap().rearrange("n (hh t) -> n hh t", t=Q)), inc=(s_y[gb], 16), dma=True)
            last_grp_sp[(c, g)] = (s_y[gb], s_y[gb].n)

        pend = None
        for g in range(16):
            gb = g % 2
            py = pf[0 + gb]
            pS = pf[2 + gb]
            P.wait("vector", *rt_ld[g])
            P.wait("vector", s_v, v_btg)
            P.wait("vector", s_sc, last_grp_sc.get((c, g - 2), 0))
            P.wait("vector", s_te, last_grp_te.get((c, g - 2), last_grp_te.get((c - 1, 14 + g), 0)))
            P.emit("vector", lambda v, gb=gb: v.tensor_add(rtm_b[gb][:, :], rt_b[gb][:, :], mneg4[:, :]), inc=(s_v, 1))
            v_rtm = s_v.n
            P.wait("scalar", s_v, v_rtm)
            P.wait("scalar", s_v, v_trans)
            P.emit("scalar", lambda e, gb=gb: e.activation(edec_b[gb][:, :], rt_b[gb][:, :], AF.Exp), chain='start')
            for hj in range(4):
                hh = 4 * g + hj
                P.emit("scalar", lambda e, gb=gb, hj=hj, hh=hh: e.activation(
                    l4_b[gb][:, 128 * hj:128 * (hj + 1)], rtm_b[gb][:, 128 * hj:128 * (hj + 1)],
                    AF.Exp, bias=trc[:, 64 + hh:65 + hh]), chain='end' if hj == 3 else 'mid')
            sc_exps = s_sc.n
            last_rt_sc[gb] = sc_exps
            last_grp_sc[(c, g)] = sc_exps
            if g + 2 < 16:
                P.wait("sync", s_sc, sc_exps)
                P.emit("sync", lambda e, c=c, g=g, gb=gb: e.dma_start(
                    out=rt_b[gb][:, :],
                    in_=cs_scr[c % 2].rearrange("h t -> (h t)").unsqueeze(0)[0:1, 512 * (g + 2):512 * (g + 3)].broadcast_to((128, 512))),
                    inc=(s_rt[gb], 16), dma=True)
                rt_ld[g + 2] = (s_rt[gb], s_rt[gb].n)
            P.wait("vector", s_sc, sc_exps)
            P.wait("vector", *last_grp_sp.get((c, g - 2), last_grp_sp.get((c - 1, 14 + g), (s_spg, 0))))
            P.emit("vector", lambda v, gb=gb: v.tensor_mul(cdec_b[gb][:, :], crep[:, :], edec_b[gb][:, :]), chain='start')
            P.emit("vector", lambda v, gb=gb: v.tensor_mul(m4_b[gb][:, :], grep[:, :], l4_b[gb][:, :]), chain='end')
            P.wait("vector", *slab_ld[c])
            for hj in range(4):
                hh = 4 * g + hj
                P.emit("vector", lambda v, pb=pb, gb=gb, hh=hh, hj=hj: v.scalar_tensor_tensor(
                    dtxT[gb][:, 128 * hj:128 * (hj + 1)], XTc[pb][:, 128 * hh:128 * (hh + 1)],
                    dtT_bf[:, hh:hh + 1], XTc[pb][:, 128 * hh:128 * (hh + 1)], ALU.mult, ALU.bypass),
                    chain='start' if hj == 0 else 'mid')
                P.emit("vector", lambda v, gb=gb, hh=hh, hj=hj: v.scalar_tensor_tensor(
                    bdT[gb][:, 128 * hj:128 * (hj + 1)], bt_b[:, :], wT_bf[:, hh:hh + 1],
                    bt_b[:, :], ALU.mult, ALU.bypass), chain='end' if hj == 3 else 'mid')
            v_heads = s_v.n
            P.wait("tensor", s_v, v_heads)
            P.wait("tensor", s_v, last_grp_v.get((c, g - 2), last_grp_v.get((c - 1, 14 + g), 0)))
            for hj in range(4):
                hh = 4 * g + hj
                P.emit("tensor", lambda t, gb=gb, hj=hj, py=py: t.matmul(
                    py.ap()[0:128, 128 * hj:128 * (hj + 1)], dtxT[gb][:, 128 * hj:128 * (hj + 1)],
                    m4_b[gb][:, 128 * hj:128 * (hj + 1)], start=True, stop=False))
                P.emit("tensor", lambda t, gb=gb, hj=hj, hh=hh, py=py: t.matmul(
                    py.ap()[0:128, 128 * hj:128 * (hj + 1)], h_bf[:, 128 * hh:128 * (hh + 1)],
                    cdec_b[gb][:, 128 * hj:128 * (hj + 1)], start=False, stop=True))
                P.emit("tensor", lambda t, gb=gb, hj=hj, pS=pS: t.matmul(
                    pS.ap()[0:128, 128 * hj:128 * (hj + 1)], bdT[gb][:, 128 * hj:128 * (hj + 1)],
                    dtxT[gb][:, 128 * hj:128 * (hj + 1)], start=True, stop=True),
                    inc=(s_te, 1) if hj == 3 else None)
            last_grp_te[(c, g)] = s_te.n
            if pend is not None:
                stage_E(pend)
            pend = g
        stage_E(pend)
        last_cs_ld[pb] = [rt_ld[14], rt_ld[15], (s_ab[pb], s_ab[pb].n)]
    v_p3a = s_v.n
    if debug:
        P.wait("scalar", s_v, v_p3a)
        P.emit("scalar", lambda e: e.dma_start(out=dbg["dbg_h"], in_=h_f32[:]), inc=(s_spg, 16), dma=True)

    P.barrier(s_bar)
    cx3a.close()

    cxX = contextlib.ExitStack()
    metag = sb(cxX, "metag", [8, NH], F32)
    w8 = sb(cxX, "w8", [8, NH], F32)
    pmcol = sb(cxX, "pmcol", [8, 1], F32)
    btT = sb(cxX, "btT", [8, 8], F32)
    meta_bc = sb(cxX, "meta_bc", [128, NH], F32)
    wbc = [sb(cxX, f"wbc{j}", [128, NH], F32) for j in range(8)]
    sg_b = [sb(cxX, f"sg_b{i}", [128, NH * 128], BF16) for i in range(2)]

    for g8 in range(8):
        P.emit("scalar", lambda e, g8=g8: e.dma_start(
            out=cc_s_in.ap().rearrange("(hh n) p -> n hh p", n=128)[:, 8 * g8:8 * (g8 + 1), :],
            in_=h_bf.ap().rearrange("n (hh p) -> n hh p", p=128)[:, 8 * g8:8 * (g8 + 1), :]), inc=(s_ccw, 16), dma=True)
    P.emit("scalar", lambda e: e.dma_start(out=small_scr[0:1, 0:64].rearrange("o h -> h o"),
                                           in_=logtot[:, :]), inc=(s_sm, 16), dma=True)
    P.wait("sync", s_sm, s_sm.n)
    P.emit("sync", lambda e: e.dma_start(out=meta_bc[:], in_=small_scr[0:1, 0:64].broadcast_to((128, NH))), inc=(s_meta, 16), dma=True)
    P.wait("scalar", s_meta, s_meta.n)
    P.emit("scalar", lambda e: e.dma_start(out=cc_m_in.ap(), in_=meta_bc[:]), inc=(s_ccw, 16), dma=True)
    P.wait("gpsimd", s_ccw, s_ccw.n)
    P.emit("gpsimd", lambda g: g.collective_compute(
        "AllGather", ALU.bypass, replica_groups=[[0, 1, 2, 3, 4, 5, 6, 7]],
        ins=[cc_s_in.ap().opt()], outs=[cc_s_out.ap().opt()]), inc=(s_cc, 1))
    P.emit("gpsimd", lambda g: g.collective_compute(
        "AllGather", ALU.bypass, replica_groups=[[0, 1, 2, 3, 4, 5, 6, 7]],
        ins=[cc_m_in.ap().opt()], outs=[cc_m_out.ap().opt()]), inc=(s_cc, 1))
    P.wait("sync", s_cc, 2)
    P.wait("tensor", s_cc, 2)

    P.emit("sync", lambda e: e.dma_start(
        out=metag[:, :],
        in_=cc_m_out.ap().rearrange("(j r) h -> j r h", r=128)[:, 0:1, :].rearrange("j o h -> j (o h)")), inc=(s_meta, 16), dma=True)
    P.emit("sync", lambda e: e.dma_start(out=btT[:, :], in_=betweenT), inc=(s_meta, 16), dma=True)
    P.emit("sync", lambda e: e.dma_start(out=pmcol[:, :], in_=prevmask), inc=(s_meta, 16), dma=True)
    P.wait("tensor", s_meta, s_meta.n)
    P.emit("tensor", lambda t: t.matmul(pf[4].ap()[0:8, 0:NH], btT[:, :], metag[:, :], start=True, stop=True), inc=(s_te, 1))
    P.wait("scalar", s_te, s_te.n)
    P.emit("scalar", lambda e: e.activation(w8[:, :], pf[4][0:8, 0:NH], AF.Exp), inc=(s_sc, 1))
    P.wait("vector", s_sc, s_sc.n)
    P.emit("vector", lambda v: v.scalar_tensor_tensor(
        w8[:, :], w8[:, :], pmcol[:, 0:1], w8[:, :], ALU.mult, ALU.bypass), inc=(s_v, 1))
    P.wait("scalar", s_v, s_v.n)
    P.emit("scalar", lambda e: e.dma_start(out=wj_scr.rearrange("o (j h) -> j h", j=8), in_=w8[:, :]), inc=(s_ccw, 16), dma=True)
    P.wait("sync", s_ccw, s_ccw.n)
    for j in range(8):
        P.emit("sync", lambda e, j=j: e.dma_start(
            out=wbc[j][:, :], in_=wj_scr[0:1, NH * j:NH * (j + 1)].broadcast_to((128, NH))), inc=(s_meta, 16), dma=True)
    P.emit("vector", lambda v: v.memset(h_f32[:], 0.0), inc=(s_v, 1))
    vj = {}
    for j in range(8):
        jb = j % 2
        if j >= 2:
            P.wait("sync", s_v, vj[j - 2])
        for g8 in range(8):
            P.emit("sync", lambda e, j=j, jb=jb, g8=g8: e.dma_start(
                out=sg_b[jb].ap().rearrange("n (hh p) -> n hh p", p=128)[:, 8 * g8:8 * (g8 + 1), :],
                in_=cc_s_out.ap().rearrange("(j hh n) p -> j n hh p", hh=NH, n=128)[j][:, 8 * g8:8 * (g8 + 1), :]),
                inc=(s_sl[jb], 16), dma=True)
        P.wait("vector", s_sl[jb], s_sl[jb].n)
        P.wait("vector", s_meta, s_meta.n)
        for hh in range(NH):
            P.emit("vector", lambda v, j=j, jb=jb, hh=hh: v.scalar_tensor_tensor(
                h_f32[:, 128 * hh:128 * (hh + 1)], sg_b[jb][:, 128 * hh:128 * (hh + 1)],
                wbc[j][:, hh:hh + 1], h_f32[:, 128 * hh:128 * (hh + 1)], ALU.mult, ALU.add),
                chain=('start' if hh == 0 else ('end' if hh == NH - 1 else 'mid')))
        vj[j] = s_v.n
    P.wait("scalar", s_v, s_v.n)
    P.emit("scalar", lambda e: e.copy(h_bf[:, :], h_f32[:, :]), inc=(s_sc, 1))
    P.barrier(s_bar)
    cxX.close()

    cx3b = contextlib.ExitStack()
    Yc = [sb(cx3b, f"Yc{i}", [128, NH * 128], BF16) for i in range(2)]
    Zc = [sb(cx3b, f"Zc{i}", [128, NH * 128], BF16) for i in range(2)]
    Cc_ = [sb(cx3b, f"Cc_{i}", [128, NH * 128], BF16) for i in range(2)]
    o2_b = [sb(cx3b, f"o2_b{i}", [128, 512], F32) for i in range(2)]
    th_b = [sb(cx3b, f"th_b{i}", [128, 512], F32) for i in range(2)]
    o3_b = [sb(cx3b, f"o3_b{i}", [128, 512], BF16) for i in range(2)]
    sq_b = [sb(cx3b, f"sq_b{i}", [128, 512], BF16) for i in range(2)]
    o4_b = [sb(cx3b, f"o4_b{i}", [128, 512], BF16) for i in range(2)]
    rrow = sb(cx3b, "rrow", [1, Q], F32)

    for s_ in (s_y[0], s_y[1], s_z[0], s_z[1]):
        P.wait("sync", s_, s_.n)
    slab_load(Yc[0], y_scr, 0, s_sl[0])
    slab_load(Zc[0], z_scr, 0, s_sl[0])
    slab_load(Cc_[0], cd_scr, 0, s_sl[0])
    slab3b_ld = {0: (s_sl[0], s_sl[0].n)}
    bl_v = {}
    bl_sp = {}
    bl_te = {}
    bl_sc = {}
    bl_sqte = {}
    bl_silu = {}
    rrow_v = {}

    for c in range(NCHUNK):
        pb = c % 2
        if c + 1 < NCHUNK:
            if c >= 1:
                P.wait("sync", s_v, bl_v.get((c - 1, 15), 0))
                P.wait("sync", s_sc, bl_sc.get((c - 1, 15), 0))
                P.wait("sync", s_te, bl_te.get((c - 1, 15), 0))
            slab_load(Yc[(c + 1) % 2], y_scr, c + 1, s_sl[(c + 1) % 2])
            slab_load(Zc[(c + 1) % 2], z_scr, c + 1, s_sl[(c + 1) % 2])
            slab_load(Cc_[(c + 1) % 2], cd_scr, c + 1, s_sl[(c + 1) % 2])
            slab3b_ld[c + 1] = (s_sl[(c + 1) % 2], s_sl[(c + 1) % 2].n)
        def stage_F(g):
            gb = g % 2
            py = pf[0 + gb]
            P.wait("vector", s_te, bl_te[(c, g)])
            P.wait("vector", s_sc, bl_sc.get((c, g - 2), 0))
            P.emit("vector", lambda v, pb=pb, g=g, gb=gb, py=py: v.tensor_add(
                o2_b[gb][:, :], py[0:128, 0:512], Yc[pb][:, 512 * g:512 * (g + 1)]))
            P.wait("vector", s_sc, bl_silu[(c, g)])
            P.emit("vector", lambda v, gb=gb: v.tensor_mul(o3_b[gb][:, :], o2_b[gb][:, :], th_b[gb][:, :]), inc=(s_v, 1))
            v_o3 = s_v.n
            P.wait("scalar", s_v, v_o3)
            P.wait("scalar", s_te, bl_sqte.get((c, g - 2), 0))
            P.emit("scalar", lambda e, gb=gb: e.activation(sq_b[gb][:, :], o3_b[gb][:, :], AF.Square), inc=(s_sc, 1))
            bl_sc[(c, g)] = s_sc.n
            P.wait("vector", *bl_sp.get((c, g - 2), bl_sp.get((c - 1, 14 + g), (s_spg, 0))))
            for hj in range(4):
                hh = 4 * g + hj
                P.emit("vector", lambda v, gb=gb, hh=hh, hj=hj: v.scalar_tensor_tensor(
                    o4_b[gb][:, 128 * hj:128 * (hj + 1)], o3_b[gb][:, 128 * hj:128 * (hj + 1)],
                    nwcols[:, hh:hh + 1], o3_b[gb][:, 128 * hj:128 * (hj + 1)], ALU.mult, ALU.bypass),
                    chain='start' if hj == 0 else 'mid')
                P.emit("vector", lambda v, c=c, hh=hh: v.scalar_tensor_tensor(
                    h_f32[:, 128 * hh:128 * (hh + 1)], h_f32[:, 128 * hh:128 * (hh + 1)],
                    ab_t[c][:, hh:hh + 1], h_f32[:, 128 * hh:128 * (hh + 1)], ALU.mult, ALU.bypass),
                    chain='end' if hj == 3 else 'mid')
            bl_v[(c, g)] = s_v.n
            P.wait("scalar", s_v, bl_v[(c, g)])
            P.emit("scalar", lambda e, g=g: e.copy(
                h_bf[:, 512 * g:512 * (g + 1)], h_f32[:, 512 * g:512 * (g + 1)]), inc=(s_sc, 1))
            P.wait("tensor", s_sc, bl_sc[(c, g)])
            if g == 0 and c >= 1:
                P.wait("tensor", s_v, rrow_v[c - 1])
            for hj in range(4):
                P.emit("tensor", lambda t, g=g, gb=gb, hj=hj: t.matmul(
                    pf[5].ap()[0:1, 0:Q], onescol_bf[:, 0:1], sq_b[gb][:, 128 * hj:128 * (hj + 1)],
                    start=(g == 0 and hj == 0), stop=(g == 15 and hj == 3), skip_group_check=True),
                    inc=(s_te, 1) if hj == 3 else None)
            bl_sqte[(c, g)] = s_te.n
            P.emit("scalar", lambda e, c=c, g=g, gb=gb: e.dma_start(
                out=yn_scr.rearrange("(hh p) t -> p hh t", p=128)[:, 4 * g:4 * (g + 1), Q * c:Q * (c + 1)],
                in_=o4_b[gb].ap().rearrange("p (hh t) -> p hh t", t=Q)), inc=(s_y[gb], 16), dma=True)
            bl_sp[(c, g)] = (s_y[gb], s_y[gb].n)

        pend = None
        for g in range(16):
            gb = g % 2
            py = pf[0 + gb]
            P.wait("tensor", *slab3b_ld[c])
            P.wait("tensor", s_v, bl_v.get((c, g - 2), bl_v.get((c - 1, 14 + g), 0)))
            for hj in range(4):
                hh = 4 * g + hj
                P.emit("tensor", lambda t, pb=pb, hh=hh, hj=hj, py=py: t.matmul(
                    py.ap()[0:128, 128 * hj:128 * (hj + 1)], h_bf[:, 128 * hh:128 * (hh + 1)],
                    Cc_[pb][:, 128 * hh:128 * (hh + 1)], start=True, stop=True),
                    inc=(s_te, 1) if hj == 3 else None)
            bl_te[(c, g)] = s_te.n
            P.wait("scalar", *slab3b_ld[c])
            P.wait("scalar", s_v, bl_v.get((c, g - 2), bl_v.get((c - 1, 14 + g), 0)))
            P.emit("scalar", lambda e, pb=pb, g=g, gb=gb: e.activation(
                th_b[gb][:, :], Zc[pb][:, 512 * g:512 * (g + 1)], AF.Silu), inc=(s_sc, 1))
            bl_silu[(c, g)] = s_sc.n
            if pend is not None:
                stage_F(pend)
            pend = g
        stage_F(pend)
        P.wait("vector", s_te, bl_sqte[(c, 15)])
        P.wait("vector", s_spg, s_spg.n)
        P.emit("vector", lambda v: v.tensor_copy(rrow[:, :], pf[5][0:1, 0:Q]), inc=(s_v, 1))
        rrow_v[c] = s_v.n
        P.wait("scalar", s_v, s_v.n)
        P.emit("scalar", lambda e: e.activation(rrow[:, :], rrow[:, :], AF.Ln, bias=cbias[0:1, 3:4], scale=1.0 / DSSM))
        P.emit("scalar", lambda e: e.activation(rrow[:, :], rrow[:, :], AF.Exp, scale=-0.5), inc=(s_sc, 1))
        P.wait("scalar", s_sc, s_sc.n)
        P.emit("scalar", lambda e, c=c: e.dma_start(out=rstd_scr[c:c + 1, :], in_=rrow[:, :]), inc=(s_spg, 16), dma=True)
    P.barrier(s_bar)
    cx3b.close()
    cxH.close()

    cx4 = contextlib.ExitStack()
    wo_bf = [sb(cx4, f"wo_bf{i}", [128, NH * 512], BF16) for i in range(2)]
    yns = [sb(cx4, f"yns{i}", [128, 4 * SPAN], BF16) for i in range(2)]
    rst8 = sb(cx4, "rst8", [NCHUNK, Q], F32)
    rstT = sb(cx4, "rstT", [128, NCHUNK], F32)
    osc = [sb(cx4, f"osc{i}", [128, 8 * 512], F32) for i in range(2)]
    fin = [sb(cx4, f"fin{i}", [128, 512], F32) for i in range(2)]

    P.wait("sync", s_spg, s_spg.n)
    for s_ in (s_y[0], s_y[1]):
        P.wait("sync", s_, s_.n)
    P.emit("sync", lambda e: e.dma_start(out=rst8[:, :], in_=rstd_scr), inc=(s_meta, 16), dma=True)
    P.wait("tensor", s_meta, s_meta.n)
    P.emit("tensor", lambda t: t.transpose(pf[4].ap()[0:128, 0:8], rst8[:, :], ident_f[0:8, 0:8]), inc=(s_te, 1))
    P.wait("vector", s_te, s_te.n)
    P.emit("vector", lambda v: v.tensor_copy(rstT[:, :], pf[4][0:128, 0:8]), inc=(s_v, 1))
    v_rstT = s_v.n

    def emit_woload(dc):
        wb = dc % 2
        for gkc in range(4):
            P.emit("gpsimd", lambda g, wb=wb, dc=dc, gkc=gkc: g.dma_start(
                out=wo_bf[wb].ap().rearrange("p (kc c) -> p kc c", c=512)[:, 16 * gkc:16 * (gkc + 1), :],
                in_=w_out.rearrange("(kc p) c -> p kc c", p=128)[:, 16 * gkc:16 * (gkc + 1), 512 * dc:512 * (dc + 1)]),
                inc=(s_w[wb], 16), dma=True)
        wo_ld[dc] = (s_w[wb], s_w[wb].n)

    wo_ld = {}
    emit_woload(0)
    te_dc = {}
    te_blk = {}
    v_evac = {}
    st_evac = {}
    osc_ld = {}
    pview = [pf[i].ap()[0:128, 0:512] for i in range(6)] + \
            [pbf[i].ap().bitcast(F32)[0:128, 0:512] for i in range(2)]

    for dc in range(8):
        wb = dc % 2
        ob = dc % 2
        if dc >= 2:
            P.wait("sync", s_v, v_evac[dc - 2])
        if dc >= 1:
            P.wait("sync", *st_evac.get(dc - 1, (s_spg, 0)))
        P.emit("sync", lambda e, ob=ob, dc=dc: e.dma_start(
            out=osc[ob].ap().rearrange("p (tc c) -> p tc c", c=512),
            in_=out[:, 512 * dc:512 * (dc + 1)].rearrange("(tc p) c -> p tc c", p=128)),
            inc=(s_os[ob], 16), dma=True)
        osc_ld[dc] = (s_os[ob], s_os[ob].n)
        if dc + 1 < 8:
            if dc >= 1:
                P.wait("gpsimd", s_te, te_dc[dc - 1])
            emit_woload(dc + 1)
        P.wait("tensor", *wo_ld[dc])
        P.wait("tensor", s_v, v_rstT)
        if dc >= 1:
            P.wait("tensor", s_v, v_evac[dc - 1])
        for blk in range(16):
            yb = blk % 2
            P.wait("sync", s_te, te_blk.get((dc, blk - 2), te_blk.get((dc - 1, 14 + blk), 0)))
            P.emit("sync", lambda e, yb=yb, blk=blk: e.dma_start(
                out=yns[yb].ap().rearrange("p (k t) -> p k t", k=4),
                in_=yn_scr[512 * blk:512 * (blk + 1), :].rearrange("(k p) t -> p k t", p=128)),
                inc=(s_sl[yb], 16), dma=True)
            P.wait("tensor", s_sl[yb], s_sl[yb].n)
            for k in range(4):
                ch = 4 * blk + k
                for tc in range(8):
                    P.emit("tensor", lambda t, wb=wb, yb=yb, k=k, ch=ch, tc=tc: t.matmul(
                        pview[tc], yns[yb][:, 1024 * k + 128 * tc:1024 * k + 128 * (tc + 1)],
                        wo_bf[wb][:, 512 * ch:512 * (ch + 1)],
                        start=(ch == 0), stop=(ch == NH - 1), skip_group_check=True),
                        inc=(s_te, 1) if tc == 7 else None)
            te_blk[(dc, blk)] = s_te.n
        te_dc[dc] = s_te.n
        P.wait("vector", s_te, te_dc[dc])
        P.wait("vector", *osc_ld[dc])
        for tc in range(8):
            fb = tc % 2
            if tc >= 2 or dc >= 1:
                P.wait("vector", *st_evac.get((dc, tc - 2), st_evac.get((dc - 1, 6 + tc), (s_spg, 0))))
            P.emit("vector", lambda v, ob=ob, fb=fb, tc=tc: v.scalar_tensor_tensor(
                fin[fb][:, :], pview[tc], rstT[:, tc:tc + 1], osc[ob][:, 512 * tc:512 * (tc + 1)],
                ALU.mult, ALU.add), inc=(s_v, 1))
            v_evac[(dc, tc)] = s_v.n
            P.wait("sync", s_v, s_v.n)
            P.emit("sync", lambda e, fb=fb, tc=tc, dc=dc: e.dma_start(
                out=out[128 * tc:128 * (tc + 1), 512 * dc:512 * (dc + 1)], in_=fin[fb][:, :]), inc=(s_st[fb], 16), dma=True)
            st_evac[(dc, tc)] = (s_st[fb], s_st[fb].n)
        v_evac[dc] = s_v.n
        st_evac[dc] = (s_st[0], s_st[0].n)

    if debug:
        for s_ in (s_y[0], s_y[1], s_z[0], s_z[1], s_xo[0], s_xo[1], s_spg):
            P.wait("gpsimd", s_, s_.n)
        P.emit("gpsimd", lambda g: g.dma_start(out=dbg["dbg_z"], in_=z_scr), inc=(s_spg, 16), dma=True)
        P.emit("gpsimd", lambda g: g.dma_start(out=dbg["dbg_x"], in_=x_scr), inc=(s_spg, 16), dma=True)
        P.emit("gpsimd", lambda g: g.dma_start(out=dbg["dbg_y"], in_=y_scr), inc=(s_spg, 16), dma=True)
        P.emit("gpsimd", lambda g: g.dma_start(out=dbg["dbg_yn"], in_=yn_scr), inc=(s_spg, 16), dma=True)
        P.wait("gpsimd", s_spg, s_spg.n)

    P.wait("sync", s_st[0], s_st[0].n)
    P.wait("sync", s_st[1], s_st[1].n)
    P.barrier(s_bar)

    with nc.Block() as block:
        @block.sync
        def _(eng):
            for fn in P.q["sync"]:
                fn(eng)

        @block.scalar
        def _(eng):
            for fn in P.q["scalar"]:
                fn(eng)

        @block.gpsimd
        def _(eng):
            for fn in P.q["gpsimd"]:
                fn(eng)

        @block.vector
        def _(eng):
            for fn in P.q["vector"]:
                fn(eng)

        @block.tensor
        def _(eng):
            for fn in P.q["tensor"]:
                fn(eng)

    cx4.close()
    ctx.close()
    return nc


_NC_CACHE = {}


def kernel(**inputs):
    x = np.ascontiguousarray(inputs["L0_attn_hidden_states"], dtype=np.float32)
    w_in_a = np.ascontiguousarray(inputs["L0_attn_W_in"], dtype=np.float32)
    conv_w_a = np.ascontiguousarray(inputs["L0_attn_conv_w"], dtype=np.float32)
    dt_bias_a = np.ascontiguousarray(inputs["L0_attn_dt_bias"], dtype=np.float32).reshape(NH, 1)
    a_log_a = np.ascontiguousarray(inputs["L0_attn_A_log"], dtype=np.float32).reshape(NH, 1)
    d_par_a = np.ascontiguousarray(inputs["L0_attn_D_param"], dtype=np.float32).reshape(1, NH)
    nw_a = np.ascontiguousarray(inputs["L0_attn_norm_weight"], dtype=np.float32).reshape(NH, 128)
    w_out_a = np.ascontiguousarray(inputs["L0_attn_W_out"], dtype=np.float32)
    w_res_a = np.ascontiguousarray(inputs["L0_ffn_W_res"], dtype=np.float32)
    w_pre_a = np.ascontiguousarray(inputs["L0_ffn_W_pre"], dtype=np.float32)
    w_post_a = np.ascontiguousarray(inputs["L0_ffn_W_post"], dtype=np.float32)
    h_bases_a = np.concatenate([
        np.asarray(inputs["L0_ffn_H_res_base"], np.float32).reshape(-1),
        np.asarray(inputs["L0_ffn_H_pre_base"], np.float32).reshape(-1),
        np.asarray(inputs["L0_ffn_H_post_base"], np.float32).reshape(-1)]).reshape(1, 24)
    gate_a = np.ascontiguousarray(inputs["L0_ffn_gate"], dtype=np.float32).reshape(1, 1)
    w_br_a = np.ascontiguousarray(inputs["L0_ffn_W_branch"], dtype=np.float32)

    if "nc" not in _NC_CACHE:
        _NC_CACHE["nc"] = build_program(debug=False)
    nc = _NC_CACHE["nc"]

    in_maps = []
    for cid in range(8):
        b, s = cid // 4, cid % 4
        t0 = s * SPAN
        if s == 0:
            xh_a = np.concatenate([np.zeros((HALO, HD), np.float32), x[b, t0:t0 + SPAN]], 0)
        else:
            xh_a = x[b, t0 - HALO:t0 + SPAN]
        between = np.zeros((8, 8), np.float32)
        pmask = np.zeros((8, 1), np.float32)
        for j in range(8):
            jb, js = j // 4, j % 4
            if jb == b and js < s:
                pmask[j, 0] = 1.0
                for m in range(8):
                    mb, ms = m // 4, m % 4
                    if mb == b and js < ms < s:
                        between[m, j] = 1.0
        in_maps.append({
            "xh": np.ascontiguousarray(xh_a), "w_in": w_in_a, "conv_w": conv_w_a,
            "dt_bias": dt_bias_a, "a_log": a_log_a, "d_par": d_par_a, "nw_in": nw_a,
            "w_out": w_out_a, "w_res": w_res_a, "w_pre": w_pre_a, "w_post": w_post_a,
            "h_bases": h_bases_a, "gate": gate_a, "w_br": w_br_a,
            "betweenT": between, "prevmask": pmask,
        })
    res = run_bass_kernel_spmd(nc, in_maps, core_ids=list(range(8)))
    outp = np.zeros((2, 4096, HD), np.float32)
    for cid in range(8):
        b, s = cid // 4, cid % 4
        outp[b, s * SPAN:(s + 1) * SPAN] = res.results[cid]["out"]
    return outp
